# version 03
# baseline (speedup 1.0000x reference)
import contextlib
import numpy as np
import concourse.bass as bass
import concourse.mybir as mybir
from concourse.bass_utils import run_bass_kernel_spmd

F32 = mybir.dt.float32
BF16 = mybir.dt.bfloat16
U32 = mybir.dt.uint32
AF = mybir.ActivationFunctionType
ALU = mybir.AluOpType
AX = mybir.AxisListType

T = 1024
D = 2048
NCH = 16
NI = 8
NE = 18
NH = 32
S = 2048
SCALE = float(1.0 / np.sqrt(np.float32(32.0)))
EPS_LN = 1e-5
EPS_GATE = 1e-9



class Buf:
    __slots__ = ("name", "w", "readers")

    def __init__(self, name):
        self.name = name
        self.w = None
        self.readers = []


class SB:
    ENGS = ("pe", "act", "dve", "pool", "sq", "aq", "gq")
    DMA = frozenset(("sq", "aq", "gq"))
    SECT = {"pe": "tensor", "act": "scalar", "dve": "vector",
            "pool": "gpsimd", "sq": "sync", "aq": "scalar", "gq": "gpsimd"}

    def __init__(self):
        self.prog = {"tensor": [], "scalar": [], "vector": [], "gpsimd": [], "sync": []}
        self.cnt = {e: 0 for e in self.ENGS}
        self.hw = {e: {d: 0 for d in self.ENGS} for e in self.ENGS}
        self.pend = {e: [] for e in self.ENGS}
        self.sems = {}

    def raw(self, eng, fn):
        self.prog[self.SECT[eng]].append(fn)

    def _wait(self, eng, dep, val):
        if val <= self.hw[eng][dep]:
            return
        self.hw[eng][dep] = val
        sems = self.sems

        def w(q, _d=dep, _v=val):
            q.wait_ge(sems[_d], _v)
        self.raw(eng, w)

    def op(self, eng, fn, reads=(), writes=(), inc=True):
        deps = {}
        for b in reads:
            if b.w is not None:
                e, c = b.w
                deps[e] = max(deps.get(e, 0), c)
        for b in writes:
            if b.w is not None:
                e, c = b.w
                deps[e] = max(deps.get(e, 0), c)
            for e, c in b.readers:
                deps[e] = max(deps.get(e, 0), c)
        for e, c in deps.items():
            if e == "pe" and eng == "pe":
                continue
            self._wait(eng, e, c)
        step = 16 if eng in self.DMA else 1
        if inc:
            self.cnt[eng] += step
            newc = self.cnt[eng]
            sems = self.sems

            def run(q, _fn=fn, _e=eng):
                _fn(q).then_inc(sems[_e], 16 if _e in self.DMA else 1)
            self.raw(eng, run)
            for b, is_w in self.pend[eng]:
                if is_w:
                    b.w = (eng, newc)
                    b.readers = []
                else:
                    b.readers.append((eng, newc))
            self.pend[eng].clear()
            for b in writes:
                b.w = (eng, newc)
                b.readers = []
            for b in reads:
                b.readers.append((eng, newc))
        else:
            self.raw(eng, lambda q, _fn=fn: _fn(q))
            for b in writes:
                self.pend[eng].append((b, True))
            for b in reads:
                self.pend[eng].append((b, False))

    def barrier(self):
        for e in self.ENGS:
            assert not self.pend[e], f"pending unstamped ops on {e} at barrier"



def build_program(stage="full"):
    nc = bass.Bass("TRN2", target_bir_lowering=False)
    sb = SB()

    def ein(name, shape, dt=F32):
        return nc.dram_tensor(name, shape, dt, kind="ExternalInput").ap()

    x_in = ein("x_in", [D, T])
    wq_d = ein("wq", [NCH, 128, NCH, 128])
    wk_d = ein("wk", [NCH, 128, NCH, 128])
    wo_d = ein("wo", [NCH, 128, NCH, 128])
    wv_d = ein("wv", [D, D])
    wg_d = ein("wg", [NE, NI, 128, NCH, 128])
    wu_d = ein("wu", [NE, NI, 128, NCH, 128])
    wd_d = ein("wd", [NE, NCH, 128, NI, 128])
    ct_d = ein("ct", [NCH, 128, 16])
    bq_d = ein("bq", [D])
    bk_d = ein("bk", [D])
    bv_d = ein("bv", [D])
    bo_d = ein("bo", [D])
    gb_d = ein("gb", [16])
    ident_d = ein("ident", [128, 128])
    selb_d = ein("selb", [64, NE * 128], BF16)
    shmat_d = ein("shmat", [64, 128], BF16)
    out_d = nc.dram_tensor("out", [D, T], F32, kind="ExternalOutput").ap()
    dbg1_d = nc.dram_tensor("dbg1", [D, T], F32, kind="ExternalOutput").ap()
    dbg2_d = nc.dram_tensor("dbg2", [D, T], F32, kind="ExternalOutput").ap()

    def dint(name, shape, dt):
        return nc.dram_tensor(name, shape, dt, kind="Internal").ap()

    q_park = dint("q_park", [D, T], BF16)
    k_park = dint("k_park", [D, T], BF16)
    v_park = dint("v_park", [T, NH * 65], BF16)
    bhl_park = dint("bhl_park", [64, T], BF16)
    bnc_stat = dint("bnc_stat", [2, T], F32)
    bnc_csum = dint("bnc_csum", [16], F32)
    bnc_qn = dint("bnc_qn", [NH, T], F32)
    kn2_loc = dint("kn2_loc", [NH, T], F32)
    bnc_sums = dint("bnc_sums", [2, T], F32)

    kg = nc.dram_tensor("kg", [2 * D, T], BF16, kind="Internal").ap()
    vg = nc.dram_tensor("vg", [S, NH * 65], BF16, kind="Internal").ap()
    ng = nc.dram_tensor("ng", [2 * NH, T], F32, kind="Internal").ap()

    es = contextlib.ExitStack()
    B = {}

    def buf(name):
        if name not in B:
            B[name] = Buf(name)
        return B[name]

    with es:
        for e in SB.ENGS:
            sb.sems[e] = es.enter_context(nc.semaphore("sem_" + e))
        cc_k = es.enter_context(nc.semaphore("cc_k"))
        cc_v = es.enter_context(nc.semaphore("cc_v"))
        cc_n = es.enter_context(nc.semaphore("cc_n"))

        xfm = es.enter_context(nc.sbuf_tensor("xfm", [128, NCH * T], F32))
        xn = es.enter_context(nc.sbuf_tensor("xn", [128, NCH * T], BF16))
        wga = es.enter_context(nc.sbuf_tensor("wga", [128, 2048], BF16))
        wgb = es.enter_context(nc.sbuf_tensor("wgb", [128, 2048], BF16))
        wua = es.enter_context(nc.sbuf_tensor("wua", [128, 2048], BF16))
        wub = es.enter_context(nc.sbuf_tensor("wub", [128, 2048], BF16))
        mean_b = es.enter_context(nc.sbuf_tensor("mean_b", [128, T], F32))
        rstd_b = es.enter_context(nc.sbuf_tensor("rstd_b", [128, T], F32))
        scr32 = es.enter_context(nc.sbuf_tensor("scr32", [128, T], F32))
        rows32 = es.enter_context(nc.sbuf_tensor("rows32", [128, 2048], F32))
        stg0 = es.enter_context(nc.sbuf_tensor("stg0", [128, T], BF16))
        stg1 = es.enter_context(nc.sbuf_tensor("stg1", [128, T], BF16))
        ident = es.enter_context(nc.sbuf_tensor("ident_sb", [128, 128], F32))
        biases = es.enter_context(nc.sbuf_tensor("biases", [128, 48], F32))
        gcol = es.enter_context(nc.sbuf_tensor("gcol", [128, 48], F32))
        csum_b = es.enter_context(nc.sbuf_tensor("csum_b", [128, 16], F32))
        gb_b = es.enter_context(nc.sbuf_tensor("gb_b", [128, 16], F32))
        ones_c = es.enter_context(nc.sbuf_tensor("ones_c", [128, 1], F32))
        ones_cb = es.enter_context(nc.sbuf_tensor("ones_cb", [128, 1], BF16))
        ones_r = es.enter_context(nc.sbuf_tensor("ones_r", [1, 128], F32))
        ones2d = es.enter_context(nc.sbuf_tensor("ones2d", [128, 128], F32))
        u32t = es.enter_context(nc.sbuf_tensor("u32t", [1, 4], U32))

        PT = [es.enter_context(nc.psum_tensor(f"pt{i}", [128, 1024], F32)) for i in range(4)]
        ptb = [buf(f"pt{i}") for i in range(4)]

        xb = [buf(f"x{c}") for c in range(NCH)]
        xnb = [buf(f"xn{c}") for c in range(NCH)]
        wgab, wgbb = buf("wga"), buf("wgb")
        wuab, wubb = buf("wua"), buf("wub")
        m_bias = buf("biases")
        m_ident = buf("ident")
        m_gcol = buf("gcol")
        m_csum = buf("csum_b")
        m_gbb = buf("gb_b")
        m_ones = buf("ones")
        m_ones2 = buf("ones2d")
        m_u32 = buf("u32t")
        m_meanb = buf("mean_b")
        m_rstdb = buf("rstd_b")
        m_scr = buf("scr32")
        m_rows = buf("rows32")
        m_stg0, m_stg1 = buf("stg0"), buf("stg1")
        m_statd = buf("bnc_stat")
        m_csumd = buf("bnc_csum")
        m_qnd = buf("bnc_qn")
        m_kn2d = buf("kn2_loc")
        m_sumsd = buf("bnc_sums")
        m_qpark = buf("q_park")
        m_kpark = buf("k_park")
        m_vpark = buf("v_park")
        m_bhl = buf("bhl_park")
        m_sh = buf("shared_kv")

        sb.bar_src = u32t[0:1, 2:3]
        sb.bar_dst = {"sq": u32t[0:1, 3:4], "aq": u32t[0:1, 3:4], "gq": u32t[0:1, 3:4]}


        MINCTX = stage.startswith("min")
        for c in range(NCH):
            sb.op("sq", lambda q, c=c: q.dma_start(
                out=xfm[:, c * T:(c + 1) * T], in_=x_in[c * 128:(c + 1) * 128, :]),
                writes=[xb[c]])
        if not MINCTX:
            sb.op("sq", lambda q: q.dma_start(out=ident[:, :], in_=ident_d), writes=[m_ident])
            for j, bd in enumerate((bq_d, bk_d, bo_d)):
                for c in range(NCH):
                    sb.op("sq", lambda q, j=j, c=c, bd=bd: q.dma_start(
                        out=biases[:, j * 16 + c:j * 16 + c + 1],
                        in_=bd[c * 128:(c + 1) * 128].rearrange("(p o) -> p o", o=1)),
                        writes=[m_bias])
            sb.op("sq", lambda q: q.dma_start(
                out=gcol[0:1, 32:48], in_=gb_d.rearrange("(o n) -> o n", o=1)),
                writes=[m_gcol])
        sb.op("dve", lambda q: q.memset(ones_c[:, :], 1.0), writes=[m_ones])
        sb.op("dve", lambda q: q.memset(ones_cb[:, :], 1.0), writes=[m_ones])
        sb.op("dve", lambda q: q.memset(ones_r[:, :], 1.0), writes=[m_ones])
        sb.op("dve", lambda q: q.memset(ones2d[:, :], 1.0), writes=[m_ones2])
        sb.op("dve", lambda q: q.memset(gcol[:, 15:16], EPS_LN), writes=[m_gcol])

        if not MINCTX:
            sb.op("pe", lambda q: q.matmul(PT[2][0:128, 0:16], ones_r[:, :],
                                           gcol[0:1, 32:48], start=True, stop=True),
                  reads=[m_ones, m_gcol], writes=[ptb[2]])
            sb.op("act", lambda q: q.copy(out=gb_b[:, :], in_=PT[2][:, 0:16]),
                  reads=[ptb[2]], writes=[m_gbb])

            ct_sb = es.enter_context(nc.sbuf_tensor("ct_sb", [128, NCH * 16], F32))
            m_ct = buf("ct_sb")
            for c in range(NCH):
                sb.op("sq", lambda q, c=c: q.dma_start(
                    out=ct_sb[:, c * 16:(c + 1) * 16], in_=ct_d[c]), writes=[m_ct])

            for c in range(NCH):
                sb.op("pe", lambda q, c=c: q.matmul(
                    PT[3][0:16, 0:1], ct_sb[:, c * 16:(c + 1) * 16], ones_c[:, :],
                    start=(c == 0), stop=(c == NCH - 1)),
                    reads=[m_ct, m_ones], writes=[ptb[3]], inc=(c == NCH - 1))
            sb.op("act", lambda q: q.copy(out=gcol[0:16, 0:1], in_=PT[3][0:16, 0:1]),
                  reads=[ptb[3]], writes=[m_gcol])
            sb.op("sq", lambda q: q.dma_start(
                out=bnc_csum.rearrange("(p o) -> p o", o=1), in_=gcol[0:16, 0:1]),
                reads=[m_gcol], writes=[m_csumd])
            sb.op("sq", lambda q: q.dma_start(
                out=gcol[0:1, 16:32], in_=bnc_csum.rearrange("(o n) -> o n", o=1)),
                reads=[m_csumd], writes=[m_gcol])
            sb.op("pe", lambda q: q.matmul(PT[2][0:128, 0:16], ones_r[:, :],
                                           gcol[0:1, 16:32], start=True, stop=True),
                  reads=[m_ones, m_gcol], writes=[ptb[2]])
            sb.op("act", lambda q: q.copy(out=csum_b[:, :], in_=PT[2][:, 0:16]),
                  reads=[ptb[2]], writes=[m_csum])

        def emit_ln(bounce_stats):
            for c in range(NCH):
                sb.op("act", lambda q, c=c: q.activation(
                    scr32[:, :], xfm[:, c * T:(c + 1) * T], AF.Square),
                    reads=[xb[c]], writes=[m_scr])
                for n in range(2):
                    sb.op("pe", lambda q, c=c, n=n: q.matmul(
                        PT[2][:, n * 512:(n + 1) * 512], ones2d[:, :],
                        xfm[:, c * T + n * 512:c * T + (n + 1) * 512],
                        start=(c == 0), stop=(c == NCH - 1)),
                        reads=[m_ones2, xb[c]], writes=[ptb[2]],
                        inc=(c == NCH - 1 and n == 1))
                for n in range(2):
                    sb.op("pe", lambda q, c=c, n=n: q.matmul(
                        PT[3][:, n * 512:(n + 1) * 512], ones2d[:, :],
                        scr32[:, n * 512:(n + 1) * 512],
                        start=(c == 0), stop=(c == NCH - 1)),
                        reads=[m_ones2, m_scr], writes=[ptb[3]],
                        inc=(n == 1))
            sb.op("act", lambda q: q.activation(mean_b[:, :], PT[2][:, :],
                                                AF.Copy, scale=1.0 / D),
                  reads=[ptb[2]], writes=[m_meanb])
            sb.op("act", lambda q: q.activation(scr32[:, :], PT[3][:, :],
                                                AF.Copy, scale=1.0 / D),
                  reads=[ptb[3]], writes=[m_scr])
            sb.op("act", lambda q: q.activation(rstd_b[:, :], mean_b[:, :],
                                                AF.Square),
                  reads=[m_meanb], writes=[m_rstdb])
            sb.op("dve", lambda q: q.tensor_sub(scr32[:, :], scr32[:, :],
                                                rstd_b[:, :]),
                  reads=[m_scr, m_rstdb], writes=[m_scr])
            sb.op("act", lambda q: q.activation(scr32[:, :], scr32[:, :],
                                                AF.Sqrt, bias=gcol[:, 15:16]),
                  reads=[m_scr, m_gcol], writes=[m_scr])
            sb.op("dve", lambda q: q.reciprocal(rstd_b[:, :], scr32[:, :]),
                  reads=[m_scr], writes=[m_rstdb])
            if bounce_stats:
                sb.op("sq", lambda q: q.dma_start(out=bnc_stat[0:1, :],
                                                  in_=mean_b[0:1, :]),
                      reads=[m_meanb], writes=[m_statd])
                sb.op("sq", lambda q: q.dma_start(out=bnc_stat[1:2, :],
                                                  in_=rstd_b[0:1, :]),
                      reads=[m_rstdb], writes=[m_statd])
            for c in range(NCH):
                sb.op("dve", lambda q, c=c: q.tensor_sub(
                    scr32[:, :], xfm[:, c * T:(c + 1) * T], mean_b[:, :]),
                    reads=[xb[c], m_meanb], writes=[m_scr])
                sb.op("dve", lambda q, c=c: q.tensor_mul(
                    xn[:, c * T:(c + 1) * T], scr32[:, :], rstd_b[:, :]),
                    reads=[m_scr, m_rstdb], writes=[xnb[c]])

        def emit_gating(gsc, m_gsc, gat, m_gat, maskT, m_maskT):
            for c in range(NCH):
                for n in range(2):
                    sb.op("pe", lambda q, c=c, n=n: q.matmul(
                        PT[0][0:16, n * 512:(n + 1) * 512],
                        ct_sb[:, c * 16:(c + 1) * 16],
                        xfm[:, c * T + n * 512:c * T + (n + 1) * 512],
                        start=(c == 0), stop=(c == NCH - 1)),
                        reads=[m_ct, xb[c]], writes=[ptb[0]],
                        inc=(c == NCH - 1 and n == 1))
            sb.op("act", lambda q: q.copy(out=gsc[0:16, 0:T], in_=PT[0][0:16, :]),
                  reads=[ptb[0]], writes=[m_gsc])
            sb.op("dve", lambda q: q.memset(maskT[:, :], 0.0), writes=[m_maskT])
            sb.op("dve", lambda q: q.memset(maskT[32:34, :], 0.05), writes=[m_maskT])
            for t in range(8):
                sb.op("pe", lambda q, t=t: q.transpose(
                    PT[1][0:128, 0:16], gsc[0:16, t * 128:(t + 1) * 128],
                    ident[0:16, 0:16]),
                    reads=[m_gsc, m_ident], writes=[ptb[1]])
                sb.op("act", lambda q, t=t: q.copy(out=gat[:, 0:16], in_=PT[1][:, 0:16]),
                      reads=[ptb[1]], writes=[m_gat])
                sb.op("sq", lambda q, t=t: q.dma_start(
                    out=gcol[:, 0:1],
                    in_=bnc_stat[0, t * 128:(t + 1) * 128].rearrange("(p o) -> p o", o=1)),
                    reads=[m_statd], writes=[m_gcol])
                sb.op("sq", lambda q, t=t: q.dma_start(
                    out=gcol[:, 1:2],
                    in_=bnc_stat[1, t * 128:(t + 1) * 128].rearrange("(p o) -> p o", o=1)),
                    reads=[m_statd], writes=[m_gcol])
                sb.op("dve", lambda q: q.tensor_scalar(
                    gat[:, 16:32], csum_b[:, :], gcol[:, 0:1], None, ALU.mult),
                    reads=[m_csum, m_gcol], writes=[m_gat])
                sb.op("dve", lambda q: q.tensor_sub(
                    gat[:, 16:32], gat[:, 0:16], gat[:, 16:32]),
                    reads=[m_gat], writes=[m_gat])
                sb.op("dve", lambda q: q.tensor_scalar(
                    gat[:, 16:32], gat[:, 16:32], gcol[:, 1:2], None, ALU.mult),
                    reads=[m_gat, m_gcol], writes=[m_gat])
                sb.op("act", lambda q: q.activation(
                    gat[:, 32:48], gat[:, 16:32], AF.Sigmoid),
                    reads=[m_gat], writes=[m_gat])
                sb.op("dve", lambda q: q.tensor_add(
                    gat[:, 48:64], gat[:, 32:48], gb_b[:, :]),
                    reads=[m_gat, m_gbb], writes=[m_gat])
                sb.op("dve", lambda q: q.reduce_max(
                    gcol[:, 2:3], gat[:, 48:64], axis=AX.X),
                    reads=[m_gat], writes=[m_gcol])
                sb.op("dve", lambda q: q.tensor_single_scalar(
                    gat[:, 64:80], gat[:, 48:64], gcol[:, 2:3], ALU.is_equal),
                    reads=[m_gat, m_gcol], writes=[m_gat])
                sb.op("dve", lambda q: q.tensor_mul(
                    gat[:, 96:112], gat[:, 32:48], gat[:, 64:80]),
                    reads=[m_gat], writes=[m_gat])
                sb.op("dve", lambda q: q.reduce_sum(
                    gcol[:, 3:4], gat[:, 96:112], axis=AX.X),
                    reads=[m_gat], writes=[m_gcol])
                sb.op("dve", lambda q: q.scalar_tensor_tensor(
                    gat[:, 80:96], gat[:, 64:80], -1e30, gat[:, 48:64],
                    ALU.mult, ALU.add),
                    reads=[m_gat], writes=[m_gat])
                sb.op("dve", lambda q: q.reduce_max(
                    gcol[:, 4:5], gat[:, 80:96], axis=AX.X),
                    reads=[m_gat], writes=[m_gcol])
                sb.op("dve", lambda q: q.tensor_single_scalar(
                    gat[:, 96:112], gat[:, 80:96], gcol[:, 4:5], ALU.is_equal),
                    reads=[m_gat, m_gcol], writes=[m_gat])
                sb.op("dve", lambda q: q.tensor_mul(
                    gat[:, 112:128], gat[:, 32:48], gat[:, 96:112]),
                    reads=[m_gat], writes=[m_gat])
                sb.op("dve", lambda q: q.reduce_sum(
                    gcol[:, 5:6], gat[:, 112:128], axis=AX.X),
                    reads=[m_gat], writes=[m_gcol])
                sb.op("dve", lambda q: q.tensor_add(
                    gcol[:, 6:7], gcol[:, 3:4], gcol[:, 5:6]),
                    reads=[m_gcol], writes=[m_gcol])
                sb.op("dve", lambda q: q.tensor_scalar_add(
                    gcol[:, 6:7], gcol[:, 6:7], EPS_GATE),
                    reads=[m_gcol], writes=[m_gcol])
                sb.op("dve", lambda q: q.reciprocal(
                    gcol[:, 7:8], gcol[:, 6:7]),
                    reads=[m_gcol], writes=[m_gcol])
                sb.op("dve", lambda q: q.tensor_scalar(
                    gcol[:, 8:9], gcol[:, 3:4], gcol[:, 7:8], 0.5, ALU.mult, ALU.mult),
                    reads=[m_gcol], writes=[m_gcol])
                sb.op("dve", lambda q: q.tensor_scalar(
                    gcol[:, 9:10], gcol[:, 5:6], gcol[:, 7:8], 0.5, ALU.mult, ALU.mult),
                    reads=[m_gcol], writes=[m_gcol])
                sb.op("dve", lambda q: q.tensor_scalar(
                    gat[:, 0:16], gat[:, 64:80], gcol[:, 8:9], None, ALU.mult),
                    reads=[m_gat, m_gcol], writes=[m_gat])
                sb.op("dve", lambda q: q.scalar_tensor_tensor(
                    gat[:, 16:32], gat[:, 96:112], gcol[:, 9:10], gat[:, 0:16],
                    ALU.mult, ALU.add),
                    reads=[m_gat, m_gcol], writes=[m_gat])
                sb.op("pe", lambda q, t=t: q.transpose(
                    PT[1][0:16, 0:128], gat[:, 16:32], ident[:, :]),
                    reads=[m_gat, m_ident], writes=[ptb[1]])
                sb.op("act", lambda q, t=t: q.copy(
                    out=maskT[0:16, t * 128:(t + 1) * 128], in_=PT[1][0:16, 0:128]),
                    reads=[ptb[1]], writes=[m_maskT])

        def emit_ffn(tag, sub="full"):
            sc = contextlib.ExitStack()
            with sc:
                he = sc.enter_context(nc.sbuf_tensor(f"he_{tag}", [128, 16 * T], BF16))
                selb_s = sc.enter_context(nc.sbuf_tensor(f"selb_{tag}", [64, NE * 128], BF16))
                gsc = sc.enter_context(nc.sbuf_tensor(f"gsc_{tag}", [128, T], F32))
                gat = sc.enter_context(nc.sbuf_tensor(f"gat_{tag}", [128, 128], F32))
                maskT = sc.enter_context(nc.sbuf_tensor(f"maskT_{tag}", [64, T], BF16))
                mkb0 = sc.enter_context(nc.sbuf_tensor(f"mkb0_{tag}", [128, T], BF16))
                mkb1 = sc.enter_context(nc.sbuf_tensor(f"mkb1_{tag}", [128, T], BF16))
                wda = sc.enter_context(nc.sbuf_tensor(f"wda_{tag}", [128, 1024], BF16))
                wdb = sc.enter_context(nc.sbuf_tensor(f"wdb_{tag}", [128, 1024], BF16))
                heb = [buf(f"he_{tag}{j}") for j in range(16)]
                m_selb = buf(f"selb_{tag}")
                m_gsc = buf(f"gsc_{tag}")
                m_gat = buf(f"gat_{tag}")
                m_maskT = buf(f"maskT_{tag}")
                mkbB = [buf(f"mkb0_{tag}"), buf(f"mkb1_{tag}")]
                wdB = [buf(f"wda_{tag}"), buf(f"wdb_{tag}")]
                mkb = [mkb0, mkb1]
                wdt = [wda, wdb]

                sb.op("gq", lambda q: q.dma_start(out=selb_s[:, :], in_=selb_d),
                      writes=[m_selb])
                emit_ln(bounce_stats=True)
                if sub == "ln":
                    sb.barrier()
                    return
                emit_gating(gsc, m_gsc, gat, m_gat, maskT, m_maskT)
                if sub == "gate":
                    sb.barrier()
                    return

                def load_gu(e, i):
                    slot = i % 2
                    wt_g = wga if slot == 0 else wgb
                    wt_u = wua if slot == 0 else wub
                    bg = wgab if slot == 0 else wgbb
                    bu = wuab if slot == 0 else wubb
                    sb.op("gq", lambda q, e=e, i=i, w=wt_g: q.dma_start(
                        out=w[:, :], in_=wg_d[e, i].rearrange("p a b -> p (a b)")),
                        writes=[bg])
                    sb.op("gq", lambda q, e=e, i=i, w=wt_u: q.dma_start(
                        out=w[:, :], in_=wu_d[e, i].rearrange("p a b -> p (a b)")),
                        writes=[bu])

                def mask_bcast(e, eslot):
                    for n in range(2):
                        sb.op("pe", lambda q, e=e, n=n: q.matmul(
                            PT[0][:, n * 512:(n + 1) * 512],
                            selb_s[:, e * 128:(e + 1) * 128],
                            maskT[:, n * 512:(n + 1) * 512], start=True, stop=True),
                            reads=[m_selb, m_maskT], writes=[ptb[0]], inc=(n == 1))
                    sb.op("act", lambda q, eslot=eslot: q.copy(
                        out=mkb[eslot][:, :], in_=PT[0][:, :]),
                        reads=[ptb[0]], writes=[mkbB[eslot]])

                for g in range(1 if sub == "g1" else 9):
                    exps = [g * 2, g * 2 + 1]
                    for je, e in enumerate(exps):
                        mask_bcast(e, je)
                        for i in range(NI):
                            load_gu(e, i)
                            slot = i % 2
                            wt_g = wga if slot == 0 else wgb
                            wt_u = wua if slot == 0 else wub
                            bg = wgab if slot == 0 else wgbb
                            bu = wuab if slot == 0 else wubb
                            ptg, ptu = PT[2 * slot], PT[2 * slot + 1]
                            pbg, pbu = ptb[2 * slot], ptb[2 * slot + 1]
                            for kc in range(NCH):
                                for n in range(2):
                                    sb.op("pe", lambda q, w=wt_g, kc=kc, n=n, p=ptg: q.matmul(
                                        p[:, n * 512:(n + 1) * 512],
                                        w[:, kc * 128:(kc + 1) * 128],
                                        xn[:, kc * T + n * 512:kc * T + (n + 1) * 512],
                                        start=(kc == 0), stop=(kc == NCH - 1)),
                                        reads=[bg] + ([xnb[kc]] if n == 0 else []),
                                        writes=[pbg],
                                        inc=(kc == NCH - 1 and n == 1))
                            for kc in range(NCH):
                                for n in range(2):
                                    sb.op("pe", lambda q, w=wt_u, kc=kc, n=n, p=ptu: q.matmul(
                                        p[:, n * 512:(n + 1) * 512],
                                        w[:, kc * 128:(kc + 1) * 128],
                                        xn[:, kc * T + n * 512:kc * T + (n + 1) * 512],
                                        start=(kc == 0), stop=(kc == NCH - 1)),
                                        reads=[bu] + ([xnb[kc]] if n == 0 else []),
                                        writes=[pbu],
                                        inc=(kc == NCH - 1 and n == 1))
                            st = stg0 if slot == 0 else stg1
                            m_st = m_stg0 if slot == 0 else m_stg1
                            sb.op("act", lambda q, p=ptg, st=st: q.activation(
                                st[:, :], p[:, :], AF.Silu),
                                reads=[pbg], writes=[m_st])
                            hslot = je * 8 + i
                            sb.op("dve", lambda q, p=ptu, st=st, h=hslot: q.tensor_mul(
                                he[:, h * T:(h + 1) * T], p[:, :], st[:, :]),
                                reads=[pbu, m_st], writes=[heb[hslot]])
                            sb.op("dve", lambda q, h=hslot, je=je: q.tensor_mul(
                                he[:, h * T:(h + 1) * T], he[:, h * T:(h + 1) * T],
                                mkb[je][:, :]),
                                reads=[heb[hslot], mkbB[je]], writes=[heb[hslot]])
                    for f in range(NCH):
                        for je, e in enumerate(exps):
                            w = wdt[je]
                            wb = wdB[je]
                            sb.op("gq", lambda q, e=e, f=f, w=w: q.dma_start(
                                out=w[:, :], in_=wd_d[e, f].rearrange("p a b -> p (a b)")),
                                writes=[wb])
                            for i in range(NI):
                                h = je * 8 + i
                                first = (je == 0 and i == 0)
                                last = (je == 1 and i == NI - 1)
                                pt = PT[2 + (f % 2)]
                                pb = ptb[2 + (f % 2)]
                                for n in range(2):
                                    sb.op("pe", lambda q, w=w, i=i, n=n, h=h, pt=pt,
                                          fi=first, la=last: q.matmul(
                                        pt[:, n * 512:(n + 1) * 512],
                                        w[:, i * 128:(i + 1) * 128],
                                        he[:, h * T + n * 512:h * T + (n + 1) * 512],
                                        start=fi, stop=la),
                                        reads=[wb] + ([heb[h]] if n == 0 else []),
                                        writes=[pb],
                                        inc=(last and n == 1))
                        pt = PT[2 + (f % 2)]
                        pb = ptb[2 + (f % 2)]
                        sb.op("dve", lambda q, f=f, pt=pt: q.tensor_add(
                            xfm[:, f * T:(f + 1) * T], pt[:, :],
                            xfm[:, f * T:(f + 1) * T]),
                            reads=[pb, xb[f]], writes=[xb[f]])
                sb.barrier()

        def emit_qkv_and_exchange():
            sc = contextlib.ExitStack()
            with sc:
                vstg0 = sc.enter_context(nc.sbuf_tensor("vstg0", [128, 2080], BF16))
                vstg1 = sc.enter_context(nc.sbuf_tensor("vstg1", [128, 2080], BF16))
                bvb = sc.enter_context(nc.sbuf_tensor("bvb", [128, D], BF16))
                q2t = sc.enter_context(nc.sbuf_tensor("q2t", [128, T], BF16))
                sel2 = sc.enter_context(nc.sbuf_tensor("sel2", [128, 2], BF16))
                bndq = sc.enter_context(nc.sbuf_tensor("bndq", [32, T], F32))
                bnds = sc.enter_context(nc.sbuf_tensor("bnds", [32, T], F32))
                bndb = sc.enter_context(nc.sbuf_tensor("bndb", [32, 2 * T], BF16))
                k2f = sc.enter_context(nc.sbuf_tensor("k2f", [32, S], F32))
                kmx = sc.enter_context(nc.sbuf_tensor("kmx", [32, 2], F32))
                vstgB = [buf("vstg0"), buf("vstg1")]
                vstg = [vstg0, vstg1]
                m_bvb = buf("bvb")
                m_q2 = buf("q2t")
                m_sel2 = buf("sel2")
                m_bndq = buf("bndq")
                m_bnds = buf("bnds")
                m_bndb = buf("bndb")
                m_k2f = buf("k2f")
                m_kmx = buf("kmx")

                emit_ln(bounce_stats=False)

                sb.op("dve", lambda q: q.memset(sel2[:, :], 0.0), writes=[m_sel2])
                sb.op("dve", lambda q: q.memset(sel2[0:64, 0:1], 1.0), writes=[m_sel2])
                sb.op("dve", lambda q: q.memset(sel2[64:128, 1:2], 1.0), writes=[m_sel2])

                sb.op("sq", lambda q: q.dma_start(
                    out=rows32[0:1, 0:D], in_=bv_d.rearrange("(o n) -> o n", o=1)),
                    writes=[m_rows])
                for j in range(4):
                    pt = PT[2] if j < 2 else PT[3]
                    pb = ptb[2] if j < 2 else ptb[3]
                    off = (j % 2) * 512
                    sb.op("pe", lambda q, j=j, pt=pt, off=off: q.matmul(
                        pt[:, off:off + 512], ones_r[:, :],
                        rows32[0:1, j * 512:(j + 1) * 512], start=True, stop=True),
                        reads=[m_ones, m_rows], writes=[pb], inc=(j % 2 == 1))
                for j in range(4):
                    pt = PT[2] if j < 2 else PT[3]
                    pb = ptb[2] if j < 2 else ptb[3]
                    off = (j % 2) * 512
                    sb.op("act", lambda q, j=j, pt=pt, off=off: q.copy(
                        out=bvb[:, j * 512:(j + 1) * 512], in_=pt[:, off:off + 512]),
                        reads=[pb], writes=[m_bvb])

                for which, (w_d, bcol, parkd, m_park, normd, m_normd) in enumerate(
                        ((wq_d, 0, q_park, m_qpark, bnc_qn, m_qnd),
                         (wk_d, 1, k_park, m_kpark, kn2_loc, m_kn2d))):
                    for fc in range(NCH):
                        slot = fc % 2
                        wt = wga if slot == 0 else wgb
                        wtb = wgab if slot == 0 else wgbb
                        sb.op("gq", lambda q, fc=fc, wt=wt, w_d=w_d: q.dma_start(
                            out=wt[:, :], in_=w_d[fc].rearrange("p a b -> p (a b)")),
                            writes=[wtb])
                        pt, pb = PT[slot], ptb[slot]
                        for kc in range(NCH):
                            for n in range(2):
                                sb.op("pe", lambda q, wt=wt, kc=kc, n=n, pt=pt: q.matmul(
                                    pt[:, n * 512:(n + 1) * 512],
                                    wt[:, kc * 128:(kc + 1) * 128],
                                    xn[:, kc * T + n * 512:kc * T + (n + 1) * 512],
                                    start=(kc == 0), stop=(kc == NCH - 1)),
                                    reads=[wtb] + ([xnb[kc]] if n == 0 else []),
                                    writes=[pb], inc=(kc == NCH - 1 and n == 1))
                        st = stg0 if slot == 0 else stg1
                        m_st = m_stg0 if slot == 0 else m_stg1
                        sb.op("act", lambda q, pt=pt, st=st, fc=fc, bcol=bcol: q.activation(
                            st[:, :], pt[:, :], AF.Identity,
                            bias=biases[:, bcol * 16 + fc:bcol * 16 + fc + 1]),
                            reads=[pb, m_bias], writes=[m_st])
                        sb.op("sq", lambda q, st=st, fc=fc, parkd=parkd: q.dma_start(
                            out=parkd[fc * 128:(fc + 1) * 128, :], in_=st[:, :]),
                            reads=[m_st], writes=[m_park])
                        sb.op("act", lambda q, st=st: q.activation(
                            q2t[:, :], st[:, :], AF.Square),
                            reads=[m_st], writes=[m_q2])
                        for n in range(2):
                            sb.op("pe", lambda q, n=n: q.matmul(
                                PT[2][0:2, n * 512:(n + 1) * 512], sel2[:, :],
                                q2t[:, n * 512:(n + 1) * 512],
                                start=True, stop=True),
                                reads=[m_q2, m_sel2], writes=[ptb[2]], inc=(n == 1))
                        sb.op("act", lambda q: q.copy(
                            out=scr32[0:2, 0:T], in_=PT[2][0:2, :]),
                            reads=[ptb[2]], writes=[m_scr])
                        for hh in range(2):
                            sb.op("sq", lambda q, fc=fc, hh=hh, normd=normd: q.dma_start(
                                out=normd[fc * 2 + hh:fc * 2 + hh + 1, :],
                                in_=scr32[hh:hh + 1, 0:T]),
                                reads=[m_scr], writes=[m_normd])

                sb.op("dve", lambda q: q.memset(
                    vstg0.ap().rearrange("p (h e) -> p h e", e=65)[:, :, 64:65], 1.0),
                    writes=[vstgB[0]])
                sb.op("dve", lambda q: q.memset(
                    vstg1.ap().rearrange("p (h e) -> p h e", e=65)[:, :, 64:65], 1.0),
                    writes=[vstgB[1]])
                for tp in range(4):
                    for kc in range(NCH):
                        slot = kc % 2
                        wt = wga if slot == 0 else wgb
                        wtb = wgab if slot == 0 else wgbb
                        sb.op("gq", lambda q, kc=kc, wt=wt: q.dma_start(
                            out=wt[:, :], in_=wv_d[kc * 128:(kc + 1) * 128, :]),
                            writes=[wtb])
                        for half in range(2):
                            tc = tp * 2 + half
                            pts = (PT[0], PT[1]) if half == 0 else (PT[2], PT[3])
                            pbs = (ptb[0], ptb[1]) if half == 0 else (ptb[2], ptb[3])
                            for j in range(4):
                                pt = pts[j // 2]
                                pb = pbs[j // 2]
                                off = (j % 2) * 512
                                sb.op("pe", lambda q, tc=tc, kc=kc, j=j, wt=wt,
                                      pt=pt, off=off: q.matmul(
                                    pt[:, off:off + 512],
                                    xn[:, kc * T + tc * 128:kc * T + (tc + 1) * 128],
                                    wt[:, j * 512:(j + 1) * 512],
                                    start=(kc == 0), stop=(kc == NCH - 1)),
                                    reads=[wtb] + ([xnb[kc]] if j == 0 else []),
                                    writes=[pb],
                                    inc=(kc == NCH - 1 and j == 3))
                    for half in range(2):
                        tc = tp * 2 + half
                        pts = (PT[0], PT[1]) if half == 0 else (PT[2], PT[3])
                        pbs = (ptb[0], ptb[1]) if half == 0 else (ptb[2], ptb[3])
                        vs = vstg[half]
                        vb = vstgB[half]
                        for j in range(4):
                            pt = pts[j // 2]
                            pb = pbs[j // 2]
                            off = (j % 2) * 512
                            sb.op("dve", lambda q, vs=vs, j=j, pt=pt, off=off: q.tensor_add(
                                vs.ap()[:, j * 520:(j + 1) * 520].rearrange(
                                    "p (h e) -> p h e", e=65)[:, :, 0:64],
                                pt[:, off:off + 512].rearrange("p (h e) -> p h e", e=64),
                                bvb[:, j * 512:(j + 1) * 512].rearrange(
                                    "p (h e) -> p h e", e=64)),
                                reads=[pb, m_bvb], writes=[vb])
                        sb.op("sq", lambda q, tc=tc, vs=vs: q.dma_start(
                            out=v_park[tc * 128:(tc + 1) * 128, :], in_=vs[:, :]),
                            reads=[vb], writes=[m_vpark])

                sb.op("sq", lambda q: q.dma_start(out=bndq[:, :], in_=bnc_qn),
                      reads=[m_qnd], writes=[m_bndq])
                sb.op("act", lambda q: q.activation(bnds[:, :], bndq[:, :], AF.Sqrt),
                      reads=[m_bndq], writes=[m_bnds])

                PAIRS = [[0, 1], [2, 3], [4, 5], [6, 7]]
                sb.op("gq", lambda q: q.dma_start(out=u32t[0:1, 3:4],
                                                  in_=u32t[0:1, 2:3]),
                      reads=[m_kpark, m_vpark, m_kn2d])

                def colls(q):
                    q.collective_compute(
                        "AllGather", ALU.bypass, replica_groups=PAIRS,
                        ins=[k_park.opt()], outs=[kg.opt()]).then_inc(cc_k)
                    q.wait_ge(cc_k, 1)
                    q.collective_compute(
                        "AllGather", ALU.bypass, replica_groups=PAIRS,
                        ins=[v_park.opt()], outs=[vg.opt()]).then_inc(cc_v)
                    q.wait_ge(cc_v, 1)
                    q.collective_compute(
                        "AllGather", ALU.bypass, replica_groups=PAIRS,
                        ins=[kn2_loc.opt()], outs=[ng.opt()]).then_inc(cc_n)
                    q.wait_ge(cc_n, 1)
                sb.raw("gq", colls)
                sb.op("gq", lambda q: q.dma_start(out=u32t[0:1, 3:4],
                                                  in_=u32t[0:1, 2:3]),
                      writes=[m_sh])

                sb.op("sq", lambda q: q.dma_start(out=k2f[:, 0:T], in_=ng[0:NH, :]),
                      reads=[m_sh], writes=[m_k2f])
                sb.op("sq", lambda q: q.dma_start(out=k2f[:, T:2 * T], in_=ng[NH:2 * NH, :]),
                      reads=[m_sh], writes=[m_k2f])
                sb.op("dve", lambda q: q.reduce_max(kmx[:, 0:1], k2f[:, :], axis=AX.X),
                      reads=[m_k2f], writes=[m_kmx])
                sb.op("act", lambda q: q.activation(kmx[:, 1:2], kmx[:, 0:1], AF.Sqrt),
                      reads=[m_kmx], writes=[m_kmx])
                sb.op("dve", lambda q: q.tensor_scalar(
                    bndq[:, :], bnds[:, :], kmx[:, 1:2], -1.01, ALU.mult, ALU.mult),
                    reads=[m_bnds, m_kmx], writes=[m_bndq])
                sb.op("dve", lambda q: q.tensor_scalar_add(
                    bndq[:, :], bndq[:, :], -6.0),
                    reads=[m_bndq], writes=[m_bndq])
                sb.op("dve", lambda q: q.tensor_copy(bndb[:, 0:T], bndq[:, :]),
                      reads=[m_bndq], writes=[m_bndb])
                sb.op("dve", lambda q: q.tensor_sub(bnds[:, :], bndq[:, :],
                                                    bndb[:, 0:T]),
                      reads=[m_bndq, m_bndb], writes=[m_bnds])
                sb.op("dve", lambda q: q.tensor_copy(bndb[:, T:2 * T], bnds[:, :]),
                      reads=[m_bnds], writes=[m_bndb])
                sb.op("sq", lambda q: q.dma_start(out=bhl_park[0:32, :], in_=bndb[:, 0:T]),
                      reads=[m_bndb], writes=[m_bhl])
                sb.op("sq", lambda q: q.dma_start(out=bhl_park[32:64, :], in_=bndb[:, T:2 * T]),
                      reads=[m_bndb], writes=[m_bhl])
                sb.barrier()

        def emit_attention_core():
            sc = contextlib.ExitStack()
            with sc:
                atf = sc.enter_context(nc.sbuf_tensor("atf", [128, 16 * T], BF16))
                vq = sc.enter_context(nc.sbuf_tensor("vq", [128, 16 * 520], BF16))
                ka0 = sc.enter_context(nc.sbuf_tensor("ka0", [66, S], BF16))
                ka1 = sc.enter_context(nc.sbuf_tensor("ka1", [66, S], BF16))
                qa0 = sc.enter_context(nc.sbuf_tensor("qa0", [66, T], BF16))
                qa1 = sc.enter_context(nc.sbuf_tensor("qa1", [66, T], BF16))
                pb0 = sc.enter_context(nc.sbuf_tensor("pb0", [128, T], BF16))
                pb1 = sc.enter_context(nc.sbuf_tensor("pb1", [128, T], BF16))
                shm = sc.enter_context(nc.sbuf_tensor("shm", [64, 128], BF16))
                odd = sc.enter_context(nc.sbuf_tensor("odd", [64, T], BF16))
                atfb = [buf(f"atf{j}") for j in range(NCH)]
                m_vq = buf("vq")
                kaB = [buf("ka0"), buf("ka1")]
                qaB = [buf("qa0"), buf("qa1")]
                pbB = [buf("pb0"), buf("pb1")]
                m_shm = buf("shm")
                m_odd = buf("odd")
                ka = [ka0, ka1]
                qa = [qa0, qa1]
                pbt = [pb0, pb1]

                sb.op("dve", lambda q: q.memset(ka0[64:66, :], 1.0), writes=[kaB[0]])
                sb.op("dve", lambda q: q.memset(ka1[64:66, :], 1.0), writes=[kaB[1]])
                sb.op("gq", lambda q: q.dma_start(out=shm[:, :], in_=shmat_d),
                      writes=[m_shm])

                for quarter in range(4):
                    for kc in range(NCH):
                        sb.op("aq", lambda q, quarter=quarter, kc=kc: q.dma_start(
                            out=vq[:, kc * 520:(kc + 1) * 520],
                            in_=vg[kc * 128:(kc + 1) * 128,
                                   quarter * 520:(quarter + 1) * 520]),
                            reads=[m_sh], writes=[m_vq])
                    for hh in range(8):
                        h = quarter * 8 + hh
                        hs = hh % 2
                        kat, qat = ka[hs], qa[hs]
                        kab, qab = kaB[hs], qaB[hs]
                        sb.op("aq", lambda q, h=h, kat=kat: q.dma_start(
                            out=kat[0:64, 0:T], in_=kg[h * 64:(h + 1) * 64, :]),
                            reads=[m_sh], writes=[kab])
                        sb.op("aq", lambda q, h=h, kat=kat: q.dma_start(
                            out=kat[0:64, T:2 * T],
                            in_=kg[D + h * 64:D + (h + 1) * 64, :]),
                            reads=[m_sh], writes=[kab])
                        sb.op("aq", lambda q, h=h, qat=qat: q.dma_start(
                            out=qat[0:64, :], in_=q_park[h * 64:(h + 1) * 64, :]),
                            reads=[m_qpark], writes=[qab])
                        sb.op("aq", lambda q, h=h, qat=qat: q.dma_start(
                            out=qat[64:65, :], in_=bhl_park[h:h + 1, :]),
                            reads=[m_bhl], writes=[qab])
                        sb.op("aq", lambda q, h=h, qat=qat: q.dma_start(
                            out=qat[65:66, :], in_=bhl_park[32 + h:33 + h, :]),
                            reads=[m_bhl], writes=[qab])
                        avp = PT[2 + hs]
                        avb = ptb[2 + hs]
                        for kc in range(NCH):
                            qkp = PT[kc % 2]
                            qkb = ptb[kc % 2]
                            for n in range(2):
                                sb.op("pe", lambda q, kat=kat, qat=qat, kc=kc, n=n,
                                      qkp=qkp: q.matmul(
                                    qkp[:, n * 512:(n + 1) * 512],
                                    kat[:, kc * 128:(kc + 1) * 128],
                                    qat[:, n * 512:(n + 1) * 512],
                                    start=True, stop=True),
                                    reads=[kab, qab], writes=[qkb], inc=(n == 1))
                            pbuf = pbt[kc % 2]
                            pbb = pbB[kc % 2]
                            sb.op("act", lambda q, qkp=qkp, pbuf=pbuf: q.activation(
                                pbuf[:, :], qkp[:, :], AF.Exp, scale=SCALE),
                                reads=[qkb], writes=[pbb])
                            for n in range(2):
                                sb.op("pe", lambda q, kc=kc, hh=hh, n=n, pbuf=pbuf,
                                      avp=avp: q.matmul(
                                    avp[0:65, n * 512:(n + 1) * 512],
                                    vq[:, kc * 520 + hh * 65:kc * 520 + (hh + 1) * 65],
                                    pbuf[:, n * 512:(n + 1) * 512],
                                    start=(kc == 0), stop=(kc == NCH - 1)),
                                    reads=[m_vq, pbb], writes=[avb],
                                    inc=(n == 1))
                        sp = h % 2
                        sb.op("act", lambda q, avp=avp: q.copy(
                            out=scr32[64:65, 0:T], in_=avp[64:65, :]),
                            reads=[avb], writes=[m_scr])
                        sb.op("sq", lambda q, sp=sp: q.dma_start(
                            out=bnc_sums[sp:sp + 1, :], in_=scr32[64:65, 0:T]),
                            reads=[m_scr], writes=[m_sumsd])
                        sb.op("sq", lambda q, sp=sp: q.dma_start(
                            out=mean_b[0:1, :], in_=bnc_sums[sp:sp + 1, :]),
                            reads=[m_sumsd], writes=[m_meanb])
                        sb.op("dve", lambda q: q.reciprocal(
                            rstd_b[0:1, :], mean_b[0:1, :]),
                            reads=[m_meanb], writes=[m_rstdb])
                        for n in range(2):
                            sb.op("pe", lambda q, n=n: q.matmul(
                                PT[0][0:64, n * 512:(n + 1) * 512],
                                ones_r[:, 0:64],
                                rstd_b[0:1, n * 512:(n + 1) * 512],
                                start=True, stop=True),
                                reads=[m_ones, m_rstdb], writes=[ptb[0]], inc=(n == 1))
                        sb.op("act", lambda q: q.copy(
                            out=scr32[0:64, :], in_=PT[0][0:64, :]),
                            reads=[ptb[0]], writes=[m_scr])
                        if h % 2 == 0:
                            sb.op("dve", lambda q, h=h, avp=avp: q.tensor_mul(
                                atf[0:64, (h // 2) * T:(h // 2 + 1) * T],
                                avp[0:64, :], scr32[0:64, :]),
                                reads=[avb, m_scr], writes=[atfb[h // 2]])
                        else:
                            sb.op("dve", lambda q, avp=avp: q.tensor_mul(
                                odd[:, :], avp[0:64, :], scr32[0:64, :]),
                                reads=[avb, m_scr], writes=[m_odd])
                            for n in range(2):
                                sb.op("pe", lambda q, n=n: q.matmul(
                                    PT[1][:, n * 512:(n + 1) * 512], shm[:, :],
                                    odd[:, n * 512:(n + 1) * 512],
                                    start=True, stop=True),
                                    reads=[m_shm, m_odd], writes=[ptb[1]], inc=(n == 1))
                            sb.op("act", lambda q, h=h: q.copy(
                                out=atf[64:128, (h // 2) * T:(h // 2 + 1) * T],
                                in_=PT[1][64:128, :]),
                                reads=[ptb[1]], writes=[atfb[h // 2]])

                for fc in range(NCH):
                    slot = fc % 2
                    wt = wga if slot == 0 else wgb
                    wtb = wgab if slot == 0 else wgbb
                    sb.op("gq", lambda q, fc=fc, wt=wt: q.dma_start(
                        out=wt[:, :], in_=wo_d[fc].rearrange("p a b -> p (a b)")),
                        writes=[wtb])
                    pt, pb = PT[slot], ptb[slot]
                    for kc in range(NCH):
                        for n in range(2):
                            sb.op("pe", lambda q, wt=wt, kc=kc, n=n, pt=pt: q.matmul(
                                pt[:, n * 512:(n + 1) * 512],
                                wt[:, kc * 128:(kc + 1) * 128],
                                atf[:, kc * T + n * 512:kc * T + (n + 1) * 512],
                                start=(kc == 0), stop=(kc == NCH - 1)),
                                reads=[wtb] + ([atfb[kc]] if n == 0 else []),
                                writes=[pb], inc=(kc == NCH - 1 and n == 1))
                    sb.op("dve", lambda q, fc=fc, pt=pt: q.scalar_tensor_tensor(
                        xfm[:, fc * T:(fc + 1) * T], pt[:, :],
                        biases[:, 32 + fc:33 + fc], xfm[:, fc * T:(fc + 1) * T],
                        ALU.add, ALU.add),
                        reads=[pb, m_bias, xb[fc]], writes=[xb[fc]])
                sb.barrier()

        def dump_x(dst):
            for c in range(NCH):
                sb.op("sq", lambda q, c=c: q.dma_start(
                    out=dst[c * 128:(c + 1) * 128, :],
                    in_=xfm[:, c * T:(c + 1) * T]),
                    reads=[xb[c]])

        if stage.startswith("ffn1_"):
            emit_ffn("f1", sub=stage.split("_", 1)[1])
            dump_x(dbg1_d)
        elif stage in ("ffn1", "qkv", "attn", "full"):
            emit_ffn("f1")
            dump_x(dbg1_d)
        if stage in ("qkv", "attn", "full"):
            emit_qkv_and_exchange()
        if stage in ("attn", "full"):
            emit_attention_core()
            dump_x(dbg2_d)
        if stage == "full":
            emit_ffn("f2")

        m_out = buf("out")
        for c in range(NCH):
            sb.op("sq", lambda q, c=c: q.dma_start(
                out=out_d[c * 128:(c + 1) * 128, :], in_=xfm[:, c * T:(c + 1) * T]),
                reads=[xb[c]], writes=[m_out])
        fin = sb.cnt["sq"]
        sb.raw("sq", lambda q: q.wait_ge(sb.sems["sq"], fin))

        blk = es.enter_context(nc.Block())
        for sect in ("tensor", "scalar", "vector", "gpsimd", "sync"):
            fns = sb.prog[sect]

            def body(q, _fns=fns):
                for f in _fns:
                    f(q)
            getattr(blk, sect)(body)

    return nc



_CACHE = {}


def _relayout(inputs):
    f = np.float32
    wq = inputs["L0_attn_Wq"].astype(f, copy=False)
    wk = inputs["L0_attn_Wk"].astype(f, copy=False)
    wo = inputs["L0_attn_Wo"].astype(f, copy=False)

    def wpl(w):
        return np.ascontiguousarray(
            w.reshape(16, 128, 16, 128).transpose(2, 1, 0, 3))

    wg = inputs["L0_ffn_wg"]
    wu = inputs["L0_ffn_wu"]
    wd = inputs["L0_ffn_wd"]
    wg_s = inputs["L0_ffn_wg_s"]
    wu_s = inputs["L0_ffn_wu_s"]
    wd_s = inputs["L0_ffn_wd_s"]

    def gu_pl(m, ms):
        a = m.reshape(16, 16, 128, 8, 128).transpose(0, 3, 2, 1, 4)
        b = ms.reshape(16, 128, 2, 8, 128).transpose(2, 3, 1, 0, 4)
        return np.ascontiguousarray(np.concatenate([a, b], axis=0))

    def d_pl(m, ms):
        a = m.reshape(16, 8, 128, 16, 128).transpose(0, 3, 2, 1, 4)
        b = ms.reshape(2, 8, 128, 16, 128).transpose(0, 3, 2, 1, 4)
        return np.ascontiguousarray(np.concatenate([a, b], axis=0))

    selb = np.zeros((64, NE * 128), np.float32)
    for e in range(NE):
        r = e if e < 16 else 16 + e
        selb[r, e * 128:(e + 1) * 128] = 1.0
    import ml_dtypes
    selb = selb.astype(ml_dtypes.bfloat16)
    shmat = np.zeros((64, 128), np.float32)
    for p in range(64):
        shmat[p, 64 + p] = 1.0
    shmat = shmat.astype(ml_dtypes.bfloat16)

    ct = np.ascontiguousarray(
        inputs["L0_ffn_centroids"].astype(f, copy=False).T.reshape(16, 128, 16))

    return {
        "wq": wpl(wq), "wk": wpl(wk), "wo": wpl(wo),
        "wv": np.ascontiguousarray(inputs["L0_attn_Wv"].astype(f, copy=False)),
        "wg": gu_pl(wg, wg_s), "wu": gu_pl(wu, wu_s), "wd": d_pl(wd, wd_s),
        "ct": ct,
        "bq": np.ascontiguousarray(inputs["L0_attn_bq"].astype(f, copy=False)),
        "bk": np.ascontiguousarray(inputs["L0_attn_bk"].astype(f, copy=False)),
        "bv": np.ascontiguousarray(inputs["L0_attn_bv"].astype(f, copy=False)),
        "bo": np.ascontiguousarray(inputs["L0_attn_bo"].astype(f, copy=False)),
        "gb": np.ascontiguousarray(inputs["L0_ffn_gate_bias"].astype(f, copy=False)),
        "ident": np.eye(128, dtype=np.float32),
        "selb": selb, "shmat": shmat,
    }


def kernel(**inputs):
    x = np.asarray(inputs["L0_attn_x"], dtype=np.float32)
    shared = _relayout(inputs)

    nc = build_program()
    in_maps = []
    for c in range(8):
        b, h = c // 2, c % 2
        m = dict(shared)
        m["x_in"] = np.ascontiguousarray(x[b, h * T:(h + 1) * T, :].T)
        in_maps.append(m)
    res = run_bass_kernel_spmd(nc, in_maps, core_ids=list(range(8)))
    out = np.empty((4, 2048, 2048), np.float32)
    for c in range(8):
        b, h = c // 2, c % 2
        out[b, h * T:(h + 1) * T, :] = res.results[c]["out"].T
    return out


# version 04
# speedup vs baseline: 1.0163x; 1.0163x over previous
import contextlib
import numpy as np
import concourse.bass as bass
import concourse.mybir as mybir
from concourse.bass_utils import run_bass_kernel_spmd

F32 = mybir.dt.float32
BF16 = mybir.dt.bfloat16
U32 = mybir.dt.uint32
AF = mybir.ActivationFunctionType
ALU = mybir.AluOpType
AX = mybir.AxisListType

T = 1024
D = 2048
NCH = 16
NI = 8
NE = 18
NH = 32
S = 2048
SCALE = float(1.0 / np.sqrt(np.float32(32.0)))
EPS_LN = 1e-5
EPS_GATE = 1e-9



class Buf:
    __slots__ = ("name", "w", "readers")

    def __init__(self, name):
        self.name = name
        self.w = None
        self.readers = []


class SB:
    ENGS = ("pe", "act", "dve", "pool", "sq", "aq", "gq")
    DMA = frozenset(("sq", "aq", "gq"))
    SECT = {"pe": "tensor", "act": "scalar", "dve": "vector",
            "pool": "gpsimd", "sq": "sync", "aq": "scalar", "gq": "gpsimd"}

    def __init__(self):
        self.prog = {"tensor": [], "scalar": [], "vector": [], "gpsimd": [], "sync": []}
        self.cnt = {e: 0 for e in self.ENGS}
        self.hw = {e: {d: 0 for d in self.ENGS} for e in self.ENGS}
        self.pend = {e: [] for e in self.ENGS}
        self.sems = {}

    def raw(self, eng, fn):
        self.prog[self.SECT[eng]].append(fn)

    def _wait(self, eng, dep, val):
        if val <= self.hw[eng][dep]:
            return
        self.hw[eng][dep] = val
        sems = self.sems

        def w(q, _d=dep, _v=val):
            q.wait_ge(sems[_d], _v)
        self.raw(eng, w)

    def op(self, eng, fn, reads=(), writes=(), inc=True):
        deps = {}
        for b in reads:
            if b.w is not None:
                e, c = b.w
                deps[e] = max(deps.get(e, 0), c)
        for b in writes:
            if b.w is not None:
                e, c = b.w
                deps[e] = max(deps.get(e, 0), c)
            for e, c in b.readers:
                deps[e] = max(deps.get(e, 0), c)
        for e, c in deps.items():
            if e == "pe" and eng == "pe":
                continue
            self._wait(eng, e, c)
        step = 16 if eng in self.DMA else 1
        if inc:
            self.cnt[eng] += step
            newc = self.cnt[eng]
            sems = self.sems

            def run(q, _fn=fn, _e=eng):
                _fn(q).then_inc(sems[_e], 16 if _e in self.DMA else 1)
            self.raw(eng, run)
            for b, is_w in self.pend[eng]:
                if is_w:
                    b.w = (eng, newc)
                    b.readers = []
                else:
                    b.readers.append((eng, newc))
            self.pend[eng].clear()
            for b in writes:
                b.w = (eng, newc)
                b.readers = []
            for b in reads:
                b.readers.append((eng, newc))
        else:
            self.raw(eng, lambda q, _fn=fn: _fn(q))
            for b in writes:
                self.pend[eng].append((b, True))
            for b in reads:
                self.pend[eng].append((b, False))

    def barrier(self):
        for e in self.ENGS:
            assert not self.pend[e], f"pending unstamped ops on {e} at barrier"



def build_program(stage="full"):
    nc = bass.Bass("TRN2", target_bir_lowering=False)
    sb = SB()

    def ein(name, shape, dt=F32):
        return nc.dram_tensor(name, shape, dt, kind="ExternalInput").ap()

    x_in = ein("x_in", [D, T])
    wq_d = ein("wq", [NCH, 128, NCH, 128])
    wk_d = ein("wk", [NCH, 128, NCH, 128])
    wo_d = ein("wo", [NCH, 128, NCH, 128])
    wv_d = ein("wv", [D, D])
    wg_d = ein("wg", [NE, NI, 128, NCH, 128])
    wu_d = ein("wu", [NE, NI, 128, NCH, 128])
    wd_d = ein("wd", [NE, NCH, 128, NI, 128])
    ct_d = ein("ct", [NCH, 128, 16])
    bq_d = ein("bq", [D])
    bk_d = ein("bk", [D])
    bv_d = ein("bv", [D])
    bo_d = ein("bo", [D])
    gb_d = ein("gb", [16])
    ident_d = ein("ident", [128, 128])
    selb_d = ein("selb", [64, NE * 128], BF16)
    shmat_d = ein("shmat", [64, 128], BF16)
    out_d = nc.dram_tensor("out", [D, T], F32, kind="ExternalOutput").ap()
    dbg1_d = nc.dram_tensor("dbg1", [D, T], F32, kind="ExternalOutput").ap()
    dbg2_d = nc.dram_tensor("dbg2", [D, T], F32, kind="ExternalOutput").ap()

    def dint(name, shape, dt):
        return nc.dram_tensor(name, shape, dt, kind="Internal").ap()

    q_park = dint("q_park", [D, T], BF16)
    k_park = dint("k_park", [D, T], BF16)
    v_park = dint("v_park", [T, NH * 65], BF16)
    bhl_park = dint("bhl_park", [64, T], BF16)
    bnc_stat = dint("bnc_stat", [2, T], F32)
    bnc_csum = dint("bnc_csum", [16], F32)
    bnc_qn = dint("bnc_qn", [NH, T], F32)
    kn2_loc = dint("kn2_loc", [NH, T], F32)
    bnc_sums = dint("bnc_sums", [2, T], F32)

    kg = nc.dram_tensor("kg", [2 * D, T], BF16, kind="Internal").ap()
    vg = nc.dram_tensor("vg", [S, NH * 65], BF16, kind="Internal").ap()
    ng = nc.dram_tensor("ng", [2 * NH, T], F32, kind="Internal").ap()

    es = contextlib.ExitStack()
    B = {}

    def buf(name):
        if name not in B:
            B[name] = Buf(name)
        return B[name]

    with es:
        for e in SB.ENGS:
            sb.sems[e] = es.enter_context(nc.semaphore("sem_" + e))
        cc_k = es.enter_context(nc.semaphore("cc_k"))
        cc_v = es.enter_context(nc.semaphore("cc_v"))
        cc_n = es.enter_context(nc.semaphore("cc_n"))

        xfm = es.enter_context(nc.sbuf_tensor("xfm", [128, NCH * T], F32))
        xn = es.enter_context(nc.sbuf_tensor("xn", [128, NCH * T], BF16))
        wga = es.enter_context(nc.sbuf_tensor("wga", [128, 2048], BF16))
        wgb = es.enter_context(nc.sbuf_tensor("wgb", [128, 2048], BF16))
        wua = es.enter_context(nc.sbuf_tensor("wua", [128, 2048], BF16))
        wub = es.enter_context(nc.sbuf_tensor("wub", [128, 2048], BF16))
        mean_b = es.enter_context(nc.sbuf_tensor("mean_b", [128, T], F32))
        rstd_b = es.enter_context(nc.sbuf_tensor("rstd_b", [128, T], F32))
        scr32 = es.enter_context(nc.sbuf_tensor("scr32", [128, T], F32))
        rows32 = es.enter_context(nc.sbuf_tensor("rows32", [128, 2048], F32))
        stg0 = es.enter_context(nc.sbuf_tensor("stg0", [128, T], BF16))
        stg1 = es.enter_context(nc.sbuf_tensor("stg1", [128, T], BF16))
        ident = es.enter_context(nc.sbuf_tensor("ident_sb", [128, 128], F32))
        biases = es.enter_context(nc.sbuf_tensor("biases", [128, 48], F32))
        gcol = es.enter_context(nc.sbuf_tensor("gcol", [128, 48], F32))
        csum_b = es.enter_context(nc.sbuf_tensor("csum_b", [128, 16], F32))
        gb_b = es.enter_context(nc.sbuf_tensor("gb_b", [128, 16], F32))
        ones_c = es.enter_context(nc.sbuf_tensor("ones_c", [128, 1], F32))
        ones_cb = es.enter_context(nc.sbuf_tensor("ones_cb", [128, 1], BF16))
        ones_r = es.enter_context(nc.sbuf_tensor("ones_r", [1, 128], F32))
        ones2d = es.enter_context(nc.sbuf_tensor("ones2d", [128, 128], F32))
        u32t = es.enter_context(nc.sbuf_tensor("u32t", [1, 4], U32))

        PT = [es.enter_context(nc.psum_tensor(f"pt{i}", [128, 1024], F32)) for i in range(4)]
        ptb = [buf(f"pt{i}") for i in range(4)]

        xb = [buf(f"x{c}") for c in range(NCH)]
        xnb = [buf(f"xn{c}") for c in range(NCH)]
        wgab, wgbb = buf("wga"), buf("wgb")
        wuab, wubb = buf("wua"), buf("wub")
        m_bias = buf("biases")
        m_ident = buf("ident")
        m_gcol = buf("gcol")
        m_csum = buf("csum_b")
        m_gbb = buf("gb_b")
        m_ones = buf("ones")
        m_ones2 = buf("ones2d")
        m_u32 = buf("u32t")
        m_meanb = buf("mean_b")
        m_rstdb = buf("rstd_b")
        m_scr = buf("scr32")
        m_rows = buf("rows32")
        m_stg0, m_stg1 = buf("stg0"), buf("stg1")
        m_statd = buf("bnc_stat")
        m_csumd = buf("bnc_csum")
        m_qnd = buf("bnc_qn")
        m_kn2d = buf("kn2_loc")
        m_sumsd = buf("bnc_sums")
        m_qpark = buf("q_park")
        m_kpark = buf("k_park")
        m_vpark = buf("v_park")
        m_bhl = buf("bhl_park")
        m_sh = buf("shared_kv")

        sb.bar_src = u32t[0:1, 2:3]
        sb.bar_dst = {"sq": u32t[0:1, 3:4], "aq": u32t[0:1, 3:4], "gq": u32t[0:1, 3:4]}


        MINCTX = stage.startswith("min")
        for c in range(NCH):
            sb.op("sq", lambda q, c=c: q.dma_start(
                out=xfm[:, c * T:(c + 1) * T], in_=x_in[c * 128:(c + 1) * 128, :]),
                writes=[xb[c]])
        if not MINCTX:
            sb.op("sq", lambda q: q.dma_start(out=ident[:, :], in_=ident_d), writes=[m_ident])
            for j, bd in enumerate((bq_d, bk_d, bo_d)):
                for c in range(NCH):
                    sb.op("sq", lambda q, j=j, c=c, bd=bd: q.dma_start(
                        out=biases[:, j * 16 + c:j * 16 + c + 1],
                        in_=bd[c * 128:(c + 1) * 128].rearrange("(p o) -> p o", o=1)),
                        writes=[m_bias])
            sb.op("sq", lambda q: q.dma_start(
                out=gcol[0:1, 32:48], in_=gb_d.rearrange("(o n) -> o n", o=1)),
                writes=[m_gcol])
        sb.op("dve", lambda q: q.memset(ones_c[:, :], 1.0), writes=[m_ones])
        sb.op("dve", lambda q: q.memset(ones_cb[:, :], 1.0), writes=[m_ones])
        sb.op("dve", lambda q: q.memset(ones_r[:, :], 1.0), writes=[m_ones])
        sb.op("dve", lambda q: q.memset(ones2d[:, :], 1.0), writes=[m_ones2])
        sb.op("dve", lambda q: q.memset(gcol[:, 15:16], EPS_LN), writes=[m_gcol])

        if not MINCTX:
            sb.op("pe", lambda q: q.matmul(PT[2][0:128, 0:16], ones_r[:, :],
                                           gcol[0:1, 32:48], start=True, stop=True),
                  reads=[m_ones, m_gcol], writes=[ptb[2]])
            sb.op("act", lambda q: q.copy(out=gb_b[:, :], in_=PT[2][:, 0:16]),
                  reads=[ptb[2]], writes=[m_gbb])

            ct_sb = es.enter_context(nc.sbuf_tensor("ct_sb", [128, NCH * 16], F32))
            m_ct = buf("ct_sb")
            for c in range(NCH):
                sb.op("sq", lambda q, c=c: q.dma_start(
                    out=ct_sb[:, c * 16:(c + 1) * 16], in_=ct_d[c]), writes=[m_ct])

            for c in range(NCH):
                sb.op("pe", lambda q, c=c: q.matmul(
                    PT[3][0:16, 0:1], ct_sb[:, c * 16:(c + 1) * 16], ones_c[:, :],
                    start=(c == 0), stop=(c == NCH - 1)),
                    reads=[m_ct, m_ones], writes=[ptb[3]], inc=(c == NCH - 1))
            sb.op("act", lambda q: q.copy(out=gcol[0:16, 0:1], in_=PT[3][0:16, 0:1]),
                  reads=[ptb[3]], writes=[m_gcol])
            sb.op("sq", lambda q: q.dma_start(
                out=bnc_csum.rearrange("(p o) -> p o", o=1), in_=gcol[0:16, 0:1]),
                reads=[m_gcol], writes=[m_csumd])
            sb.op("sq", lambda q: q.dma_start(
                out=gcol[0:1, 16:32], in_=bnc_csum.rearrange("(o n) -> o n", o=1)),
                reads=[m_csumd], writes=[m_gcol])
            sb.op("pe", lambda q: q.matmul(PT[2][0:128, 0:16], ones_r[:, :],
                                           gcol[0:1, 16:32], start=True, stop=True),
                  reads=[m_ones, m_gcol], writes=[ptb[2]])
            sb.op("act", lambda q: q.copy(out=csum_b[:, :], in_=PT[2][:, 0:16]),
                  reads=[ptb[2]], writes=[m_csum])

        def emit_ln(bounce_stats):
            for c in range(NCH):
                sb.op("act", lambda q, c=c: q.activation(
                    scr32[:, :], xfm[:, c * T:(c + 1) * T], AF.Square),
                    reads=[xb[c]], writes=[m_scr])
                for n in range(2):
                    sb.op("pe", lambda q, c=c, n=n: q.matmul(
                        PT[2][:, n * 512:(n + 1) * 512], ones2d[:, :],
                        xfm[:, c * T + n * 512:c * T + (n + 1) * 512],
                        start=(c == 0), stop=(c == NCH - 1)),
                        reads=[m_ones2, xb[c]], writes=[ptb[2]],
                        inc=(c == NCH - 1 and n == 1))
                for n in range(2):
                    sb.op("pe", lambda q, c=c, n=n: q.matmul(
                        PT[3][:, n * 512:(n + 1) * 512], ones2d[:, :],
                        scr32[:, n * 512:(n + 1) * 512],
                        start=(c == 0), stop=(c == NCH - 1)),
                        reads=[m_ones2, m_scr], writes=[ptb[3]],
                        inc=(n == 1))
            sb.op("act", lambda q: q.activation(mean_b[:, :], PT[2][:, :],
                                                AF.Copy, scale=1.0 / D),
                  reads=[ptb[2]], writes=[m_meanb])
            sb.op("act", lambda q: q.activation(scr32[:, :], PT[3][:, :],
                                                AF.Copy, scale=1.0 / D),
                  reads=[ptb[3]], writes=[m_scr])
            sb.op("act", lambda q: q.activation(rstd_b[:, :], mean_b[:, :],
                                                AF.Square),
                  reads=[m_meanb], writes=[m_rstdb])
            sb.op("dve", lambda q: q.tensor_sub(scr32[:, :], scr32[:, :],
                                                rstd_b[:, :]),
                  reads=[m_scr, m_rstdb], writes=[m_scr])
            sb.op("act", lambda q: q.activation(scr32[:, :], scr32[:, :],
                                                AF.Sqrt, bias=gcol[:, 15:16]),
                  reads=[m_scr, m_gcol], writes=[m_scr])
            sb.op("dve", lambda q: q.reciprocal(rstd_b[:, :], scr32[:, :]),
                  reads=[m_scr], writes=[m_rstdb])
            if bounce_stats:
                sb.op("sq", lambda q: q.dma_start(out=bnc_stat[0:1, :],
                                                  in_=mean_b[0:1, :]),
                      reads=[m_meanb], writes=[m_statd])
                sb.op("sq", lambda q: q.dma_start(out=bnc_stat[1:2, :],
                                                  in_=rstd_b[0:1, :]),
                      reads=[m_rstdb], writes=[m_statd])
            for c in range(NCH):
                sb.op("dve", lambda q, c=c: q.tensor_sub(
                    scr32[:, :], xfm[:, c * T:(c + 1) * T], mean_b[:, :]),
                    reads=[xb[c], m_meanb], writes=[m_scr])
                sb.op("dve", lambda q, c=c: q.tensor_mul(
                    xn[:, c * T:(c + 1) * T], scr32[:, :], rstd_b[:, :]),
                    reads=[m_scr, m_rstdb], writes=[xnb[c]])

        def emit_gating(gsc, m_gsc, gat, m_gat, maskT, m_maskT):
            for c in range(NCH):
                for n in range(2):
                    sb.op("pe", lambda q, c=c, n=n: q.matmul(
                        PT[0][0:16, n * 512:(n + 1) * 512],
                        ct_sb[:, c * 16:(c + 1) * 16],
                        xfm[:, c * T + n * 512:c * T + (n + 1) * 512],
                        start=(c == 0), stop=(c == NCH - 1)),
                        reads=[m_ct, xb[c]], writes=[ptb[0]],
                        inc=(c == NCH - 1 and n == 1))
            sb.op("act", lambda q: q.copy(out=gsc[0:16, 0:T], in_=PT[0][0:16, :]),
                  reads=[ptb[0]], writes=[m_gsc])
            sb.op("dve", lambda q: q.memset(maskT[:, :], 0.0), writes=[m_maskT])
            sb.op("dve", lambda q: q.memset(maskT[32:34, :], 0.05), writes=[m_maskT])
            for t in range(8):
                sb.op("pe", lambda q, t=t: q.transpose(
                    PT[1][0:128, 0:16], gsc[0:16, t * 128:(t + 1) * 128],
                    ident[0:16, 0:16]),
                    reads=[m_gsc, m_ident], writes=[ptb[1]])
                sb.op("act", lambda q, t=t: q.copy(out=gat[:, 0:16], in_=PT[1][:, 0:16]),
                      reads=[ptb[1]], writes=[m_gat])
                sb.op("sq", lambda q, t=t: q.dma_start(
                    out=gcol[:, 0:1],
                    in_=bnc_stat[0, t * 128:(t + 1) * 128].rearrange("(p o) -> p o", o=1)),
                    reads=[m_statd], writes=[m_gcol])
                sb.op("sq", lambda q, t=t: q.dma_start(
                    out=gcol[:, 1:2],
                    in_=bnc_stat[1, t * 128:(t + 1) * 128].rearrange("(p o) -> p o", o=1)),
                    reads=[m_statd], writes=[m_gcol])
                sb.op("dve", lambda q: q.tensor_scalar(
                    gat[:, 16:32], csum_b[:, :], gcol[:, 0:1], None, ALU.mult),
                    reads=[m_csum, m_gcol], writes=[m_gat])
                sb.op("dve", lambda q: q.tensor_sub(
                    gat[:, 16:32], gat[:, 0:16], gat[:, 16:32]),
                    reads=[m_gat], writes=[m_gat])
                sb.op("dve", lambda q: q.tensor_scalar(
                    gat[:, 16:32], gat[:, 16:32], gcol[:, 1:2], None, ALU.mult),
                    reads=[m_gat, m_gcol], writes=[m_gat])
                sb.op("act", lambda q: q.activation(
                    gat[:, 32:48], gat[:, 16:32], AF.Sigmoid),
                    reads=[m_gat], writes=[m_gat])
                sb.op("dve", lambda q: q.tensor_add(
                    gat[:, 48:64], gat[:, 32:48], gb_b[:, :]),
                    reads=[m_gat, m_gbb], writes=[m_gat])
                sb.op("dve", lambda q: q.reduce_max(
                    gcol[:, 2:3], gat[:, 48:64], axis=AX.X),
                    reads=[m_gat], writes=[m_gcol])
                sb.op("dve", lambda q: q.tensor_single_scalar(
                    gat[:, 64:80], gat[:, 48:64], gcol[:, 2:3], ALU.is_equal),
                    reads=[m_gat, m_gcol], writes=[m_gat])
                sb.op("dve", lambda q: q.tensor_mul(
                    gat[:, 96:112], gat[:, 32:48], gat[:, 64:80]),
                    reads=[m_gat], writes=[m_gat])
                sb.op("dve", lambda q: q.reduce_sum(
                    gcol[:, 3:4], gat[:, 96:112], axis=AX.X),
                    reads=[m_gat], writes=[m_gcol])
                sb.op("dve", lambda q: q.scalar_tensor_tensor(
                    gat[:, 80:96], gat[:, 64:80], -1e30, gat[:, 48:64],
                    ALU.mult, ALU.add),
                    reads=[m_gat], writes=[m_gat])
                sb.op("dve", lambda q: q.reduce_max(
                    gcol[:, 4:5], gat[:, 80:96], axis=AX.X),
                    reads=[m_gat], writes=[m_gcol])
                sb.op("dve", lambda q: q.tensor_single_scalar(
                    gat[:, 96:112], gat[:, 80:96], gcol[:, 4:5], ALU.is_equal),
                    reads=[m_gat, m_gcol], writes=[m_gat])
                sb.op("dve", lambda q: q.tensor_mul(
                    gat[:, 112:128], gat[:, 32:48], gat[:, 96:112]),
                    reads=[m_gat], writes=[m_gat])
                sb.op("dve", lambda q: q.reduce_sum(
                    gcol[:, 5:6], gat[:, 112:128], axis=AX.X),
                    reads=[m_gat], writes=[m_gcol])
                sb.op("dve", lambda q: q.tensor_add(
                    gcol[:, 6:7], gcol[:, 3:4], gcol[:, 5:6]),
                    reads=[m_gcol], writes=[m_gcol])
                sb.op("dve", lambda q: q.tensor_scalar_add(
                    gcol[:, 6:7], gcol[:, 6:7], EPS_GATE),
                    reads=[m_gcol], writes=[m_gcol])
                sb.op("dve", lambda q: q.reciprocal(
                    gcol[:, 7:8], gcol[:, 6:7]),
                    reads=[m_gcol], writes=[m_gcol])
                sb.op("dve", lambda q: q.tensor_scalar(
                    gcol[:, 8:9], gcol[:, 3:4], gcol[:, 7:8], 0.5, ALU.mult, ALU.mult),
                    reads=[m_gcol], writes=[m_gcol])
                sb.op("dve", lambda q: q.tensor_scalar(
                    gcol[:, 9:10], gcol[:, 5:6], gcol[:, 7:8], 0.5, ALU.mult, ALU.mult),
                    reads=[m_gcol], writes=[m_gcol])
                sb.op("dve", lambda q: q.tensor_scalar(
                    gat[:, 0:16], gat[:, 64:80], gcol[:, 8:9], None, ALU.mult),
                    reads=[m_gat, m_gcol], writes=[m_gat])
                sb.op("dve", lambda q: q.scalar_tensor_tensor(
                    gat[:, 16:32], gat[:, 96:112], gcol[:, 9:10], gat[:, 0:16],
                    ALU.mult, ALU.add),
                    reads=[m_gat, m_gcol], writes=[m_gat])
                sb.op("pe", lambda q, t=t: q.transpose(
                    PT[1][0:16, 0:128], gat[:, 16:32], ident[:, :]),
                    reads=[m_gat, m_ident], writes=[ptb[1]])
                sb.op("act", lambda q, t=t: q.copy(
                    out=maskT[0:16, t * 128:(t + 1) * 128], in_=PT[1][0:16, 0:128]),
                    reads=[ptb[1]], writes=[m_maskT])

        def emit_ffn(tag, sub="full"):
            sc = contextlib.ExitStack()
            with sc:
                he = sc.enter_context(nc.sbuf_tensor(f"he_{tag}", [128, 16 * T], BF16))
                selb_s = sc.enter_context(nc.sbuf_tensor(f"selb_{tag}", [64, NE * 128], BF16))
                gsc = sc.enter_context(nc.sbuf_tensor(f"gsc_{tag}", [128, T], F32))
                gat = sc.enter_context(nc.sbuf_tensor(f"gat_{tag}", [128, 128], F32))
                maskT = sc.enter_context(nc.sbuf_tensor(f"maskT_{tag}", [64, T], BF16))
                mkb0 = sc.enter_context(nc.sbuf_tensor(f"mkb0_{tag}", [128, T], BF16))
                mkb1 = sc.enter_context(nc.sbuf_tensor(f"mkb1_{tag}", [128, T], BF16))
                wda = sc.enter_context(nc.sbuf_tensor(f"wda_{tag}", [128, 1024], BF16))
                wdb = sc.enter_context(nc.sbuf_tensor(f"wdb_{tag}", [128, 1024], BF16))
                heb = [buf(f"he_{tag}{j}") for j in range(16)]
                m_selb = buf(f"selb_{tag}")
                m_gsc = buf(f"gsc_{tag}")
                m_gat = buf(f"gat_{tag}")
                m_maskT = buf(f"maskT_{tag}")
                mkbB = [buf(f"mkb0_{tag}"), buf(f"mkb1_{tag}")]
                wdB = [buf(f"wda_{tag}"), buf(f"wdb_{tag}")]
                mkb = [mkb0, mkb1]
                wdt = [wda, wdb]

                sb.op("gq", lambda q: q.dma_start(out=selb_s[:, :], in_=selb_d),
                      writes=[m_selb])
                emit_ln(bounce_stats=True)
                if sub == "ln":
                    sb.barrier()
                    return
                emit_gating(gsc, m_gsc, gat, m_gat, maskT, m_maskT)
                if sub == "gate":
                    sb.barrier()
                    return

                def load_gu(e, i):
                    slot = i % 2
                    wt_g = wga if slot == 0 else wgb
                    wt_u = wua if slot == 0 else wub
                    bg = wgab if slot == 0 else wgbb
                    bu = wuab if slot == 0 else wubb
                    sb.op("gq", lambda q, e=e, i=i, w=wt_g: q.dma_start(
                        out=w[:, :], in_=wg_d[e, i].rearrange("p a b -> p (a b)")),
                        writes=[bg])
                    sb.op("gq", lambda q, e=e, i=i, w=wt_u: q.dma_start(
                        out=w[:, :], in_=wu_d[e, i].rearrange("p a b -> p (a b)")),
                        writes=[bu])

                def mask_bcast(e, eslot):
                    for n in range(2):
                        sb.op("pe", lambda q, e=e, n=n: q.matmul(
                            PT[0][:, n * 512:(n + 1) * 512],
                            selb_s[:, e * 128:(e + 1) * 128],
                            maskT[:, n * 512:(n + 1) * 512], start=True, stop=True),
                            reads=[m_selb, m_maskT], writes=[ptb[0]], inc=(n == 1))
                    sb.op("act", lambda q, eslot=eslot: q.copy(
                        out=mkb[eslot][:, :], in_=PT[0][:, :]),
                        reads=[ptb[0]], writes=[mkbB[eslot]])

                for g in range(1 if sub == "g1" else 9):
                    exps = [g * 2, g * 2 + 1]
                    for je, e in enumerate(exps):
                        mask_bcast(e, je)
                        for i in range(NI):
                            load_gu(e, i)
                            slot = i % 2
                            wt_g = wga if slot == 0 else wgb
                            wt_u = wua if slot == 0 else wub
                            bg = wgab if slot == 0 else wgbb
                            bu = wuab if slot == 0 else wubb
                            ptg, ptu = PT[2 * slot], PT[2 * slot + 1]
                            pbg, pbu = ptb[2 * slot], ptb[2 * slot + 1]
                            for kc in range(NCH):
                                for n in range(2):
                                    sb.op("pe", lambda q, w=wt_g, kc=kc, n=n, p=ptg: q.matmul(
                                        p[:, n * 512:(n + 1) * 512],
                                        w[:, kc * 128:(kc + 1) * 128],
                                        xn[:, kc * T + n * 512:kc * T + (n + 1) * 512],
                                        start=(kc == 0), stop=(kc == NCH - 1)),
                                        reads=[bg] + ([xnb[kc]] if n == 0 else []),
                                        writes=[pbg],
                                        inc=(kc == NCH - 1 and n == 1))
                            for kc in range(NCH):
                                for n in range(2):
                                    sb.op("pe", lambda q, w=wt_u, kc=kc, n=n, p=ptu: q.matmul(
                                        p[:, n * 512:(n + 1) * 512],
                                        w[:, kc * 128:(kc + 1) * 128],
                                        xn[:, kc * T + n * 512:kc * T + (n + 1) * 512],
                                        start=(kc == 0), stop=(kc == NCH - 1)),
                                        reads=[bu] + ([xnb[kc]] if n == 0 else []),
                                        writes=[pbu],
                                        inc=(kc == NCH - 1 and n == 1))
                            st = stg0 if slot == 0 else stg1
                            m_st = m_stg0 if slot == 0 else m_stg1
                            sb.op("act", lambda q, p=ptg, st=st: q.activation(
                                st[:, :], p[:, :], AF.Silu),
                                reads=[pbg], writes=[m_st])
                            hslot = je * 8 + i
                            sb.op("dve", lambda q, p=ptu, st=st, h=hslot: q.tensor_mul(
                                he[:, h * T:(h + 1) * T], p[:, :], st[:, :]),
                                reads=[pbu, m_st], writes=[heb[hslot]])
                            sb.op("dve", lambda q, h=hslot, je=je: q.tensor_mul(
                                he[:, h * T:(h + 1) * T], he[:, h * T:(h + 1) * T],
                                mkb[je][:, :]),
                                reads=[heb[hslot], mkbB[je]], writes=[heb[hslot]])
                    for f in range(NCH):
                        for je, e in enumerate(exps):
                            w = wdt[je]
                            wb = wdB[je]
                            sb.op("gq", lambda q, e=e, f=f, w=w: q.dma_start(
                                out=w[:, :], in_=wd_d[e, f].rearrange("p a b -> p (a b)")),
                                writes=[wb])
                            for i in range(NI):
                                h = je * 8 + i
                                first = (je == 0 and i == 0)
                                last = (je == 1 and i == NI - 1)
                                pt = PT[2 + (f % 2)]
                                pb = ptb[2 + (f % 2)]
                                for n in range(2):
                                    sb.op("pe", lambda q, w=w, i=i, n=n, h=h, pt=pt,
                                          fi=first, la=last: q.matmul(
                                        pt[:, n * 512:(n + 1) * 512],
                                        w[:, i * 128:(i + 1) * 128],
                                        he[:, h * T + n * 512:h * T + (n + 1) * 512],
                                        start=fi, stop=la),
                                        reads=[wb] + ([heb[h]] if n == 0 else []),
                                        writes=[pb],
                                        inc=(last and n == 1))
                        pt = PT[2 + (f % 2)]
                        pb = ptb[2 + (f % 2)]
                        sb.op("dve", lambda q, f=f, pt=pt: q.tensor_add(
                            xfm[:, f * T:(f + 1) * T], pt[:, :],
                            xfm[:, f * T:(f + 1) * T]),
                            reads=[pb, xb[f]], writes=[xb[f]])
                sb.barrier()

        def emit_qkv_and_exchange():
            sc = contextlib.ExitStack()
            with sc:
                vstg0 = sc.enter_context(nc.sbuf_tensor("vstg0", [128, 2080], BF16))
                vstg1 = sc.enter_context(nc.sbuf_tensor("vstg1", [128, 2080], BF16))
                bvb = sc.enter_context(nc.sbuf_tensor("bvb", [128, D], BF16))
                q2t = sc.enter_context(nc.sbuf_tensor("q2t", [128, T], BF16))
                sel2 = sc.enter_context(nc.sbuf_tensor("sel2", [128, 2], BF16))
                bndq = sc.enter_context(nc.sbuf_tensor("bndq", [32, T], F32))
                bnds = sc.enter_context(nc.sbuf_tensor("bnds", [32, T], F32))
                bndb = sc.enter_context(nc.sbuf_tensor("bndb", [32, 2 * T], BF16))
                k2f = sc.enter_context(nc.sbuf_tensor("k2f", [32, S], F32))
                kmx = sc.enter_context(nc.sbuf_tensor("kmx", [32, 2], F32))
                vstgB = [buf("vstg0"), buf("vstg1")]
                vstg = [vstg0, vstg1]
                m_bvb = buf("bvb")
                m_q2 = buf("q2t")
                m_sel2 = buf("sel2")
                m_bndq = buf("bndq")
                m_bnds = buf("bnds")
                m_bndb = buf("bndb")
                m_k2f = buf("k2f")
                m_kmx = buf("kmx")

                emit_ln(bounce_stats=False)

                sb.op("dve", lambda q: q.memset(sel2[:, :], 0.0), writes=[m_sel2])
                sb.op("dve", lambda q: q.memset(sel2[0:64, 0:1], 1.0), writes=[m_sel2])
                sb.op("dve", lambda q: q.memset(sel2[64:128, 1:2], 1.0), writes=[m_sel2])

                sb.op("sq", lambda q: q.dma_start(
                    out=rows32[0:1, 0:D], in_=bv_d.rearrange("(o n) -> o n", o=1)),
                    writes=[m_rows])
                for j in range(4):
                    pt = PT[2] if j < 2 else PT[3]
                    pb = ptb[2] if j < 2 else ptb[3]
                    off = (j % 2) * 512
                    sb.op("pe", lambda q, j=j, pt=pt, off=off: q.matmul(
                        pt[:, off:off + 512], ones_r[:, :],
                        rows32[0:1, j * 512:(j + 1) * 512], start=True, stop=True),
                        reads=[m_ones, m_rows], writes=[pb], inc=(j % 2 == 1))
                for j in range(4):
                    pt = PT[2] if j < 2 else PT[3]
                    pb = ptb[2] if j < 2 else ptb[3]
                    off = (j % 2) * 512
                    sb.op("act", lambda q, j=j, pt=pt, off=off: q.copy(
                        out=bvb[:, j * 512:(j + 1) * 512], in_=pt[:, off:off + 512]),
                        reads=[pb], writes=[m_bvb])

                def proj_fm(w_d, bcol, parkd, m_park, normd, m_normd):
                    for fc in range(NCH):
                        slot = fc % 2
                        wt = wga if slot == 0 else wgb
                        wtb = wgab if slot == 0 else wgbb
                        sb.op("gq", lambda q, fc=fc, wt=wt, w_d=w_d: q.dma_start(
                            out=wt[:, :], in_=w_d[fc].rearrange("p a b -> p (a b)")),
                            writes=[wtb])
                        pt, pb = PT[slot], ptb[slot]
                        for kc in range(NCH):
                            for n in range(2):
                                sb.op("pe", lambda q, wt=wt, kc=kc, n=n, pt=pt: q.matmul(
                                    pt[:, n * 512:(n + 1) * 512],
                                    wt[:, kc * 128:(kc + 1) * 128],
                                    xn[:, kc * T + n * 512:kc * T + (n + 1) * 512],
                                    start=(kc == 0), stop=(kc == NCH - 1)),
                                    reads=[wtb] + ([xnb[kc]] if n == 0 else []),
                                    writes=[pb], inc=(kc == NCH - 1 and n == 1))
                        st = stg0 if slot == 0 else stg1
                        m_st = m_stg0 if slot == 0 else m_stg1
                        sb.op("act", lambda q, pt=pt, st=st, fc=fc, bcol=bcol: q.activation(
                            st[:, :], pt[:, :], AF.Identity,
                            bias=biases[:, bcol * 16 + fc:bcol * 16 + fc + 1]),
                            reads=[pb, m_bias], writes=[m_st])
                        sb.op("sq", lambda q, st=st, fc=fc, parkd=parkd: q.dma_start(
                            out=parkd[fc * 128:(fc + 1) * 128, :], in_=st[:, :]),
                            reads=[m_st], writes=[m_park])
                        sb.op("act", lambda q, st=st: q.activation(
                            q2t[:, :], st[:, :], AF.Square),
                            reads=[m_st], writes=[m_q2])
                        for n in range(2):
                            sb.op("pe", lambda q, n=n: q.matmul(
                                PT[2][0:2, n * 512:(n + 1) * 512], sel2[:, :],
                                q2t[:, n * 512:(n + 1) * 512],
                                start=True, stop=True),
                                reads=[m_q2, m_sel2], writes=[ptb[2]], inc=(n == 1))
                        sb.op("act", lambda q: q.copy(
                            out=scr32[0:2, 0:T], in_=PT[2][0:2, :]),
                            reads=[ptb[2]], writes=[m_scr])
                        for hh in range(2):
                            sb.op("sq", lambda q, fc=fc, hh=hh, normd=normd: q.dma_start(
                                out=normd[fc * 2 + hh:fc * 2 + hh + 1, :],
                                in_=scr32[hh:hh + 1, 0:T]),
                                reads=[m_scr], writes=[m_normd])

                proj_fm(wk_d, 1, k_park, m_kpark, kn2_loc, m_kn2d)

                sb.op("dve", lambda q: q.memset(
                    vstg0.ap().rearrange("p (h e) -> p h e", e=65)[:, :, 64:65], 1.0),
                    writes=[vstgB[0]])
                sb.op("dve", lambda q: q.memset(
                    vstg1.ap().rearrange("p (h e) -> p h e", e=65)[:, :, 64:65], 1.0),
                    writes=[vstgB[1]])
                for tp in range(4):
                    for kc in range(NCH):
                        slot = kc % 2
                        wt = wga if slot == 0 else wgb
                        wtb = wgab if slot == 0 else wgbb
                        sb.op("gq", lambda q, kc=kc, wt=wt: q.dma_start(
                            out=wt[:, :], in_=wv_d[kc * 128:(kc + 1) * 128, :]),
                            writes=[wtb])
                        for half in range(2):
                            tc = tp * 2 + half
                            pts = (PT[0], PT[1]) if half == 0 else (PT[2], PT[3])
                            pbs = (ptb[0], ptb[1]) if half == 0 else (ptb[2], ptb[3])
                            for j in range(4):
                                pt = pts[j // 2]
                                pb = pbs[j // 2]
                                off = (j % 2) * 512
                                sb.op("pe", lambda q, tc=tc, kc=kc, j=j, wt=wt,
                                      pt=pt, off=off: q.matmul(
                                    pt[:, off:off + 512],
                                    xn[:, kc * T + tc * 128:kc * T + (tc + 1) * 128],
                                    wt[:, j * 512:(j + 1) * 512],
                                    start=(kc == 0), stop=(kc == NCH - 1)),
                                    reads=[wtb] + ([xnb[kc]] if j == 0 else []),
                                    writes=[pb],
                                    inc=(kc == NCH - 1 and j == 3))
                    for half in range(2):
                        tc = tp * 2 + half
                        pts = (PT[0], PT[1]) if half == 0 else (PT[2], PT[3])
                        pbs = (ptb[0], ptb[1]) if half == 0 else (ptb[2], ptb[3])
                        vs = vstg[half]
                        vb = vstgB[half]
                        for j in range(4):
                            pt = pts[j // 2]
                            pb = pbs[j // 2]
                            off = (j % 2) * 512
                            sb.op("dve", lambda q, vs=vs, j=j, pt=pt, off=off: q.tensor_add(
                                vs.ap()[:, j * 520:(j + 1) * 520].rearrange(
                                    "p (h e) -> p h e", e=65)[:, :, 0:64],
                                pt[:, off:off + 512].rearrange("p (h e) -> p h e", e=64),
                                bvb[:, j * 512:(j + 1) * 512].rearrange(
                                    "p (h e) -> p h e", e=64)),
                                reads=[pb, m_bvb], writes=[vb])
                        sb.op("sq", lambda q, tc=tc, vs=vs: q.dma_start(
                            out=v_park[tc * 128:(tc + 1) * 128, :], in_=vs[:, :]),
                            reads=[vb], writes=[m_vpark])

                PAIRS = [[0, 1], [2, 3], [4, 5], [6, 7]]
                sb.op("gq", lambda q: q.dma_start(out=u32t[0:1, 3:4],
                                                  in_=u32t[0:1, 2:3]),
                      reads=[m_kpark, m_vpark, m_kn2d])

                def colls(q):
                    q.collective_compute(
                        "AllGather", ALU.bypass, replica_groups=PAIRS,
                        ins=[k_park.opt()], outs=[kg.opt()]).then_inc(cc_k)
                    q.collective_compute(
                        "AllGather", ALU.bypass, replica_groups=PAIRS,
                        ins=[v_park.opt()], outs=[vg.opt()]).then_inc(cc_v)
                    q.collective_compute(
                        "AllGather", ALU.bypass, replica_groups=PAIRS,
                        ins=[kn2_loc.opt()], outs=[ng.opt()]).then_inc(cc_n)
                sb.raw("gq", colls)

                proj_fm(wq_d, 0, q_park, m_qpark, bnc_qn, m_qnd)
                sb.op("sq", lambda q: q.dma_start(out=bndq[:, :], in_=bnc_qn),
                      reads=[m_qnd], writes=[m_bndq])
                sb.op("act", lambda q: q.activation(bnds[:, :], bndq[:, :], AF.Sqrt),
                      reads=[m_bndq], writes=[m_bnds])

                def collwait(q):
                    q.wait_ge(cc_k, 1)
                    q.wait_ge(cc_v, 1)
                    q.wait_ge(cc_n, 1)
                sb.raw("gq", collwait)
                sb.op("gq", lambda q: q.dma_start(out=u32t[0:1, 3:4],
                                                  in_=u32t[0:1, 2:3]),
                      writes=[m_sh])

                sb.op("sq", lambda q: q.dma_start(out=k2f[:, 0:T], in_=ng[0:NH, :]),
                      reads=[m_sh], writes=[m_k2f])
                sb.op("sq", lambda q: q.dma_start(out=k2f[:, T:2 * T], in_=ng[NH:2 * NH, :]),
                      reads=[m_sh], writes=[m_k2f])
                sb.op("dve", lambda q: q.reduce_max(kmx[:, 0:1], k2f[:, :], axis=AX.X),
                      reads=[m_k2f], writes=[m_kmx])
                sb.op("act", lambda q: q.activation(kmx[:, 1:2], kmx[:, 0:1], AF.Sqrt),
                      reads=[m_kmx], writes=[m_kmx])
                sb.op("dve", lambda q: q.tensor_scalar(
                    bndq[:, :], bnds[:, :], kmx[:, 1:2], -1.01, ALU.mult, ALU.mult),
                    reads=[m_bnds, m_kmx], writes=[m_bndq])
                sb.op("dve", lambda q: q.tensor_scalar_add(
                    bndq[:, :], bndq[:, :], -6.0),
                    reads=[m_bndq], writes=[m_bndq])
                sb.op("dve", lambda q: q.tensor_copy(bndb[:, 0:T], bndq[:, :]),
                      reads=[m_bndq], writes=[m_bndb])
                sb.op("dve", lambda q: q.tensor_sub(bnds[:, :], bndq[:, :],
                                                    bndb[:, 0:T]),
                      reads=[m_bndq, m_bndb], writes=[m_bnds])
                sb.op("dve", lambda q: q.tensor_copy(bndb[:, T:2 * T], bnds[:, :]),
                      reads=[m_bnds], writes=[m_bndb])
                sb.op("sq", lambda q: q.dma_start(out=bhl_park[0:32, :], in_=bndb[:, 0:T]),
                      reads=[m_bndb], writes=[m_bhl])
                sb.op("sq", lambda q: q.dma_start(out=bhl_park[32:64, :], in_=bndb[:, T:2 * T]),
                      reads=[m_bndb], writes=[m_bhl])
                sb.barrier()

        def emit_attention_core():
            sc = contextlib.ExitStack()
            with sc:
                atf = sc.enter_context(nc.sbuf_tensor("atf", [128, 16 * T], BF16))
                vq = sc.enter_context(nc.sbuf_tensor("vq", [128, 16 * 520], BF16))
                ka0 = sc.enter_context(nc.sbuf_tensor("ka0", [66, S], BF16))
                ka1 = sc.enter_context(nc.sbuf_tensor("ka1", [66, S], BF16))
                qa0 = sc.enter_context(nc.sbuf_tensor("qa0", [66, T], BF16))
                qa1 = sc.enter_context(nc.sbuf_tensor("qa1", [66, T], BF16))
                pb0 = sc.enter_context(nc.sbuf_tensor("pb0", [128, T], BF16))
                pb1 = sc.enter_context(nc.sbuf_tensor("pb1", [128, T], BF16))
                shm = sc.enter_context(nc.sbuf_tensor("shm", [64, 128], BF16))
                odd = sc.enter_context(nc.sbuf_tensor("odd", [64, T], BF16))
                atfb = [buf(f"atf{j}") for j in range(NCH)]
                m_vq = buf("vq")
                kaB = [buf("ka0"), buf("ka1")]
                qaB = [buf("qa0"), buf("qa1")]
                pbB = [buf("pb0"), buf("pb1")]
                m_shm = buf("shm")
                m_odd = buf("odd")
                ka = [ka0, ka1]
                qa = [qa0, qa1]
                pbt = [pb0, pb1]

                sb.op("dve", lambda q: q.memset(ka0[64:66, :], 1.0), writes=[kaB[0]])
                sb.op("dve", lambda q: q.memset(ka1[64:66, :], 1.0), writes=[kaB[1]])
                sb.op("gq", lambda q: q.dma_start(out=shm[:, :], in_=shmat_d),
                      writes=[m_shm])

                dnrow = [mean_b, rstd_b]
                m_dnr = [m_meanb, m_rstdb]
                m_sums = [buf("sums0"), buf("sums1")]

                def emit_bounce(h):
                    sp = h % 2
                    avp, avb = PT[2 + sp], ptb[2 + sp]
                    sb.op("act", lambda q, avp=avp: q.copy(
                        out=scr32[64:65, 0:T], in_=avp[64:65, :]),
                        reads=[avb], writes=[m_scr])
                    sb.op("sq", lambda q, sp=sp: q.dma_start(
                        out=bnc_sums[sp:sp + 1, :], in_=scr32[64:65, 0:T]),
                        reads=[m_scr], writes=[m_sums[sp]])
                    sb.op("sq", lambda q, sp=sp: q.dma_start(
                        out=dnrow[sp][0:1, 0:T],
                        in_=bnc_sums[sp:sp + 1, :]),
                        reads=[m_sums[sp]], writes=[m_dnr[sp]])
                    sb.op("dve", lambda q, sp=sp: q.reciprocal(
                        dnrow[sp][0:1, 0:T], dnrow[sp][0:1, 0:T]),
                        reads=[m_dnr[sp]], writes=[m_dnr[sp]])

                def emit_norm(h):
                    sp = h % 2
                    avp, avb = PT[2 + sp], ptb[2 + sp]
                    for n in range(2):
                        sb.op("pe", lambda q, n=n, sp=sp: q.matmul(
                            PT[0][0:64, n * 512:(n + 1) * 512],
                            ones_r[:, 0:64],
                            dnrow[sp][0:1, n * 512:(n + 1) * 512],
                            start=True, stop=True),
                            reads=[m_ones, m_dnr[sp]], writes=[ptb[0]],
                            inc=(n == 1))
                    sb.op("act", lambda q: q.copy(
                        out=scr32[0:64, :], in_=PT[0][0:64, :]),
                        reads=[ptb[0]], writes=[m_scr])
                    if sp == 0:
                        sb.op("dve", lambda q, h=h, avp=avp: q.tensor_mul(
                            atf[0:64, (h // 2) * T:(h // 2 + 1) * T],
                            avp[0:64, :], scr32[0:64, :]),
                            reads=[avb, m_scr], writes=[atfb[h // 2]])
                    else:
                        sb.op("dve", lambda q, avp=avp: q.tensor_mul(
                            odd[:, :], avp[0:64, :], scr32[0:64, :]),
                            reads=[avb, m_scr], writes=[m_odd])
                        for n in range(2):
                            sb.op("pe", lambda q, n=n: q.matmul(
                                PT[1][:, n * 512:(n + 1) * 512], shm[:, :],
                                odd[:, n * 512:(n + 1) * 512],
                                start=True, stop=True),
                                reads=[m_shm, m_odd], writes=[ptb[1]],
                                inc=(n == 1))
                        sb.op("act", lambda q, h=h: q.copy(
                            out=atf[64:128, (h // 2) * T:(h // 2 + 1) * T],
                            in_=PT[1][64:128, :]),
                            reads=[ptb[1]], writes=[atfb[h // 2]])

                for quarter in range(4):
                    for kc in range(NCH):
                        sb.op("aq", lambda q, quarter=quarter, kc=kc: q.dma_start(
                            out=vq[:, kc * 520:(kc + 1) * 520],
                            in_=vg[kc * 128:(kc + 1) * 128,
                                   quarter * 520:(quarter + 1) * 520]),
                            reads=[m_sh], writes=[m_vq])
                    for hh in range(8):
                        h = quarter * 8 + hh
                        hs = hh % 2
                        kat, qat = ka[hs], qa[hs]
                        kab, qab = kaB[hs], qaB[hs]
                        sb.op("aq", lambda q, h=h, kat=kat: q.dma_start(
                            out=kat[0:64, 0:T], in_=kg[h * 64:(h + 1) * 64, :]),
                            reads=[m_sh], writes=[kab])
                        sb.op("aq", lambda q, h=h, kat=kat: q.dma_start(
                            out=kat[0:64, T:2 * T],
                            in_=kg[D + h * 64:D + (h + 1) * 64, :]),
                            reads=[m_sh], writes=[kab])
                        sb.op("aq", lambda q, h=h, qat=qat: q.dma_start(
                            out=qat[0:64, :], in_=q_park[h * 64:(h + 1) * 64, :]),
                            reads=[m_qpark], writes=[qab])
                        sb.op("aq", lambda q, h=h, qat=qat: q.dma_start(
                            out=qat[64:65, :], in_=bhl_park[h:h + 1, :]),
                            reads=[m_bhl], writes=[qab])
                        sb.op("aq", lambda q, h=h, qat=qat: q.dma_start(
                            out=qat[65:66, :], in_=bhl_park[32 + h:33 + h, :]),
                            reads=[m_bhl], writes=[qab])
                        avp = PT[2 + hs]
                        avb = ptb[2 + hs]
                        for kc in range(NCH):
                            qkp = PT[kc % 2]
                            qkb = ptb[kc % 2]
                            for n in range(2):
                                sb.op("pe", lambda q, kat=kat, qat=qat, kc=kc, n=n,
                                      qkp=qkp: q.matmul(
                                    qkp[:, n * 512:(n + 1) * 512],
                                    kat[:, kc * 128:(kc + 1) * 128],
                                    qat[:, n * 512:(n + 1) * 512],
                                    start=True, stop=True),
                                    reads=[kab, qab], writes=[qkb], inc=(n == 1))
                            pbuf = pbt[kc % 2]
                            pbb = pbB[kc % 2]
                            sb.op("act", lambda q, qkp=qkp, pbuf=pbuf: q.activation(
                                pbuf[:, :], qkp[:, :], AF.Exp, scale=SCALE),
                                reads=[qkb], writes=[pbb])
                            for n in range(2):
                                sb.op("pe", lambda q, kc=kc, hh=hh, n=n, pbuf=pbuf,
                                      avp=avp: q.matmul(
                                    avp[0:65, n * 512:(n + 1) * 512],
                                    vq[:, kc * 520 + hh * 65:kc * 520 + (hh + 1) * 65],
                                    pbuf[:, n * 512:(n + 1) * 512],
                                    start=(kc == 0), stop=(kc == NCH - 1)),
                                    reads=[m_vq, pbb], writes=[avb],
                                    inc=(n == 1))
                        emit_bounce(h)
                        if h > 0:
                            emit_norm(h - 1)
                emit_norm(NH - 1)

                for fc in range(NCH):
                    slot = fc % 2
                    wt = wga if slot == 0 else wgb
                    wtb = wgab if slot == 0 else wgbb
                    sb.op("gq", lambda q, fc=fc, wt=wt: q.dma_start(
                        out=wt[:, :], in_=wo_d[fc].rearrange("p a b -> p (a b)")),
                        writes=[wtb])
                    pt, pb = PT[slot], ptb[slot]
                    for kc in range(NCH):
                        for n in range(2):
                            sb.op("pe", lambda q, wt=wt, kc=kc, n=n, pt=pt: q.matmul(
                                pt[:, n * 512:(n + 1) * 512],
                                wt[:, kc * 128:(kc + 1) * 128],
                                atf[:, kc * T + n * 512:kc * T + (n + 1) * 512],
                                start=(kc == 0), stop=(kc == NCH - 1)),
                                reads=[wtb] + ([atfb[kc]] if n == 0 else []),
                                writes=[pb], inc=(kc == NCH - 1 and n == 1))
                    sb.op("dve", lambda q, fc=fc, pt=pt: q.scalar_tensor_tensor(
                        xfm[:, fc * T:(fc + 1) * T], pt[:, :],
                        biases[:, 32 + fc:33 + fc], xfm[:, fc * T:(fc + 1) * T],
                        ALU.add, ALU.add),
                        reads=[pb, m_bias, xb[fc]], writes=[xb[fc]])
                sb.barrier()

        def dump_x(dst):
            for c in range(NCH):
                sb.op("sq", lambda q, c=c: q.dma_start(
                    out=dst[c * 128:(c + 1) * 128, :],
                    in_=xfm[:, c * T:(c + 1) * T]),
                    reads=[xb[c]])

        if stage.startswith("ffn1_"):
            emit_ffn("f1", sub=stage.split("_", 1)[1])
            dump_x(dbg1_d)
        elif stage in ("ffn1", "qkv", "attn", "full"):
            emit_ffn("f1")
            dump_x(dbg1_d)
        if stage in ("qkv", "attn", "full"):
            emit_qkv_and_exchange()
        if stage in ("attn", "full"):
            emit_attention_core()
            dump_x(dbg2_d)
        if stage == "full":
            emit_ffn("f2")

        m_out = buf("out")
        for c in range(NCH):
            sb.op("sq", lambda q, c=c: q.dma_start(
                out=out_d[c * 128:(c + 1) * 128, :], in_=xfm[:, c * T:(c + 1) * T]),
                reads=[xb[c]], writes=[m_out])
        fin = sb.cnt["sq"]
        sb.raw("sq", lambda q: q.wait_ge(sb.sems["sq"], fin))

        blk = es.enter_context(nc.Block())
        for sect in ("tensor", "scalar", "vector", "gpsimd", "sync"):
            fns = sb.prog[sect]

            def body(q, _fns=fns):
                for f in _fns:
                    f(q)
            getattr(blk, sect)(body)

    return nc



_CACHE = {}


def _relayout(inputs):
    f = np.float32
    wq = inputs["L0_attn_Wq"].astype(f, copy=False)
    wk = inputs["L0_attn_Wk"].astype(f, copy=False)
    wo = inputs["L0_attn_Wo"].astype(f, copy=False)

    def wpl(w):
        return np.ascontiguousarray(
            w.reshape(16, 128, 16, 128).transpose(2, 1, 0, 3))

    wg = inputs["L0_ffn_wg"]
    wu = inputs["L0_ffn_wu"]
    wd = inputs["L0_ffn_wd"]
    wg_s = inputs["L0_ffn_wg_s"]
    wu_s = inputs["L0_ffn_wu_s"]
    wd_s = inputs["L0_ffn_wd_s"]

    def gu_pl(m, ms):
        a = m.reshape(16, 16, 128, 8, 128).transpose(0, 3, 2, 1, 4)
        b = ms.reshape(16, 128, 2, 8, 128).transpose(2, 3, 1, 0, 4)
        return np.ascontiguousarray(np.concatenate([a, b], axis=0))

    def d_pl(m, ms):
        a = m.reshape(16, 8, 128, 16, 128).transpose(0, 3, 2, 1, 4)
        b = ms.reshape(2, 8, 128, 16, 128).transpose(0, 3, 2, 1, 4)
        return np.ascontiguousarray(np.concatenate([a, b], axis=0))

    selb = np.zeros((64, NE * 128), np.float32)
    for e in range(NE):
        r = e if e < 16 else 16 + e
        selb[r, e * 128:(e + 1) * 128] = 1.0
    import ml_dtypes
    selb = selb.astype(ml_dtypes.bfloat16)
    shmat = np.zeros((64, 128), np.float32)
    for p in range(64):
        shmat[p, 64 + p] = 1.0
    shmat = shmat.astype(ml_dtypes.bfloat16)

    ct = np.ascontiguousarray(
        inputs["L0_ffn_centroids"].astype(f, copy=False).T.reshape(16, 128, 16))

    return {
        "wq": wpl(wq), "wk": wpl(wk), "wo": wpl(wo),
        "wv": np.ascontiguousarray(inputs["L0_attn_Wv"].astype(f, copy=False)),
        "wg": gu_pl(wg, wg_s), "wu": gu_pl(wu, wu_s), "wd": d_pl(wd, wd_s),
        "ct": ct,
        "bq": np.ascontiguousarray(inputs["L0_attn_bq"].astype(f, copy=False)),
        "bk": np.ascontiguousarray(inputs["L0_attn_bk"].astype(f, copy=False)),
        "bv": np.ascontiguousarray(inputs["L0_attn_bv"].astype(f, copy=False)),
        "bo": np.ascontiguousarray(inputs["L0_attn_bo"].astype(f, copy=False)),
        "gb": np.ascontiguousarray(inputs["L0_ffn_gate_bias"].astype(f, copy=False)),
        "ident": np.eye(128, dtype=np.float32),
        "selb": selb, "shmat": shmat,
    }


def kernel(**inputs):
    x = np.asarray(inputs["L0_attn_x"], dtype=np.float32)
    shared = _relayout(inputs)

    nc = build_program()
    in_maps = []
    for c in range(8):
        b, h = c // 2, c % 2
        m = dict(shared)
        m["x_in"] = np.ascontiguousarray(x[b, h * T:(h + 1) * T, :].T)
        in_maps.append(m)
    res = run_bass_kernel_spmd(nc, in_maps, core_ids=list(range(8)))
    out = np.empty((4, 2048, 2048), np.float32)
    for c in range(8):
        b, h = c // 2, c % 2
        out[b, h * T:(h + 1) * T, :] = res.results[c]["out"].T
    return out


# version 05
# speedup vs baseline: 1.1330x; 1.1148x over previous
import contextlib
import numpy as np
import concourse.bass as bass
import concourse.mybir as mybir
from concourse.bass_utils import run_bass_kernel_spmd

F32 = mybir.dt.float32
BF16 = mybir.dt.bfloat16
U32 = mybir.dt.uint32
AF = mybir.ActivationFunctionType
ALU = mybir.AluOpType
AX = mybir.AxisListType

T = 1024
D = 2048
NCH = 16
NI = 8
NE = 18
NH = 32
S = 2048
SCALE = float(1.0 / np.sqrt(np.float32(32.0)))
EPS_LN = 1e-5
EPS_GATE = 1e-9



class Buf:
    __slots__ = ("name", "w", "readers")

    def __init__(self, name):
        self.name = name
        self.w = None
        self.readers = []


class SB:
    ENGS = ("pe", "act", "dve", "pool", "sq", "aq", "gq")
    DMA = frozenset(("sq", "aq", "gq"))
    SECT = {"pe": "tensor", "act": "scalar", "dve": "vector",
            "pool": "gpsimd", "sq": "sync", "aq": "scalar", "gq": "gpsimd"}

    def __init__(self):
        self.prog = {"tensor": [], "scalar": [], "vector": [], "gpsimd": [], "sync": []}
        self.cnt = {e: 0 for e in self.ENGS}
        self.hw = {e: {d: 0 for d in self.ENGS} for e in self.ENGS}
        self.pend = {e: [] for e in self.ENGS}
        self.sems = {}

    def raw(self, eng, fn):
        self.prog[self.SECT[eng]].append(fn)

    def _wait(self, eng, dep, val):
        if val <= self.hw[eng][dep]:
            return
        self.hw[eng][dep] = val
        sems = self.sems

        def w(q, _d=dep, _v=val):
            q.wait_ge(sems[_d], _v)
        self.raw(eng, w)

    def op(self, eng, fn, reads=(), writes=(), inc=True):
        deps = {}
        for b in reads:
            if b.w is not None:
                e, c = b.w
                deps[e] = max(deps.get(e, 0), c)
        for b in writes:
            if b.w is not None:
                e, c = b.w
                deps[e] = max(deps.get(e, 0), c)
            for e, c in b.readers:
                deps[e] = max(deps.get(e, 0), c)
        for e, c in deps.items():
            if e == "pe" and eng == "pe":
                continue
            self._wait(eng, e, c)
        step = 16 if eng in self.DMA else 1
        if inc:
            self.cnt[eng] += step
            newc = self.cnt[eng]
            sems = self.sems

            def run(q, _fn=fn, _e=eng):
                _fn(q).then_inc(sems[_e], 16 if _e in self.DMA else 1)
            self.raw(eng, run)
            for b, is_w in self.pend[eng]:
                if is_w:
                    b.w = (eng, newc)
                    b.readers = []
                else:
                    b.readers.append((eng, newc))
            self.pend[eng].clear()
            for b in writes:
                b.w = (eng, newc)
                b.readers = []
            for b in reads:
                b.readers.append((eng, newc))
        else:
            self.raw(eng, lambda q, _fn=fn: _fn(q))
            for b in writes:
                self.pend[eng].append((b, True))
            for b in reads:
                self.pend[eng].append((b, False))

    def barrier(self):
        for e in self.ENGS:
            assert not self.pend[e], f"pending unstamped ops on {e} at barrier"



def build_program(stage="full"):
    nc = bass.Bass("TRN2", target_bir_lowering=False)
    sb = SB()

    def ein(name, shape, dt=F32):
        return nc.dram_tensor(name, shape, dt, kind="ExternalInput").ap()

    x_in = ein("x_in", [D, T])
    wq_d = ein("wq", [NCH, 128, NCH, 128])
    wk_d = ein("wk", [NCH, 128, NCH, 128])
    wo_d = ein("wo", [NCH, 128, NCH, 128])
    wv_d = ein("wv", [D, D])
    wg_d = ein("wg", [NE, NI, 128, NCH, 128])
    wu_d = ein("wu", [NE, NI, 128, NCH, 128])
    wd_d = ein("wd", [NE, NCH, 128, NI, 128])
    ct_d = ein("ct", [NCH, 128, 16])
    bq_d = ein("bq", [D])
    bk_d = ein("bk", [D])
    bv_d = ein("bv", [D])
    bo_d = ein("bo", [D])
    gb_d = ein("gb", [16])
    ident_d = ein("ident", [128, 128])
    selb_d = ein("selb", [64, NE * 128], BF16)
    shmat_d = ein("shmat", [64, 128], BF16)
    out_d = nc.dram_tensor("out", [D, T], F32, kind="ExternalOutput").ap()
    dbg1_d = nc.dram_tensor("dbg1", [D, T], F32, kind="ExternalOutput").ap()
    dbg2_d = nc.dram_tensor("dbg2", [D, T], F32, kind="ExternalOutput").ap()

    def dint(name, shape, dt):
        return nc.dram_tensor(name, shape, dt, kind="Internal").ap()

    q_park = dint("q_park", [D, T], BF16)
    k_park = dint("k_park", [D, T], BF16)
    v_park = dint("v_park", [T, NH * 65], BF16)
    bhl_park = dint("bhl_park", [64, T], BF16)
    bnc_stat = dint("bnc_stat", [2, T], F32)
    bnc_csum = dint("bnc_csum", [16], F32)
    bnc_qn = dint("bnc_qn", [NH, T], F32)
    kn2_loc = dint("kn2_loc", [NH, T], F32)
    bnc_sums = dint("bnc_sums", [2, T], F32)

    kg = nc.dram_tensor("kg", [2 * D, T], BF16, kind="Internal").ap()
    vg = nc.dram_tensor("vg", [S, NH * 65], BF16, kind="Internal").ap()
    ng = nc.dram_tensor("ng", [2 * NH, T], F32, kind="Internal").ap()

    es = contextlib.ExitStack()
    B = {}

    def buf(name):
        if name not in B:
            B[name] = Buf(name)
        return B[name]

    with es:
        for e in SB.ENGS:
            sb.sems[e] = es.enter_context(nc.semaphore("sem_" + e))
        cc_k = es.enter_context(nc.semaphore("cc_k"))
        cc_v = es.enter_context(nc.semaphore("cc_v"))
        cc_n = es.enter_context(nc.semaphore("cc_n"))

        xfm = es.enter_context(nc.sbuf_tensor("xfm", [128, NCH * T], F32))
        xn = es.enter_context(nc.sbuf_tensor("xn", [128, NCH * T], BF16))
        wga = es.enter_context(nc.sbuf_tensor("wga", [128, 2048], BF16))
        wgb = es.enter_context(nc.sbuf_tensor("wgb", [128, 2048], BF16))
        wua = es.enter_context(nc.sbuf_tensor("wua", [128, 2048], BF16))
        wub = es.enter_context(nc.sbuf_tensor("wub", [128, 2048], BF16))
        mean_b = es.enter_context(nc.sbuf_tensor("mean_b", [128, T], F32))
        rstd_b = es.enter_context(nc.sbuf_tensor("rstd_b", [128, T], F32))
        scr32 = es.enter_context(nc.sbuf_tensor("scr32", [128, T], F32))
        rows32 = es.enter_context(nc.sbuf_tensor("rows32", [128, 2048], F32))
        stg0 = es.enter_context(nc.sbuf_tensor("stg0", [128, T], BF16))
        stg1 = es.enter_context(nc.sbuf_tensor("stg1", [128, T], BF16))
        ident = es.enter_context(nc.sbuf_tensor("ident_sb", [128, 128], F32))
        biases = es.enter_context(nc.sbuf_tensor("biases", [128, 48], F32))
        gcol = es.enter_context(nc.sbuf_tensor("gcol", [128, 48], F32))
        csum_b = es.enter_context(nc.sbuf_tensor("csum_b", [128, 16], F32))
        gb_b = es.enter_context(nc.sbuf_tensor("gb_b", [128, 16], F32))
        ones_c = es.enter_context(nc.sbuf_tensor("ones_c", [128, 1], F32))
        ones_cb = es.enter_context(nc.sbuf_tensor("ones_cb", [128, 1], BF16))
        ones_r = es.enter_context(nc.sbuf_tensor("ones_r", [1, 128], F32))
        ones2d = es.enter_context(nc.sbuf_tensor("ones2d", [128, 128], F32))
        u32t = es.enter_context(nc.sbuf_tensor("u32t", [1, 4], U32))

        PT = [es.enter_context(nc.psum_tensor(f"pt{i}", [128, 1024], F32)) for i in range(4)]
        ptb = [buf(f"pt{i}") for i in range(4)]

        xb = [buf(f"x{c}") for c in range(NCH)]
        xnb = [buf(f"xn{c}") for c in range(NCH)]
        wgab, wgbb = buf("wga"), buf("wgb")
        wuab, wubb = buf("wua"), buf("wub")
        m_bias = buf("biases")
        m_ident = buf("ident")
        m_gcol = buf("gcol")
        m_csum = buf("csum_b")
        m_gbb = buf("gb_b")
        m_ones = buf("ones")
        m_ones2 = buf("ones2d")
        m_u32 = buf("u32t")
        m_meanb = buf("mean_b")
        m_rstdb = buf("rstd_b")
        m_scr = buf("scr32")
        m_rows = buf("rows32")
        m_stg0, m_stg1 = buf("stg0"), buf("stg1")
        m_statd = buf("bnc_stat")
        m_csumd = buf("bnc_csum")
        m_qnd = buf("bnc_qn")
        m_kn2d = buf("kn2_loc")
        m_sumsd = buf("bnc_sums")
        m_qpark = buf("q_park")
        m_kpark = buf("k_park")
        m_vpark = buf("v_park")
        m_bhl = buf("bhl_park")
        m_sh = buf("shared_kv")

        sb.bar_src = u32t[0:1, 2:3]
        sb.bar_dst = {"sq": u32t[0:1, 3:4], "aq": u32t[0:1, 3:4], "gq": u32t[0:1, 3:4]}


        MINCTX = stage.startswith("min")
        for c in range(NCH):
            sb.op("sq", lambda q, c=c: q.dma_start(
                out=xfm[:, c * T:(c + 1) * T], in_=x_in[c * 128:(c + 1) * 128, :]),
                writes=[xb[c]])
        if not MINCTX:
            sb.op("sq", lambda q: q.dma_start(out=ident[:, :], in_=ident_d), writes=[m_ident])
            for j, bd in enumerate((bq_d, bk_d, bo_d)):
                for c in range(NCH):
                    sb.op("sq", lambda q, j=j, c=c, bd=bd: q.dma_start(
                        out=biases[:, j * 16 + c:j * 16 + c + 1],
                        in_=bd[c * 128:(c + 1) * 128].rearrange("(p o) -> p o", o=1)),
                        writes=[m_bias])
            sb.op("sq", lambda q: q.dma_start(
                out=gcol[0:1, 32:48], in_=gb_d.rearrange("(o n) -> o n", o=1)),
                writes=[m_gcol])
        sb.op("dve", lambda q: q.memset(ones_c[:, :], 1.0), writes=[m_ones])
        sb.op("dve", lambda q: q.memset(ones_cb[:, :], 1.0), writes=[m_ones])
        sb.op("dve", lambda q: q.memset(ones_r[:, :], 1.0), writes=[m_ones])
        sb.op("dve", lambda q: q.memset(ones2d[:, :], 1.0), writes=[m_ones2])
        sb.op("dve", lambda q: q.memset(gcol[:, 15:16], EPS_LN), writes=[m_gcol])

        if not MINCTX:
            sb.op("pe", lambda q: q.matmul(PT[2][0:128, 0:16], ones_r[:, :],
                                           gcol[0:1, 32:48], start=True, stop=True),
                  reads=[m_ones, m_gcol], writes=[ptb[2]])
            sb.op("act", lambda q: q.copy(out=gb_b[:, :], in_=PT[2][:, 0:16]),
                  reads=[ptb[2]], writes=[m_gbb])

            ct_sb = es.enter_context(nc.sbuf_tensor("ct_sb", [128, NCH * 16], F32))
            m_ct = buf("ct_sb")
            for c in range(NCH):
                sb.op("sq", lambda q, c=c: q.dma_start(
                    out=ct_sb[:, c * 16:(c + 1) * 16], in_=ct_d[c]), writes=[m_ct])

            for c in range(NCH):
                sb.op("pe", lambda q, c=c: q.matmul(
                    PT[3][0:16, 0:1], ct_sb[:, c * 16:(c + 1) * 16], ones_c[:, :],
                    start=(c == 0), stop=(c == NCH - 1)),
                    reads=[m_ct, m_ones], writes=[ptb[3]], inc=(c == NCH - 1))
            sb.op("act", lambda q: q.copy(out=gcol[0:16, 0:1], in_=PT[3][0:16, 0:1]),
                  reads=[ptb[3]], writes=[m_gcol])
            sb.op("sq", lambda q: q.dma_start(
                out=bnc_csum.rearrange("(p o) -> p o", o=1), in_=gcol[0:16, 0:1]),
                reads=[m_gcol], writes=[m_csumd])
            sb.op("sq", lambda q: q.dma_start(
                out=gcol[0:1, 16:32], in_=bnc_csum.rearrange("(o n) -> o n", o=1)),
                reads=[m_csumd], writes=[m_gcol])
            sb.op("pe", lambda q: q.matmul(PT[2][0:128, 0:16], ones_r[:, :],
                                           gcol[0:1, 16:32], start=True, stop=True),
                  reads=[m_ones, m_gcol], writes=[ptb[2]])
            sb.op("act", lambda q: q.copy(out=csum_b[:, :], in_=PT[2][:, 0:16]),
                  reads=[ptb[2]], writes=[m_csum])

        def emit_ln(bounce_stats):
            for c in range(NCH):
                sb.op("act", lambda q, c=c: q.activation(
                    scr32[:, :], xfm[:, c * T:(c + 1) * T], AF.Square),
                    reads=[xb[c]], writes=[m_scr])
                for n in range(2):
                    sb.op("pe", lambda q, c=c, n=n: q.matmul(
                        PT[2][:, n * 512:(n + 1) * 512], ones2d[:, :],
                        xfm[:, c * T + n * 512:c * T + (n + 1) * 512],
                        start=(c == 0), stop=(c == NCH - 1)),
                        reads=[m_ones2, xb[c]], writes=[ptb[2]],
                        inc=(c == NCH - 1 and n == 1))
                for n in range(2):
                    sb.op("pe", lambda q, c=c, n=n: q.matmul(
                        PT[3][:, n * 512:(n + 1) * 512], ones2d[:, :],
                        scr32[:, n * 512:(n + 1) * 512],
                        start=(c == 0), stop=(c == NCH - 1)),
                        reads=[m_ones2, m_scr], writes=[ptb[3]],
                        inc=(n == 1))
            sb.op("act", lambda q: q.activation(mean_b[:, :], PT[2][:, :],
                                                AF.Copy, scale=1.0 / D),
                  reads=[ptb[2]], writes=[m_meanb])
            sb.op("act", lambda q: q.activation(scr32[:, :], PT[3][:, :],
                                                AF.Copy, scale=1.0 / D),
                  reads=[ptb[3]], writes=[m_scr])
            sb.op("act", lambda q: q.activation(rstd_b[:, :], mean_b[:, :],
                                                AF.Square),
                  reads=[m_meanb], writes=[m_rstdb])
            sb.op("dve", lambda q: q.tensor_sub(scr32[:, :], scr32[:, :],
                                                rstd_b[:, :]),
                  reads=[m_scr, m_rstdb], writes=[m_scr])
            sb.op("act", lambda q: q.activation(scr32[:, :], scr32[:, :],
                                                AF.Sqrt, bias=gcol[:, 15:16]),
                  reads=[m_scr, m_gcol], writes=[m_scr])
            sb.op("dve", lambda q: q.reciprocal(rstd_b[:, :], scr32[:, :]),
                  reads=[m_scr], writes=[m_rstdb])
            if bounce_stats:
                sb.op("sq", lambda q: q.dma_start(out=bnc_stat[0:1, :],
                                                  in_=mean_b[0:1, :]),
                      reads=[m_meanb], writes=[m_statd])
                sb.op("sq", lambda q: q.dma_start(out=bnc_stat[1:2, :],
                                                  in_=rstd_b[0:1, :]),
                      reads=[m_rstdb], writes=[m_statd])
            for c in range(NCH):
                sb.op("dve", lambda q, c=c: q.tensor_sub(
                    scr32[:, :], xfm[:, c * T:(c + 1) * T], mean_b[:, :]),
                    reads=[xb[c], m_meanb], writes=[m_scr])
                sb.op("dve", lambda q, c=c: q.tensor_mul(
                    xn[:, c * T:(c + 1) * T], scr32[:, :], rstd_b[:, :]),
                    reads=[m_scr, m_rstdb], writes=[xnb[c]])

        def emit_gating(gsc, m_gsc, gat, m_gat, maskT, m_maskT):
            for c in range(NCH):
                for n in range(2):
                    sb.op("pe", lambda q, c=c, n=n: q.matmul(
                        PT[0][0:16, n * 512:(n + 1) * 512],
                        ct_sb[:, c * 16:(c + 1) * 16],
                        xfm[:, c * T + n * 512:c * T + (n + 1) * 512],
                        start=(c == 0), stop=(c == NCH - 1)),
                        reads=[m_ct, xb[c]], writes=[ptb[0]],
                        inc=(c == NCH - 1 and n == 1))
            sb.op("act", lambda q: q.copy(out=gsc[0:16, 0:T], in_=PT[0][0:16, :]),
                  reads=[ptb[0]], writes=[m_gsc])
            sb.op("dve", lambda q: q.memset(maskT[:, :], 0.0), writes=[m_maskT])
            sb.op("dve", lambda q: q.memset(maskT[32:34, :], 0.05), writes=[m_maskT])
            for t in range(8):
                sb.op("pe", lambda q, t=t: q.transpose(
                    PT[1][0:128, 0:16], gsc[0:16, t * 128:(t + 1) * 128],
                    ident[0:16, 0:16]),
                    reads=[m_gsc, m_ident], writes=[ptb[1]])
                sb.op("act", lambda q, t=t: q.copy(out=gat[:, 0:16], in_=PT[1][:, 0:16]),
                      reads=[ptb[1]], writes=[m_gat])
                sb.op("sq", lambda q, t=t: q.dma_start(
                    out=gcol[:, 0:1],
                    in_=bnc_stat[0, t * 128:(t + 1) * 128].rearrange("(p o) -> p o", o=1)),
                    reads=[m_statd], writes=[m_gcol])
                sb.op("sq", lambda q, t=t: q.dma_start(
                    out=gcol[:, 1:2],
                    in_=bnc_stat[1, t * 128:(t + 1) * 128].rearrange("(p o) -> p o", o=1)),
                    reads=[m_statd], writes=[m_gcol])
                sb.op("dve", lambda q: q.tensor_scalar(
                    gat[:, 16:32], csum_b[:, :], gcol[:, 0:1], None, ALU.mult),
                    reads=[m_csum, m_gcol], writes=[m_gat])
                sb.op("dve", lambda q: q.tensor_sub(
                    gat[:, 16:32], gat[:, 0:16], gat[:, 16:32]),
                    reads=[m_gat], writes=[m_gat])
                sb.op("dve", lambda q: q.tensor_scalar(
                    gat[:, 16:32], gat[:, 16:32], gcol[:, 1:2], None, ALU.mult),
                    reads=[m_gat, m_gcol], writes=[m_gat])
                sb.op("act", lambda q: q.activation(
                    gat[:, 32:48], gat[:, 16:32], AF.Sigmoid),
                    reads=[m_gat], writes=[m_gat])
                sb.op("dve", lambda q: q.tensor_add(
                    gat[:, 48:64], gat[:, 32:48], gb_b[:, :]),
                    reads=[m_gat, m_gbb], writes=[m_gat])
                sb.op("dve", lambda q: q.reduce_max(
                    gcol[:, 2:3], gat[:, 48:64], axis=AX.X),
                    reads=[m_gat], writes=[m_gcol])
                sb.op("dve", lambda q: q.tensor_single_scalar(
                    gat[:, 64:80], gat[:, 48:64], gcol[:, 2:3], ALU.is_equal),
                    reads=[m_gat, m_gcol], writes=[m_gat])
                sb.op("dve", lambda q: q.tensor_mul(
                    gat[:, 96:112], gat[:, 32:48], gat[:, 64:80]),
                    reads=[m_gat], writes=[m_gat])
                sb.op("dve", lambda q: q.reduce_sum(
                    gcol[:, 3:4], gat[:, 96:112], axis=AX.X),
                    reads=[m_gat], writes=[m_gcol])
                sb.op("dve", lambda q: q.scalar_tensor_tensor(
                    gat[:, 80:96], gat[:, 64:80], -1e30, gat[:, 48:64],
                    ALU.mult, ALU.add),
                    reads=[m_gat], writes=[m_gat])
                sb.op("dve", lambda q: q.reduce_max(
                    gcol[:, 4:5], gat[:, 80:96], axis=AX.X),
                    reads=[m_gat], writes=[m_gcol])
                sb.op("dve", lambda q: q.tensor_single_scalar(
                    gat[:, 96:112], gat[:, 80:96], gcol[:, 4:5], ALU.is_equal),
                    reads=[m_gat, m_gcol], writes=[m_gat])
                sb.op("dve", lambda q: q.tensor_mul(
                    gat[:, 112:128], gat[:, 32:48], gat[:, 96:112]),
                    reads=[m_gat], writes=[m_gat])
                sb.op("dve", lambda q: q.reduce_sum(
                    gcol[:, 5:6], gat[:, 112:128], axis=AX.X),
                    reads=[m_gat], writes=[m_gcol])
                sb.op("dve", lambda q: q.tensor_add(
                    gcol[:, 6:7], gcol[:, 3:4], gcol[:, 5:6]),
                    reads=[m_gcol], writes=[m_gcol])
                sb.op("dve", lambda q: q.tensor_scalar_add(
                    gcol[:, 6:7], gcol[:, 6:7], EPS_GATE),
                    reads=[m_gcol], writes=[m_gcol])
                sb.op("dve", lambda q: q.reciprocal(
                    gcol[:, 7:8], gcol[:, 6:7]),
                    reads=[m_gcol], writes=[m_gcol])
                sb.op("dve", lambda q: q.tensor_scalar(
                    gcol[:, 8:9], gcol[:, 3:4], gcol[:, 7:8], 0.5, ALU.mult, ALU.mult),
                    reads=[m_gcol], writes=[m_gcol])
                sb.op("dve", lambda q: q.tensor_scalar(
                    gcol[:, 9:10], gcol[:, 5:6], gcol[:, 7:8], 0.5, ALU.mult, ALU.mult),
                    reads=[m_gcol], writes=[m_gcol])
                sb.op("dve", lambda q: q.tensor_scalar(
                    gat[:, 0:16], gat[:, 64:80], gcol[:, 8:9], None, ALU.mult),
                    reads=[m_gat, m_gcol], writes=[m_gat])
                sb.op("dve", lambda q: q.scalar_tensor_tensor(
                    gat[:, 16:32], gat[:, 96:112], gcol[:, 9:10], gat[:, 0:16],
                    ALU.mult, ALU.add),
                    reads=[m_gat, m_gcol], writes=[m_gat])
                sb.op("pe", lambda q, t=t: q.transpose(
                    PT[1][0:16, 0:128], gat[:, 16:32], ident[:, :]),
                    reads=[m_gat, m_ident], writes=[ptb[1]])
                sb.op("act", lambda q, t=t: q.copy(
                    out=maskT[0:16, t * 128:(t + 1) * 128], in_=PT[1][0:16, 0:128]),
                    reads=[ptb[1]], writes=[m_maskT])

        def emit_ffn(tag, sub="full"):
            sc = contextlib.ExitStack()
            with sc:
                he = sc.enter_context(nc.sbuf_tensor(f"he_{tag}", [128, 16 * T], BF16))
                selb_s = sc.enter_context(nc.sbuf_tensor(f"selb_{tag}", [64, NE * 128], BF16))
                gsc = sc.enter_context(nc.sbuf_tensor(f"gsc_{tag}", [128, T], F32))
                gat = sc.enter_context(nc.sbuf_tensor(f"gat_{tag}", [128, 128], F32))
                maskT = sc.enter_context(nc.sbuf_tensor(f"maskT_{tag}", [64, T], BF16))
                mkb0 = sc.enter_context(nc.sbuf_tensor(f"mkb0_{tag}", [128, T], BF16))
                mkb1 = sc.enter_context(nc.sbuf_tensor(f"mkb1_{tag}", [128, T], BF16))
                wda = sc.enter_context(nc.sbuf_tensor(f"wda_{tag}", [128, 1024], BF16))
                wdb = sc.enter_context(nc.sbuf_tensor(f"wdb_{tag}", [128, 1024], BF16))
                heb = [buf(f"he_{tag}{j}") for j in range(16)]
                m_selb = buf(f"selb_{tag}")
                m_gsc = buf(f"gsc_{tag}")
                m_gat = buf(f"gat_{tag}")
                m_maskT = buf(f"maskT_{tag}")
                mkbB = [buf(f"mkb0_{tag}"), buf(f"mkb1_{tag}")]
                wdB = [buf(f"wda_{tag}"), buf(f"wdb_{tag}")]
                mkb = [mkb0, mkb1]
                wdt = [wda, wdb]

                sb.op("gq", lambda q: q.dma_start(out=selb_s[:, :], in_=selb_d),
                      writes=[m_selb])
                emit_ln(bounce_stats=True)
                if sub == "ln":
                    sb.barrier()
                    return
                emit_gating(gsc, m_gsc, gat, m_gat, maskT, m_maskT)
                if sub == "gate":
                    sb.barrier()
                    return

                def load_gu(e, i):
                    slot = i % 2
                    wt_g = wga if slot == 0 else wgb
                    wt_u = wua if slot == 0 else wub
                    bg = wgab if slot == 0 else wgbb
                    bu = wuab if slot == 0 else wubb
                    sb.op("gq", lambda q, e=e, i=i, w=wt_g: q.dma_start(
                        out=w[:, :], in_=wg_d[e, i].rearrange("p a b -> p (a b)")),
                        writes=[bg])
                    sb.op("gq", lambda q, e=e, i=i, w=wt_u: q.dma_start(
                        out=w[:, :], in_=wu_d[e, i].rearrange("p a b -> p (a b)")),
                        writes=[bu])

                def mask_bcast(e, eslot):
                    for n in range(2):
                        sb.op("pe", lambda q, e=e, n=n: q.matmul(
                            PT[0][:, n * 512:(n + 1) * 512],
                            selb_s[:, e * 128:(e + 1) * 128],
                            maskT[:, n * 512:(n + 1) * 512], start=True, stop=True),
                            reads=[m_selb, m_maskT], writes=[ptb[0]], inc=(n == 1))
                    sb.op("act", lambda q, eslot=eslot: q.copy(
                        out=mkb[eslot][:, :], in_=PT[0][:, :]),
                        reads=[ptb[0]], writes=[mkbB[eslot]])

                for g in range(1 if sub == "g1" else 9):
                    exps = [g * 2, g * 2 + 1]
                    for je, e in enumerate(exps):
                        mask_bcast(e, je)
                        for i in range(NI):
                            load_gu(e, i)
                            slot = i % 2
                            wt_g = wga if slot == 0 else wgb
                            wt_u = wua if slot == 0 else wub
                            bg = wgab if slot == 0 else wgbb
                            bu = wuab if slot == 0 else wubb
                            ptg, ptu = PT[2 * slot], PT[2 * slot + 1]
                            pbg, pbu = ptb[2 * slot], ptb[2 * slot + 1]
                            for kc in range(NCH):
                                for n in range(2):
                                    sb.op("pe", lambda q, w=wt_g, kc=kc, n=n, p=ptg: q.matmul(
                                        p[:, n * 512:(n + 1) * 512],
                                        w[:, kc * 128:(kc + 1) * 128],
                                        xn[:, kc * T + n * 512:kc * T + (n + 1) * 512],
                                        start=(kc == 0), stop=(kc == NCH - 1)),
                                        reads=[bg] + ([xnb[kc]] if n == 0 else []),
                                        writes=[pbg],
                                        inc=(kc == NCH - 1 and n == 1))
                            for kc in range(NCH):
                                for n in range(2):
                                    sb.op("pe", lambda q, w=wt_u, kc=kc, n=n, p=ptu: q.matmul(
                                        p[:, n * 512:(n + 1) * 512],
                                        w[:, kc * 128:(kc + 1) * 128],
                                        xn[:, kc * T + n * 512:kc * T + (n + 1) * 512],
                                        start=(kc == 0), stop=(kc == NCH - 1)),
                                        reads=[bu] + ([xnb[kc]] if n == 0 else []),
                                        writes=[pbu],
                                        inc=(kc == NCH - 1 and n == 1))
                            st = stg0 if slot == 0 else stg1
                            m_st = m_stg0 if slot == 0 else m_stg1
                            sb.op("act", lambda q, p=ptg, st=st: q.activation(
                                st[:, :], p[:, :], AF.Silu),
                                reads=[pbg], writes=[m_st])
                            hslot = je * 8 + i
                            sb.op("dve", lambda q, p=ptu, st=st, h=hslot: q.tensor_mul(
                                he[:, h * T:(h + 1) * T], p[:, :], st[:, :]),
                                reads=[pbu, m_st], writes=[heb[hslot]])
                            sb.op("dve", lambda q, h=hslot, je=je: q.tensor_mul(
                                he[:, h * T:(h + 1) * T], he[:, h * T:(h + 1) * T],
                                mkb[je][:, :]),
                                reads=[heb[hslot], mkbB[je]], writes=[heb[hslot]])
                    for f in range(NCH):
                        for je, e in enumerate(exps):
                            w = wdt[je]
                            wb = wdB[je]
                            sb.op("gq", lambda q, e=e, f=f, w=w: q.dma_start(
                                out=w[:, :], in_=wd_d[e, f].rearrange("p a b -> p (a b)")),
                                writes=[wb])
                            for i in range(NI):
                                h = je * 8 + i
                                first = (je == 0 and i == 0)
                                last = (je == 1 and i == NI - 1)
                                pt = PT[2 + (f % 2)]
                                pb = ptb[2 + (f % 2)]
                                for n in range(2):
                                    sb.op("pe", lambda q, w=w, i=i, n=n, h=h, pt=pt,
                                          fi=first, la=last: q.matmul(
                                        pt[:, n * 512:(n + 1) * 512],
                                        w[:, i * 128:(i + 1) * 128],
                                        he[:, h * T + n * 512:h * T + (n + 1) * 512],
                                        start=fi, stop=la),
                                        reads=[wb] + ([heb[h]] if n == 0 else []),
                                        writes=[pb],
                                        inc=(last and n == 1))
                        pt = PT[2 + (f % 2)]
                        pb = ptb[2 + (f % 2)]
                        sb.op("dve", lambda q, f=f, pt=pt: q.tensor_add(
                            xfm[:, f * T:(f + 1) * T], pt[:, :],
                            xfm[:, f * T:(f + 1) * T]),
                            reads=[pb, xb[f]], writes=[xb[f]])
                sb.barrier()

        def emit_qkv_and_exchange():
            sc = contextlib.ExitStack()
            with sc:
                vstg0 = sc.enter_context(nc.sbuf_tensor("vstg0", [128, 2080], BF16))
                vstg1 = sc.enter_context(nc.sbuf_tensor("vstg1", [128, 2080], BF16))
                bvb = sc.enter_context(nc.sbuf_tensor("bvb", [128, D], BF16))
                q2t = sc.enter_context(nc.sbuf_tensor("q2t", [128, T], BF16))
                sel2 = sc.enter_context(nc.sbuf_tensor("sel2", [128, 2], BF16))
                bndq = sc.enter_context(nc.sbuf_tensor("bndq", [32, T], F32))
                bnds = sc.enter_context(nc.sbuf_tensor("bnds", [32, T], F32))
                bndb = sc.enter_context(nc.sbuf_tensor("bndb", [32, 2 * T], BF16))
                k2f = sc.enter_context(nc.sbuf_tensor("k2f", [32, S], F32))
                kmx = sc.enter_context(nc.sbuf_tensor("kmx", [32, 2], F32))
                vstgB = [buf("vstg0"), buf("vstg1")]
                vstg = [vstg0, vstg1]
                m_bvb = buf("bvb")
                m_q2 = buf("q2t")
                m_sel2 = buf("sel2")
                m_bndq = buf("bndq")
                m_bnds = buf("bnds")
                m_bndb = buf("bndb")
                m_k2f = buf("k2f")
                m_kmx = buf("kmx")

                emit_ln(bounce_stats=False)

                sb.op("dve", lambda q: q.memset(sel2[:, :], 0.0), writes=[m_sel2])
                sb.op("dve", lambda q: q.memset(sel2[0:64, 0:1], 1.0), writes=[m_sel2])
                sb.op("dve", lambda q: q.memset(sel2[64:128, 1:2], 1.0), writes=[m_sel2])

                sb.op("sq", lambda q: q.dma_start(
                    out=rows32[0:1, 0:D], in_=bv_d.rearrange("(o n) -> o n", o=1)),
                    writes=[m_rows])
                for j in range(4):
                    pt = PT[2] if j < 2 else PT[3]
                    pb = ptb[2] if j < 2 else ptb[3]
                    off = (j % 2) * 512
                    sb.op("pe", lambda q, j=j, pt=pt, off=off: q.matmul(
                        pt[:, off:off + 512], ones_r[:, :],
                        rows32[0:1, j * 512:(j + 1) * 512], start=True, stop=True),
                        reads=[m_ones, m_rows], writes=[pb], inc=(j % 2 == 1))
                for j in range(4):
                    pt = PT[2] if j < 2 else PT[3]
                    pb = ptb[2] if j < 2 else ptb[3]
                    off = (j % 2) * 512
                    sb.op("act", lambda q, j=j, pt=pt, off=off: q.copy(
                        out=bvb[:, j * 512:(j + 1) * 512], in_=pt[:, off:off + 512]),
                        reads=[pb], writes=[m_bvb])

                def proj_fm(w_d, bcol, parkd, m_park, normd, m_normd):
                    for fc in range(NCH):
                        slot = fc % 2
                        wt = wga if slot == 0 else wgb
                        wtb = wgab if slot == 0 else wgbb
                        sb.op("gq", lambda q, fc=fc, wt=wt, w_d=w_d: q.dma_start(
                            out=wt[:, :], in_=w_d[fc].rearrange("p a b -> p (a b)")),
                            writes=[wtb])
                        pt, pb = PT[slot], ptb[slot]
                        for kc in range(NCH):
                            for n in range(2):
                                sb.op("pe", lambda q, wt=wt, kc=kc, n=n, pt=pt: q.matmul(
                                    pt[:, n * 512:(n + 1) * 512],
                                    wt[:, kc * 128:(kc + 1) * 128],
                                    xn[:, kc * T + n * 512:kc * T + (n + 1) * 512],
                                    start=(kc == 0), stop=(kc == NCH - 1)),
                                    reads=[wtb] + ([xnb[kc]] if n == 0 else []),
                                    writes=[pb], inc=(kc == NCH - 1 and n == 1))
                        st = stg0 if slot == 0 else stg1
                        m_st = m_stg0 if slot == 0 else m_stg1
                        sb.op("act", lambda q, pt=pt, st=st, fc=fc, bcol=bcol: q.activation(
                            st[:, :], pt[:, :], AF.Identity,
                            bias=biases[:, bcol * 16 + fc:bcol * 16 + fc + 1]),
                            reads=[pb, m_bias], writes=[m_st])
                        sb.op("sq", lambda q, st=st, fc=fc, parkd=parkd: q.dma_start(
                            out=parkd[fc * 128:(fc + 1) * 128, :], in_=st[:, :]),
                            reads=[m_st], writes=[m_park])
                        sb.op("act", lambda q, st=st: q.activation(
                            q2t[:, :], st[:, :], AF.Square),
                            reads=[m_st], writes=[m_q2])
                        for n in range(2):
                            sb.op("pe", lambda q, n=n: q.matmul(
                                PT[2][0:2, n * 512:(n + 1) * 512], sel2[:, :],
                                q2t[:, n * 512:(n + 1) * 512],
                                start=True, stop=True),
                                reads=[m_q2, m_sel2], writes=[ptb[2]], inc=(n == 1))
                        sb.op("act", lambda q: q.copy(
                            out=scr32[0:2, 0:T], in_=PT[2][0:2, :]),
                            reads=[ptb[2]], writes=[m_scr])
                        for hh in range(2):
                            sb.op("sq", lambda q, fc=fc, hh=hh, normd=normd: q.dma_start(
                                out=normd[fc * 2 + hh:fc * 2 + hh + 1, :],
                                in_=scr32[hh:hh + 1, 0:T]),
                                reads=[m_scr], writes=[m_normd])

                proj_fm(wk_d, 1, k_park, m_kpark, kn2_loc, m_kn2d)

                sb.op("dve", lambda q: q.memset(
                    vstg0.ap().rearrange("p (h e) -> p h e", e=65)[:, :, 64:65], 1.0),
                    writes=[vstgB[0]])
                sb.op("dve", lambda q: q.memset(
                    vstg1.ap().rearrange("p (h e) -> p h e", e=65)[:, :, 64:65], 1.0),
                    writes=[vstgB[1]])
                for tp in range(4):
                    for kc in range(NCH):
                        slot = kc % 2
                        wt = wga if slot == 0 else wgb
                        wtb = wgab if slot == 0 else wgbb
                        sb.op("gq", lambda q, kc=kc, wt=wt: q.dma_start(
                            out=wt[:, :], in_=wv_d[kc * 128:(kc + 1) * 128, :]),
                            writes=[wtb])
                        for half in range(2):
                            tc = tp * 2 + half
                            pts = (PT[0], PT[1]) if half == 0 else (PT[2], PT[3])
                            pbs = (ptb[0], ptb[1]) if half == 0 else (ptb[2], ptb[3])
                            for j in range(4):
                                pt = pts[j // 2]
                                pb = pbs[j // 2]
                                off = (j % 2) * 512
                                sb.op("pe", lambda q, tc=tc, kc=kc, j=j, wt=wt,
                                      pt=pt, off=off: q.matmul(
                                    pt[:, off:off + 512],
                                    xn[:, kc * T + tc * 128:kc * T + (tc + 1) * 128],
                                    wt[:, j * 512:(j + 1) * 512],
                                    start=(kc == 0), stop=(kc == NCH - 1)),
                                    reads=[wtb] + ([xnb[kc]] if j == 0 else []),
                                    writes=[pb],
                                    inc=(kc == NCH - 1 and j == 3))
                    for half in range(2):
                        tc = tp * 2 + half
                        pts = (PT[0], PT[1]) if half == 0 else (PT[2], PT[3])
                        pbs = (ptb[0], ptb[1]) if half == 0 else (ptb[2], ptb[3])
                        vs = vstg[half]
                        vb = vstgB[half]
                        for j in range(4):
                            pt = pts[j // 2]
                            pb = pbs[j // 2]
                            off = (j % 2) * 512
                            sb.op("dve", lambda q, vs=vs, j=j, pt=pt, off=off: q.tensor_add(
                                vs.ap()[:, j * 520:(j + 1) * 520].rearrange(
                                    "p (h e) -> p h e", e=65)[:, :, 0:64],
                                pt[:, off:off + 512].rearrange("p (h e) -> p h e", e=64),
                                bvb[:, j * 512:(j + 1) * 512].rearrange(
                                    "p (h e) -> p h e", e=64)),
                                reads=[pb, m_bvb], writes=[vb])
                        sb.op("sq", lambda q, tc=tc, vs=vs: q.dma_start(
                            out=v_park[tc * 128:(tc + 1) * 128, :], in_=vs[:, :]),
                            reads=[vb], writes=[m_vpark])

                PAIRS = [[0, 1], [2, 3], [4, 5], [6, 7]]
                sb.op("gq", lambda q: q.dma_start(out=u32t[0:1, 3:4],
                                                  in_=u32t[0:1, 2:3]),
                      reads=[m_kpark, m_vpark, m_kn2d])

                def colls(q):
                    q.collective_compute(
                        "AllGather", ALU.bypass, replica_groups=PAIRS,
                        ins=[k_park.opt()], outs=[kg.opt()]).then_inc(cc_k)
                    q.collective_compute(
                        "AllGather", ALU.bypass, replica_groups=PAIRS,
                        ins=[v_park.opt()], outs=[vg.opt()]).then_inc(cc_v)
                    q.collective_compute(
                        "AllGather", ALU.bypass, replica_groups=PAIRS,
                        ins=[kn2_loc.opt()], outs=[ng.opt()]).then_inc(cc_n)
                sb.raw("gq", colls)

                proj_fm(wq_d, 0, q_park, m_qpark, bnc_qn, m_qnd)
                sb.op("sq", lambda q: q.dma_start(out=bndq[:, :], in_=bnc_qn),
                      reads=[m_qnd], writes=[m_bndq])
                sb.op("act", lambda q: q.activation(bnds[:, :], bndq[:, :], AF.Sqrt),
                      reads=[m_bndq], writes=[m_bnds])

                def collwait(q):
                    q.wait_ge(cc_k, 1)
                    q.wait_ge(cc_v, 1)
                    q.wait_ge(cc_n, 1)
                sb.raw("gq", collwait)
                sb.op("gq", lambda q: q.dma_start(out=u32t[0:1, 3:4],
                                                  in_=u32t[0:1, 2:3]),
                      writes=[m_sh])

                sb.op("sq", lambda q: q.dma_start(out=k2f[:, 0:T], in_=ng[0:NH, :]),
                      reads=[m_sh], writes=[m_k2f])
                sb.op("sq", lambda q: q.dma_start(out=k2f[:, T:2 * T], in_=ng[NH:2 * NH, :]),
                      reads=[m_sh], writes=[m_k2f])
                sb.op("dve", lambda q: q.reduce_max(kmx[:, 0:1], k2f[:, :], axis=AX.X),
                      reads=[m_k2f], writes=[m_kmx])
                sb.op("act", lambda q: q.activation(kmx[:, 1:2], kmx[:, 0:1], AF.Sqrt),
                      reads=[m_kmx], writes=[m_kmx])
                sb.op("dve", lambda q: q.tensor_scalar(
                    bndq[:, :], bnds[:, :], kmx[:, 1:2], -1.01, ALU.mult, ALU.mult),
                    reads=[m_bnds, m_kmx], writes=[m_bndq])
                sb.op("dve", lambda q: q.tensor_scalar_add(
                    bndq[:, :], bndq[:, :], -6.0),
                    reads=[m_bndq], writes=[m_bndq])
                sb.op("dve", lambda q: q.tensor_copy(bndb[:, 0:T], bndq[:, :]),
                      reads=[m_bndq], writes=[m_bndb])
                sb.op("dve", lambda q: q.tensor_sub(bnds[:, :], bndq[:, :],
                                                    bndb[:, 0:T]),
                      reads=[m_bndq, m_bndb], writes=[m_bnds])
                sb.op("dve", lambda q: q.tensor_copy(bndb[:, T:2 * T], bnds[:, :]),
                      reads=[m_bnds], writes=[m_bndb])
                sb.op("sq", lambda q: q.dma_start(out=bhl_park[0:32, :], in_=bndb[:, 0:T]),
                      reads=[m_bndb], writes=[m_bhl])
                sb.op("sq", lambda q: q.dma_start(out=bhl_park[32:64, :], in_=bndb[:, T:2 * T]),
                      reads=[m_bndb], writes=[m_bhl])
                sb.barrier()

        def emit_attention_core():
            sc = contextlib.ExitStack()
            with sc:
                atf = sc.enter_context(nc.sbuf_tensor("atf", [128, 16 * T], BF16))
                vq = sc.enter_context(nc.sbuf_tensor("vq", [128, 16 * 520], BF16))
                ka0 = sc.enter_context(nc.sbuf_tensor("ka0", [66, S], BF16))
                ka1 = sc.enter_context(nc.sbuf_tensor("ka1", [66, S], BF16))
                qa0 = sc.enter_context(nc.sbuf_tensor("qa0", [66, T], BF16))
                qa1 = sc.enter_context(nc.sbuf_tensor("qa1", [66, T], BF16))
                pb0 = sc.enter_context(nc.sbuf_tensor("pb0", [128, T], BF16))
                pb1 = sc.enter_context(nc.sbuf_tensor("pb1", [128, T], BF16))
                shm = sc.enter_context(nc.sbuf_tensor("shm", [64, 128], BF16))
                odd = sc.enter_context(nc.sbuf_tensor("odd", [64, T], BF16))
                atfb = [buf(f"atf{j}") for j in range(NCH)]
                m_vq = [buf("vq0"), buf("vq1")]
                kaB = [buf("ka0"), buf("ka1")]
                qaB = [buf("qa0"), buf("qa1")]
                pbB = [buf("pb0"), buf("pb1")]
                m_shm = buf("shm")
                m_odd = buf("odd")
                ka = [ka0, ka1]
                qa = [qa0, qa1]
                pbt = [pb0, pb1]

                sb.op("dve", lambda q: q.memset(ka0[64:66, :], 1.0), writes=[kaB[0]])
                sb.op("dve", lambda q: q.memset(ka1[64:66, :], 1.0), writes=[kaB[1]])
                sb.op("gq", lambda q: q.dma_start(out=shm[:, :], in_=shmat_d),
                      writes=[m_shm])

                dnrow = [mean_b, rstd_b]
                m_dnr = [m_meanb, m_rstdb]
                m_sums = [buf("sums0"), buf("sums1")]

                def emit_bounce(h):
                    sp = h % 2
                    avp, avb = PT[2 + sp], ptb[2 + sp]
                    sb.op("act", lambda q, avp=avp: q.copy(
                        out=scr32[64:65, 0:T], in_=avp[64:65, :]),
                        reads=[avb], writes=[m_scr])
                    sb.op("sq", lambda q, sp=sp: q.dma_start(
                        out=bnc_sums[sp:sp + 1, :], in_=scr32[64:65, 0:T]),
                        reads=[m_scr], writes=[m_sums[sp]])
                    sb.op("sq", lambda q, sp=sp: q.dma_start(
                        out=dnrow[sp][0:1, 0:T],
                        in_=bnc_sums[sp:sp + 1, :]),
                        reads=[m_sums[sp]], writes=[m_dnr[sp]])
                    sb.op("dve", lambda q, sp=sp: q.reciprocal(
                        dnrow[sp][0:1, 0:T], dnrow[sp][0:1, 0:T]),
                        reads=[m_dnr[sp]], writes=[m_dnr[sp]])

                def emit_norm(h):
                    sp = h % 2
                    avp, avb = PT[2 + sp], ptb[2 + sp]
                    for n in range(2):
                        sb.op("pe", lambda q, n=n, sp=sp: q.matmul(
                            PT[0][0:64, n * 512:(n + 1) * 512],
                            ones_r[:, 0:64],
                            dnrow[sp][0:1, n * 512:(n + 1) * 512],
                            start=True, stop=True),
                            reads=[m_ones, m_dnr[sp]], writes=[ptb[0]],
                            inc=(n == 1))
                    sb.op("act", lambda q: q.copy(
                        out=scr32[0:64, :], in_=PT[0][0:64, :]),
                        reads=[ptb[0]], writes=[m_scr])
                    if sp == 0:
                        sb.op("dve", lambda q, h=h, avp=avp: q.tensor_mul(
                            atf[0:64, (h // 2) * T:(h // 2 + 1) * T],
                            avp[0:64, :], scr32[0:64, :]),
                            reads=[avb, m_scr], writes=[atfb[h // 2]])
                    else:
                        sb.op("dve", lambda q, avp=avp: q.tensor_mul(
                            odd[:, :], avp[0:64, :], scr32[0:64, :]),
                            reads=[avb, m_scr], writes=[m_odd])
                        for n in range(2):
                            sb.op("pe", lambda q, n=n: q.matmul(
                                PT[1][:, n * 512:(n + 1) * 512], shm[:, :],
                                odd[:, n * 512:(n + 1) * 512],
                                start=True, stop=True),
                                reads=[m_shm, m_odd], writes=[ptb[1]],
                                inc=(n == 1))
                        sb.op("act", lambda q, h=h: q.copy(
                            out=atf[64:128, (h // 2) * T:(h // 2 + 1) * T],
                            in_=PT[1][64:128, :]),
                            reads=[ptb[1]], writes=[atfb[h // 2]])

                for quarter in range(4):
                    for kc in range(NCH):
                        sb.op("aq", lambda q, quarter=quarter, kc=kc: q.dma_start(
                            out=vq[:, kc * 520:(kc + 1) * 520],
                            in_=vg[kc * 128:(kc + 1) * 128,
                                   quarter * 520:(quarter + 1) * 520]),
                            reads=[m_sh], writes=[m_vq[kc // 8]])
                    for hh in range(8):
                        h = quarter * 8 + hh
                        hs = hh % 2
                        kat, qat = ka[hs], qa[hs]
                        kab, qab = kaB[hs], qaB[hs]
                        sb.op("aq", lambda q, h=h, kat=kat: q.dma_start(
                            out=kat[0:64, 0:T], in_=kg[h * 64:(h + 1) * 64, :]),
                            reads=[m_sh], writes=[kab])
                        sb.op("aq", lambda q, h=h, kat=kat: q.dma_start(
                            out=kat[0:64, T:2 * T],
                            in_=kg[D + h * 64:D + (h + 1) * 64, :]),
                            reads=[m_sh], writes=[kab])
                        sb.op("aq", lambda q, h=h, qat=qat: q.dma_start(
                            out=qat[0:64, :], in_=q_park[h * 64:(h + 1) * 64, :]),
                            reads=[m_qpark], writes=[qab])
                        sb.op("aq", lambda q, h=h, qat=qat: q.dma_start(
                            out=qat[64:65, :], in_=bhl_park[h:h + 1, :]),
                            reads=[m_bhl], writes=[qab])
                        sb.op("aq", lambda q, h=h, qat=qat: q.dma_start(
                            out=qat[65:66, :], in_=bhl_park[32 + h:33 + h, :]),
                            reads=[m_bhl], writes=[qab])
                        avp = PT[2 + hs]
                        avb = ptb[2 + hs]

                        def emit_qk(kc):
                            qkp, qkb = PT[kc % 2], ptb[kc % 2]
                            for n in range(2):
                                sb.op("pe", lambda q, kat=kat, qat=qat, kc=kc,
                                      n=n, qkp=qkp: q.matmul(
                                    qkp[:, n * 512:(n + 1) * 512],
                                    kat[:, kc * 128:(kc + 1) * 128],
                                    qat[:, n * 512:(n + 1) * 512],
                                    start=True, stop=True),
                                    reads=[kab, qab], writes=[qkb], inc=(n == 1))

                        emit_qk(0)
                        for kc in range(NCH):
                            qkp = PT[kc % 2]
                            qkb = ptb[kc % 2]
                            pbuf = pbt[kc % 2]
                            pbb = pbB[kc % 2]
                            sb.op("act", lambda q, qkp=qkp, pbuf=pbuf: q.activation(
                                pbuf[:, :], qkp[:, :], AF.Exp, scale=SCALE),
                                reads=[qkb], writes=[pbb])
                            if kc + 1 < NCH:
                                emit_qk(kc + 1)
                            for n in range(2):
                                sb.op("pe", lambda q, kc=kc, hh=hh, n=n, pbuf=pbuf,
                                      avp=avp: q.matmul(
                                    avp[0:65, n * 512:(n + 1) * 512],
                                    vq[:, kc * 520 + hh * 65:kc * 520 + (hh + 1) * 65],
                                    pbuf[:, n * 512:(n + 1) * 512],
                                    start=(kc == 0), stop=(kc == NCH - 1)),
                                    reads=[m_vq[kc // 8], pbb], writes=[avb],
                                    inc=(n == 1))
                        emit_bounce(h)
                        if h > 0:
                            emit_norm(h - 1)
                emit_norm(NH - 1)

                for fc in range(NCH):
                    slot = fc % 2
                    wt = wga if slot == 0 else wgb
                    wtb = wgab if slot == 0 else wgbb
                    sb.op("gq", lambda q, fc=fc, wt=wt: q.dma_start(
                        out=wt[:, :], in_=wo_d[fc].rearrange("p a b -> p (a b)")),
                        writes=[wtb])
                    pt, pb = PT[slot], ptb[slot]
                    for kc in range(NCH):
                        for n in range(2):
                            sb.op("pe", lambda q, wt=wt, kc=kc, n=n, pt=pt: q.matmul(
                                pt[:, n * 512:(n + 1) * 512],
                                wt[:, kc * 128:(kc + 1) * 128],
                                atf[:, kc * T + n * 512:kc * T + (n + 1) * 512],
                                start=(kc == 0), stop=(kc == NCH - 1)),
                                reads=[wtb] + ([atfb[kc]] if n == 0 else []),
                                writes=[pb], inc=(kc == NCH - 1 and n == 1))
                    sb.op("dve", lambda q, fc=fc, pt=pt: q.scalar_tensor_tensor(
                        xfm[:, fc * T:(fc + 1) * T], pt[:, :],
                        biases[:, 32 + fc:33 + fc], xfm[:, fc * T:(fc + 1) * T],
                        ALU.add, ALU.add),
                        reads=[pb, m_bias, xb[fc]], writes=[xb[fc]])
                sb.barrier()

        def dump_x(dst):
            for c in range(NCH):
                sb.op("sq", lambda q, c=c: q.dma_start(
                    out=dst[c * 128:(c + 1) * 128, :],
                    in_=xfm[:, c * T:(c + 1) * T]),
                    reads=[xb[c]])

        if stage.startswith("ffn1_"):
            emit_ffn("f1", sub=stage.split("_", 1)[1])
            dump_x(dbg1_d)
        elif stage in ("ffn1", "qkv", "attn", "full"):
            emit_ffn("f1")
            dump_x(dbg1_d)
        if stage in ("qkv", "attn", "full"):
            emit_qkv_and_exchange()
        if stage in ("attn", "full"):
            emit_attention_core()
            dump_x(dbg2_d)
        if stage == "full":
            emit_ffn("f2")

        m_out = buf("out")
        for c in range(NCH):
            sb.op("sq", lambda q, c=c: q.dma_start(
                out=out_d[c * 128:(c + 1) * 128, :], in_=xfm[:, c * T:(c + 1) * T]),
                reads=[xb[c]], writes=[m_out])
        fin = sb.cnt["sq"]
        sb.raw("sq", lambda q: q.wait_ge(sb.sems["sq"], fin))

        blk = es.enter_context(nc.Block())
        for sect in ("tensor", "scalar", "vector", "gpsimd", "sync"):
            fns = sb.prog[sect]

            def body(q, _fns=fns):
                for f in _fns:
                    f(q)
            getattr(blk, sect)(body)

    return nc



_CACHE = {}


def _relayout(inputs):
    f = np.float32
    wq = inputs["L0_attn_Wq"].astype(f, copy=False)
    wk = inputs["L0_attn_Wk"].astype(f, copy=False)
    wo = inputs["L0_attn_Wo"].astype(f, copy=False)

    def wpl(w):
        return np.ascontiguousarray(
            w.reshape(16, 128, 16, 128).transpose(2, 1, 0, 3))

    wg = inputs["L0_ffn_wg"]
    wu = inputs["L0_ffn_wu"]
    wd = inputs["L0_ffn_wd"]
    wg_s = inputs["L0_ffn_wg_s"]
    wu_s = inputs["L0_ffn_wu_s"]
    wd_s = inputs["L0_ffn_wd_s"]

    def gu_pl(m, ms):
        a = m.reshape(16, 16, 128, 8, 128).transpose(0, 3, 2, 1, 4)
        b = ms.reshape(16, 128, 2, 8, 128).transpose(2, 3, 1, 0, 4)
        return np.ascontiguousarray(np.concatenate([a, b], axis=0))

    def d_pl(m, ms):
        a = m.reshape(16, 8, 128, 16, 128).transpose(0, 3, 2, 1, 4)
        b = ms.reshape(2, 8, 128, 16, 128).transpose(0, 3, 2, 1, 4)
        return np.ascontiguousarray(np.concatenate([a, b], axis=0))

    selb = np.zeros((64, NE * 128), np.float32)
    for e in range(NE):
        r = e if e < 16 else 16 + e
        selb[r, e * 128:(e + 1) * 128] = 1.0
    import ml_dtypes
    selb = selb.astype(ml_dtypes.bfloat16)
    shmat = np.zeros((64, 128), np.float32)
    for p in range(64):
        shmat[p, 64 + p] = 1.0
    shmat = shmat.astype(ml_dtypes.bfloat16)

    ct = np.ascontiguousarray(
        inputs["L0_ffn_centroids"].astype(f, copy=False).T.reshape(16, 128, 16))

    return {
        "wq": wpl(wq), "wk": wpl(wk), "wo": wpl(wo),
        "wv": np.ascontiguousarray(inputs["L0_attn_Wv"].astype(f, copy=False)),
        "wg": gu_pl(wg, wg_s), "wu": gu_pl(wu, wu_s), "wd": d_pl(wd, wd_s),
        "ct": ct,
        "bq": np.ascontiguousarray(inputs["L0_attn_bq"].astype(f, copy=False)),
        "bk": np.ascontiguousarray(inputs["L0_attn_bk"].astype(f, copy=False)),
        "bv": np.ascontiguousarray(inputs["L0_attn_bv"].astype(f, copy=False)),
        "bo": np.ascontiguousarray(inputs["L0_attn_bo"].astype(f, copy=False)),
        "gb": np.ascontiguousarray(inputs["L0_ffn_gate_bias"].astype(f, copy=False)),
        "ident": np.eye(128, dtype=np.float32),
        "selb": selb, "shmat": shmat,
    }


def kernel(**inputs):
    x = np.asarray(inputs["L0_attn_x"], dtype=np.float32)
    shared = _relayout(inputs)

    nc = build_program()
    in_maps = []
    for c in range(8):
        b, h = c // 2, c % 2
        m = dict(shared)
        m["x_in"] = np.ascontiguousarray(x[b, h * T:(h + 1) * T, :].T)
        in_maps.append(m)
    res = run_bass_kernel_spmd(nc, in_maps, core_ids=list(range(8)))
    out = np.empty((4, 2048, 2048), np.float32)
    for c in range(8):
        b, h = c // 2, c % 2
        out[b, h * T:(h + 1) * T, :] = res.results[c]["out"].T
    return out


# version 06
# speedup vs baseline: 1.3733x; 1.2121x over previous
import contextlib
import numpy as np
import concourse.bass as bass
import concourse.mybir as mybir
from concourse.bass_utils import run_bass_kernel_spmd

F32 = mybir.dt.float32
BF16 = mybir.dt.bfloat16
U32 = mybir.dt.uint32
AF = mybir.ActivationFunctionType
ALU = mybir.AluOpType
AX = mybir.AxisListType

T = 1024
D = 2048
NCH = 16
NI = 8
NE = 18
NH = 32
S = 2048
SCALE = float(1.0 / np.sqrt(np.float32(32.0)))
EPS_LN = 1e-5
EPS_GATE = 1e-9



class Buf:
    __slots__ = ("name", "w", "readers")

    def __init__(self, name):
        self.name = name
        self.w = None
        self.readers = []


class SB:
    ENGS = ("pe", "act", "dve", "pool", "sq", "aq", "gq")
    DMA = frozenset(("sq", "aq", "gq"))
    SECT = {"pe": "tensor", "act": "scalar", "dve": "vector",
            "pool": "gpsimd", "sq": "sync", "aq": "scalar", "gq": "gpsimd"}

    def __init__(self):
        self.prog = {"tensor": [], "scalar": [], "vector": [], "gpsimd": [], "sync": []}
        self.cnt = {e: 0 for e in self.ENGS}
        self.hw = {e: {d: 0 for d in self.ENGS} for e in self.ENGS}
        self.pend = {e: [] for e in self.ENGS}
        self.sems = {}

    def raw(self, eng, fn):
        self.prog[self.SECT[eng]].append(fn)

    def _wait(self, eng, dep, val):
        if val <= self.hw[eng][dep]:
            return
        self.hw[eng][dep] = val
        sems = self.sems

        def w(q, _d=dep, _v=val):
            q.wait_ge(sems[_d], _v)
        self.raw(eng, w)

    def op(self, eng, fn, reads=(), writes=(), inc=True):
        deps = {}
        for b in reads:
            if b.w is not None:
                e, c = b.w
                deps[e] = max(deps.get(e, 0), c)
        for b in writes:
            if b.w is not None:
                e, c = b.w
                deps[e] = max(deps.get(e, 0), c)
            for e, c in b.readers:
                deps[e] = max(deps.get(e, 0), c)
        for e, c in deps.items():
            if e == "pe" and eng == "pe":
                continue
            self._wait(eng, e, c)
        step = 16 if eng in self.DMA else 1
        if inc:
            self.cnt[eng] += step
            newc = self.cnt[eng]
            sems = self.sems

            def run(q, _fn=fn, _e=eng):
                _fn(q).then_inc(sems[_e], 16 if _e in self.DMA else 1)
            self.raw(eng, run)
            for b, is_w in self.pend[eng]:
                if is_w:
                    b.w = (eng, newc)
                    b.readers = []
                else:
                    b.readers.append((eng, newc))
            self.pend[eng].clear()
            for b in writes:
                b.w = (eng, newc)
                b.readers = []
            for b in reads:
                b.readers.append((eng, newc))
        else:
            self.raw(eng, lambda q, _fn=fn: _fn(q))
            for b in writes:
                self.pend[eng].append((b, True))
            for b in reads:
                self.pend[eng].append((b, False))

    def barrier(self):
        for e in self.ENGS:
            assert not self.pend[e], f"pending unstamped ops on {e} at barrier"



def build_program(stage="full"):
    nc = bass.Bass("TRN2", target_bir_lowering=False)
    sb = SB()

    def ein(name, shape, dt=F32):
        return nc.dram_tensor(name, shape, dt, kind="ExternalInput").ap()

    x_in = ein("x_in", [D, T])
    wq_d = ein("wq", [NCH, 128, NCH, 128])
    wk_d = ein("wk", [NCH, 128, NCH, 128])
    wo_d = ein("wo", [NCH, 128, NCH, 128])
    wv_d = ein("wv", [D, D])
    wg_d = ein("wg", [NE, NI, 128, NCH, 128])
    wu_d = ein("wu", [NE, NI, 128, NCH, 128])
    wd_d = ein("wd", [NE, NCH, 128, NI, 128])
    ct_d = ein("ct", [NCH, 128, 16])
    bq_d = ein("bq", [D])
    bk_d = ein("bk", [D])
    bv_d = ein("bv", [D])
    bo_d = ein("bo", [D])
    gb_d = ein("gb", [16])
    ident_d = ein("ident", [128, 128])
    selb_d = ein("selb", [64, NE * 128], BF16)
    shmat_d = ein("shmat", [64, 128], BF16)
    out_d = nc.dram_tensor("out", [D, T], F32, kind="ExternalOutput").ap()
    dbg1_d = nc.dram_tensor("dbg1", [D, T], F32, kind="ExternalOutput").ap()
    dbg2_d = nc.dram_tensor("dbg2", [D, T], F32, kind="ExternalOutput").ap()

    def dint(name, shape, dt):
        return nc.dram_tensor(name, shape, dt, kind="Internal").ap()

    q_park = dint("q_park", [D, T], BF16)
    k_park = dint("k_park", [D, T], BF16)
    v_park = dint("v_park", [T, NH * 65], BF16)
    bhl_park = dint("bhl_park", [64, T], BF16)
    bnc_stat = dint("bnc_stat", [2, T], F32)
    bnc_csum = dint("bnc_csum", [16], F32)
    bnc_qn = dint("bnc_qn", [NH, T], F32)
    kn2_loc = dint("kn2_loc", [NH, T], F32)
    bnc_sums = dint("bnc_sums", [2, T], F32)

    kg = nc.dram_tensor("kg", [2 * D, T], BF16, kind="Internal").ap()
    vg = nc.dram_tensor("vg", [S, NH * 65], BF16, kind="Internal").ap()
    ng = nc.dram_tensor("ng", [2 * NH, T], F32, kind="Internal").ap()

    es = contextlib.ExitStack()
    B = {}

    def buf(name):
        if name not in B:
            B[name] = Buf(name)
        return B[name]

    with es:
        for e in SB.ENGS:
            sb.sems[e] = es.enter_context(nc.semaphore("sem_" + e))
        cc_k = es.enter_context(nc.semaphore("cc_k"))
        cc_v = es.enter_context(nc.semaphore("cc_v"))
        cc_n = es.enter_context(nc.semaphore("cc_n"))

        xfm = es.enter_context(nc.sbuf_tensor("xfm", [128, NCH * T], F32))
        xn = es.enter_context(nc.sbuf_tensor("xn", [128, NCH * T], BF16))
        wga = es.enter_context(nc.sbuf_tensor("wga", [128, 2048], BF16))
        wgb = es.enter_context(nc.sbuf_tensor("wgb", [128, 2048], BF16))
        wua = es.enter_context(nc.sbuf_tensor("wua", [128, 2048], BF16))
        wub = es.enter_context(nc.sbuf_tensor("wub", [128, 2048], BF16))
        mean_b = es.enter_context(nc.sbuf_tensor("mean_b", [128, T], F32))
        rstd_b = es.enter_context(nc.sbuf_tensor("rstd_b", [128, T], F32))
        scr32 = es.enter_context(nc.sbuf_tensor("scr32", [128, T], F32))
        stg0 = es.enter_context(nc.sbuf_tensor("stg0", [128, T], BF16))
        stg1 = es.enter_context(nc.sbuf_tensor("stg1", [128, T], BF16))
        ident = es.enter_context(nc.sbuf_tensor("ident_sb", [128, 128], F32))
        biases = es.enter_context(nc.sbuf_tensor("biases", [128, 48], F32))
        gcol = es.enter_context(nc.sbuf_tensor("gcol", [128, 48], F32))
        csum_b = es.enter_context(nc.sbuf_tensor("csum_b", [128, 16], F32))
        gb_b = es.enter_context(nc.sbuf_tensor("gb_b", [128, 16], F32))
        ones_c = es.enter_context(nc.sbuf_tensor("ones_c", [128, 1], F32))
        ones_cb = es.enter_context(nc.sbuf_tensor("ones_cb", [128, 1], BF16))
        ones_r = es.enter_context(nc.sbuf_tensor("ones_r", [1, 128], F32))
        ones2d = es.enter_context(nc.sbuf_tensor("ones2d", [128, 128], F32))
        u32t = es.enter_context(nc.sbuf_tensor("u32t", [1, 4], U32))

        PT = [es.enter_context(nc.psum_tensor(f"pt{i}", [128, 1024], F32)) for i in range(4)]
        ptb = [buf(f"pt{i}") for i in range(4)]

        xb = [buf(f"x{c}") for c in range(NCH)]
        xnb = [buf(f"xn{c}") for c in range(NCH)]
        wgab, wgbb = buf("wga"), buf("wgb")
        wuab, wubb = buf("wua"), buf("wub")
        m_bias = buf("biases")
        m_ident = buf("ident")
        m_gcol = buf("gcol")
        m_csum = buf("csum_b")
        m_gbb = buf("gb_b")
        m_ones = buf("ones")
        m_ones2 = buf("ones2d")
        m_u32 = buf("u32t")
        m_meanb = buf("mean_b")
        m_rstdb = buf("rstd_b")
        m_scr = buf("scr32")
        m_stg0, m_stg1 = buf("stg0"), buf("stg1")
        m_statd = buf("bnc_stat")
        m_csumd = buf("bnc_csum")
        m_qnd = buf("bnc_qn")
        m_kn2d = buf("kn2_loc")
        m_sumsd = buf("bnc_sums")
        m_qpark = buf("q_park")
        m_kpark = buf("k_park")
        m_vpark = buf("v_park")
        m_bhl = buf("bhl_park")
        m_sh = buf("shared_kv")

        sb.bar_src = u32t[0:1, 2:3]
        sb.bar_dst = {"sq": u32t[0:1, 3:4], "aq": u32t[0:1, 3:4], "gq": u32t[0:1, 3:4]}


        MINCTX = stage.startswith("min")
        for c in range(NCH):
            sb.op("sq", lambda q, c=c: q.dma_start(
                out=xfm[:, c * T:(c + 1) * T], in_=x_in[c * 128:(c + 1) * 128, :]),
                writes=[xb[c]])
        if not MINCTX:
            sb.op("sq", lambda q: q.dma_start(out=ident[:, :], in_=ident_d), writes=[m_ident])
            for j, bd in enumerate((bq_d, bk_d, bo_d)):
                for c in range(NCH):
                    sb.op("sq", lambda q, j=j, c=c, bd=bd: q.dma_start(
                        out=biases[:, j * 16 + c:j * 16 + c + 1],
                        in_=bd[c * 128:(c + 1) * 128].rearrange("(p o) -> p o", o=1)),
                        writes=[m_bias])
            sb.op("sq", lambda q: q.dma_start(
                out=gcol[0:1, 32:48], in_=gb_d.rearrange("(o n) -> o n", o=1)),
                writes=[m_gcol])
        sb.op("dve", lambda q: q.memset(ones_c[:, :], 1.0), writes=[m_ones])
        sb.op("dve", lambda q: q.memset(ones_cb[:, :], 1.0), writes=[m_ones])
        sb.op("dve", lambda q: q.memset(ones_r[:, :], 1.0), writes=[m_ones])
        sb.op("dve", lambda q: q.memset(ones2d[:, :], 1.0), writes=[m_ones2])
        sb.op("dve", lambda q: q.memset(gcol[:, 15:16], EPS_LN), writes=[m_gcol])

        if not MINCTX:
            sb.op("pe", lambda q: q.matmul(PT[2][0:128, 0:16], ones_r[:, :],
                                           gcol[0:1, 32:48], start=True, stop=True),
                  reads=[m_ones, m_gcol], writes=[ptb[2]])
            sb.op("act", lambda q: q.copy(out=gb_b[:, :], in_=PT[2][:, 0:16]),
                  reads=[ptb[2]], writes=[m_gbb])

            ct_sb = es.enter_context(nc.sbuf_tensor("ct_sb", [128, NCH * 16], F32))
            m_ct = buf("ct_sb")
            for c in range(NCH):
                sb.op("sq", lambda q, c=c: q.dma_start(
                    out=ct_sb[:, c * 16:(c + 1) * 16], in_=ct_d[c]), writes=[m_ct])

            for c in range(NCH):
                sb.op("pe", lambda q, c=c: q.matmul(
                    PT[3][0:16, 0:1], ct_sb[:, c * 16:(c + 1) * 16], ones_c[:, :],
                    start=(c == 0), stop=(c == NCH - 1)),
                    reads=[m_ct, m_ones], writes=[ptb[3]], inc=(c == NCH - 1))
            sb.op("act", lambda q: q.copy(out=gcol[0:16, 0:1], in_=PT[3][0:16, 0:1]),
                  reads=[ptb[3]], writes=[m_gcol])
            sb.op("sq", lambda q: q.dma_start(
                out=bnc_csum.rearrange("(p o) -> p o", o=1), in_=gcol[0:16, 0:1]),
                reads=[m_gcol], writes=[m_csumd])
            sb.op("sq", lambda q: q.dma_start(
                out=gcol[0:1, 16:32], in_=bnc_csum.rearrange("(o n) -> o n", o=1)),
                reads=[m_csumd], writes=[m_gcol])
            sb.op("pe", lambda q: q.matmul(PT[2][0:128, 0:16], ones_r[:, :],
                                           gcol[0:1, 16:32], start=True, stop=True),
                  reads=[m_ones, m_gcol], writes=[ptb[2]])
            sb.op("act", lambda q: q.copy(out=csum_b[:, :], in_=PT[2][:, 0:16]),
                  reads=[ptb[2]], writes=[m_csum])

        def emit_ln(bounce_stats):
            for c in range(NCH):
                sb.op("act", lambda q, c=c: q.activation(
                    scr32[:, :], xfm[:, c * T:(c + 1) * T], AF.Square),
                    reads=[xb[c]], writes=[m_scr])
                for n in range(2):
                    sb.op("pe", lambda q, c=c, n=n: q.matmul(
                        PT[2][:, n * 512:(n + 1) * 512], ones2d[:, :],
                        xfm[:, c * T + n * 512:c * T + (n + 1) * 512],
                        start=(c == 0), stop=(c == NCH - 1)),
                        reads=[m_ones2, xb[c]], writes=[ptb[2]],
                        inc=(c == NCH - 1 and n == 1))
                for n in range(2):
                    sb.op("pe", lambda q, c=c, n=n: q.matmul(
                        PT[3][:, n * 512:(n + 1) * 512], ones2d[:, :],
                        scr32[:, n * 512:(n + 1) * 512],
                        start=(c == 0), stop=(c == NCH - 1)),
                        reads=[m_ones2, m_scr], writes=[ptb[3]],
                        inc=(n == 1))
            sb.op("act", lambda q: q.activation(mean_b[:, :], PT[2][:, :],
                                                AF.Copy, scale=1.0 / D),
                  reads=[ptb[2]], writes=[m_meanb])
            sb.op("act", lambda q: q.activation(scr32[:, :], PT[3][:, :],
                                                AF.Copy, scale=1.0 / D),
                  reads=[ptb[3]], writes=[m_scr])
            sb.op("act", lambda q: q.activation(rstd_b[:, :], mean_b[:, :],
                                                AF.Square),
                  reads=[m_meanb], writes=[m_rstdb])
            sb.op("dve", lambda q: q.tensor_sub(scr32[:, :], scr32[:, :],
                                                rstd_b[:, :]),
                  reads=[m_scr, m_rstdb], writes=[m_scr])
            sb.op("act", lambda q: q.activation(scr32[:, :], scr32[:, :],
                                                AF.Sqrt, bias=gcol[:, 15:16]),
                  reads=[m_scr, m_gcol], writes=[m_scr])
            sb.op("dve", lambda q: q.reciprocal(rstd_b[:, :], scr32[:, :]),
                  reads=[m_scr], writes=[m_rstdb])
            if bounce_stats:
                sb.op("sq", lambda q: q.dma_start(out=bnc_stat[0:1, :],
                                                  in_=mean_b[0:1, :]),
                      reads=[m_meanb], writes=[m_statd])
                sb.op("sq", lambda q: q.dma_start(out=bnc_stat[1:2, :],
                                                  in_=rstd_b[0:1, :]),
                      reads=[m_rstdb], writes=[m_statd])
            for c in range(NCH):
                sb.op("dve", lambda q, c=c: q.tensor_sub(
                    scr32[:, :], xfm[:, c * T:(c + 1) * T], mean_b[:, :]),
                    reads=[xb[c], m_meanb], writes=[m_scr])
                sb.op("dve", lambda q, c=c: q.tensor_mul(
                    xn[:, c * T:(c + 1) * T], scr32[:, :], rstd_b[:, :]),
                    reads=[m_scr, m_rstdb], writes=[xnb[c]])

        def emit_gating(gsc, m_gsc, gat, m_gat, maskT, m_maskT):
            for c in range(NCH):
                for n in range(2):
                    sb.op("pe", lambda q, c=c, n=n: q.matmul(
                        PT[0][0:16, n * 512:(n + 1) * 512],
                        ct_sb[:, c * 16:(c + 1) * 16],
                        xfm[:, c * T + n * 512:c * T + (n + 1) * 512],
                        start=(c == 0), stop=(c == NCH - 1)),
                        reads=[m_ct, xb[c]], writes=[ptb[0]],
                        inc=(c == NCH - 1 and n == 1))
            sb.op("act", lambda q: q.copy(out=gsc[0:16, 0:T], in_=PT[0][0:16, :]),
                  reads=[ptb[0]], writes=[m_gsc])
            sb.op("dve", lambda q: q.memset(maskT[:, :], 0.0), writes=[m_maskT])
            sb.op("dve", lambda q: q.memset(maskT[32:34, :], 0.05), writes=[m_maskT])
            for t in range(8):
                sb.op("pe", lambda q, t=t: q.transpose(
                    PT[1][0:128, 0:16], gsc[0:16, t * 128:(t + 1) * 128],
                    ident[0:16, 0:16]),
                    reads=[m_gsc, m_ident], writes=[ptb[1]])
                sb.op("act", lambda q, t=t: q.copy(out=gat[:, 0:16], in_=PT[1][:, 0:16]),
                      reads=[ptb[1]], writes=[m_gat])
                sb.op("sq", lambda q, t=t: q.dma_start(
                    out=gcol[:, 0:1],
                    in_=bnc_stat[0, t * 128:(t + 1) * 128].rearrange("(p o) -> p o", o=1)),
                    reads=[m_statd], writes=[m_gcol])
                sb.op("sq", lambda q, t=t: q.dma_start(
                    out=gcol[:, 1:2],
                    in_=bnc_stat[1, t * 128:(t + 1) * 128].rearrange("(p o) -> p o", o=1)),
                    reads=[m_statd], writes=[m_gcol])
                sb.op("dve", lambda q: q.tensor_scalar(
                    gat[:, 16:32], csum_b[:, :], gcol[:, 0:1], None, ALU.mult),
                    reads=[m_csum, m_gcol], writes=[m_gat])
                sb.op("dve", lambda q: q.tensor_sub(
                    gat[:, 16:32], gat[:, 0:16], gat[:, 16:32]),
                    reads=[m_gat], writes=[m_gat])
                sb.op("dve", lambda q: q.tensor_scalar(
                    gat[:, 16:32], gat[:, 16:32], gcol[:, 1:2], None, ALU.mult),
                    reads=[m_gat, m_gcol], writes=[m_gat])
                sb.op("act", lambda q: q.activation(
                    gat[:, 32:48], gat[:, 16:32], AF.Sigmoid),
                    reads=[m_gat], writes=[m_gat])
                sb.op("dve", lambda q: q.tensor_add(
                    gat[:, 48:64], gat[:, 32:48], gb_b[:, :]),
                    reads=[m_gat, m_gbb], writes=[m_gat])
                sb.op("dve", lambda q: q.reduce_max(
                    gcol[:, 2:3], gat[:, 48:64], axis=AX.X),
                    reads=[m_gat], writes=[m_gcol])
                sb.op("dve", lambda q: q.tensor_single_scalar(
                    gat[:, 64:80], gat[:, 48:64], gcol[:, 2:3], ALU.is_equal),
                    reads=[m_gat, m_gcol], writes=[m_gat])
                sb.op("dve", lambda q: q.tensor_mul(
                    gat[:, 96:112], gat[:, 32:48], gat[:, 64:80]),
                    reads=[m_gat], writes=[m_gat])
                sb.op("dve", lambda q: q.reduce_sum(
                    gcol[:, 3:4], gat[:, 96:112], axis=AX.X),
                    reads=[m_gat], writes=[m_gcol])
                sb.op("dve", lambda q: q.scalar_tensor_tensor(
                    gat[:, 80:96], gat[:, 64:80], -1e30, gat[:, 48:64],
                    ALU.mult, ALU.add),
                    reads=[m_gat], writes=[m_gat])
                sb.op("dve", lambda q: q.reduce_max(
                    gcol[:, 4:5], gat[:, 80:96], axis=AX.X),
                    reads=[m_gat], writes=[m_gcol])
                sb.op("dve", lambda q: q.tensor_single_scalar(
                    gat[:, 96:112], gat[:, 80:96], gcol[:, 4:5], ALU.is_equal),
                    reads=[m_gat, m_gcol], writes=[m_gat])
                sb.op("dve", lambda q: q.tensor_mul(
                    gat[:, 112:128], gat[:, 32:48], gat[:, 96:112]),
                    reads=[m_gat], writes=[m_gat])
                sb.op("dve", lambda q: q.reduce_sum(
                    gcol[:, 5:6], gat[:, 112:128], axis=AX.X),
                    reads=[m_gat], writes=[m_gcol])
                sb.op("dve", lambda q: q.tensor_add(
                    gcol[:, 6:7], gcol[:, 3:4], gcol[:, 5:6]),
                    reads=[m_gcol], writes=[m_gcol])
                sb.op("dve", lambda q: q.tensor_scalar_add(
                    gcol[:, 6:7], gcol[:, 6:7], EPS_GATE),
                    reads=[m_gcol], writes=[m_gcol])
                sb.op("dve", lambda q: q.reciprocal(
                    gcol[:, 7:8], gcol[:, 6:7]),
                    reads=[m_gcol], writes=[m_gcol])
                sb.op("dve", lambda q: q.tensor_scalar(
                    gcol[:, 8:9], gcol[:, 3:4], gcol[:, 7:8], 0.5, ALU.mult, ALU.mult),
                    reads=[m_gcol], writes=[m_gcol])
                sb.op("dve", lambda q: q.tensor_scalar(
                    gcol[:, 9:10], gcol[:, 5:6], gcol[:, 7:8], 0.5, ALU.mult, ALU.mult),
                    reads=[m_gcol], writes=[m_gcol])
                sb.op("dve", lambda q: q.tensor_scalar(
                    gat[:, 0:16], gat[:, 64:80], gcol[:, 8:9], None, ALU.mult),
                    reads=[m_gat, m_gcol], writes=[m_gat])
                sb.op("dve", lambda q: q.scalar_tensor_tensor(
                    gat[:, 16:32], gat[:, 96:112], gcol[:, 9:10], gat[:, 0:16],
                    ALU.mult, ALU.add),
                    reads=[m_gat, m_gcol], writes=[m_gat])
                sb.op("pe", lambda q, t=t: q.transpose(
                    PT[1][0:16, 0:128], gat[:, 16:32], ident[:, :]),
                    reads=[m_gat, m_ident], writes=[ptb[1]])
                sb.op("act", lambda q, t=t: q.copy(
                    out=maskT[0:16, t * 128:(t + 1) * 128], in_=PT[1][0:16, 0:128]),
                    reads=[ptb[1]], writes=[m_maskT])

        def emit_ffn(tag, sub="full"):
            sc = contextlib.ExitStack()
            with sc:
                he = sc.enter_context(nc.sbuf_tensor(f"he_{tag}", [128, 16 * T], BF16))
                selb_s = sc.enter_context(nc.sbuf_tensor(f"selb_{tag}", [64, NE * 128], BF16))
                gsc = sc.enter_context(nc.sbuf_tensor(f"gsc_{tag}", [128, T], F32))
                gat = sc.enter_context(nc.sbuf_tensor(f"gat_{tag}", [128, 128], F32))
                maskT = sc.enter_context(nc.sbuf_tensor(f"maskT_{tag}", [64, T], BF16))
                mkb0 = sc.enter_context(nc.sbuf_tensor(f"mkb0_{tag}", [128, T], BF16))
                mkb1 = sc.enter_context(nc.sbuf_tensor(f"mkb1_{tag}", [128, T], BF16))
                wda = sc.enter_context(nc.sbuf_tensor(f"wda_{tag}", [128, 1024], BF16))
                wdb = sc.enter_context(nc.sbuf_tensor(f"wdb_{tag}", [128, 1024], BF16))
                wdc = sc.enter_context(nc.sbuf_tensor(f"wdc_{tag}", [128, 1024], BF16))
                wdd = sc.enter_context(nc.sbuf_tensor(f"wdd_{tag}", [128, 1024], BF16))
                heb = [buf(f"he_{tag}{j}") for j in range(16)]
                m_selb = buf(f"selb_{tag}")
                m_gsc = buf(f"gsc_{tag}")
                m_gat = buf(f"gat_{tag}")
                m_maskT = buf(f"maskT_{tag}")
                mkbB = [buf(f"mkb0_{tag}"), buf(f"mkb1_{tag}")]
                wdB = [[buf(f"wda_{tag}"), buf(f"wdb_{tag}")],
                       [buf(f"wdc_{tag}"), buf(f"wdd_{tag}")]]
                mkb = [mkb0, mkb1]
                wdt = [[wda, wdb], [wdc, wdd]]

                sb.op("gq", lambda q: q.dma_start(out=selb_s[:, :], in_=selb_d),
                      writes=[m_selb])
                emit_ln(bounce_stats=True)
                if sub == "ln":
                    sb.barrier()
                    return
                emit_gating(gsc, m_gsc, gat, m_gat, maskT, m_maskT)
                if sub == "gate":
                    sb.barrier()
                    return

                def load_gu(e, i):
                    slot = i % 2
                    wt_g = wga if slot == 0 else wgb
                    wt_u = wua if slot == 0 else wub
                    bg = wgab if slot == 0 else wgbb
                    bu = wuab if slot == 0 else wubb
                    sb.op("gq", lambda q, e=e, i=i, w=wt_g: q.dma_start(
                        out=w[:, :], in_=wg_d[e, i].rearrange("p a b -> p (a b)")),
                        writes=[bg])
                    sb.op("gq", lambda q, e=e, i=i, w=wt_u: q.dma_start(
                        out=w[:, :], in_=wu_d[e, i].rearrange("p a b -> p (a b)")),
                        writes=[bu])

                def mask_bcast(e, eslot):
                    for n in range(2):
                        sb.op("pe", lambda q, e=e, n=n: q.matmul(
                            PT[0][:, n * 512:(n + 1) * 512],
                            selb_s[:, e * 128:(e + 1) * 128],
                            maskT[:, n * 512:(n + 1) * 512], start=True, stop=True),
                            reads=[m_selb, m_maskT], writes=[ptb[0]], inc=(n == 1))
                    sb.op("act", lambda q, eslot=eslot: q.copy(
                        out=mkb[eslot][:, :], in_=PT[0][:, :]),
                        reads=[ptb[0]], writes=[mkbB[eslot]])

                for g in range(1 if sub == "g1" else 9):
                    exps = [g * 2, g * 2 + 1]
                    for je, e in enumerate(exps):
                        mask_bcast(e, je)
                        for i in range(NI):
                            load_gu(e, i)
                            slot = i % 2
                            wt_g = wga if slot == 0 else wgb
                            wt_u = wua if slot == 0 else wub
                            bg = wgab if slot == 0 else wgbb
                            bu = wuab if slot == 0 else wubb
                            ptg, ptu = PT[2 * slot], PT[2 * slot + 1]
                            pbg, pbu = ptb[2 * slot], ptb[2 * slot + 1]
                            for kc in range(NCH):
                                for n in range(2):
                                    sb.op("pe", lambda q, w=wt_g, kc=kc, n=n, p=ptg: q.matmul(
                                        p[:, n * 512:(n + 1) * 512],
                                        w[:, kc * 128:(kc + 1) * 128],
                                        xn[:, kc * T + n * 512:kc * T + (n + 1) * 512],
                                        start=(kc == 0), stop=(kc == NCH - 1)),
                                        reads=[bg] + ([xnb[kc]] if n == 0 else []),
                                        writes=[pbg],
                                        inc=(kc == NCH - 1 and n == 1))
                            for kc in range(NCH):
                                for n in range(2):
                                    sb.op("pe", lambda q, w=wt_u, kc=kc, n=n, p=ptu: q.matmul(
                                        p[:, n * 512:(n + 1) * 512],
                                        w[:, kc * 128:(kc + 1) * 128],
                                        xn[:, kc * T + n * 512:kc * T + (n + 1) * 512],
                                        start=(kc == 0), stop=(kc == NCH - 1)),
                                        reads=[bu] + ([xnb[kc]] if n == 0 else []),
                                        writes=[pbu],
                                        inc=(kc == NCH - 1 and n == 1))
                            st = stg0 if slot == 0 else stg1
                            m_st = m_stg0 if slot == 0 else m_stg1
                            sb.op("act", lambda q, p=ptg, st=st: q.activation(
                                st[:, :], p[:, :], AF.Silu),
                                reads=[pbg], writes=[m_st])
                            hslot = je * 8 + i
                            sb.op("dve", lambda q, p=ptu, st=st, h=hslot: q.tensor_mul(
                                he[:, h * T:(h + 1) * T], p[:, :], st[:, :]),
                                reads=[pbu, m_st], writes=[heb[hslot]])
                            sb.op("dve", lambda q, h=hslot, je=je: q.tensor_mul(
                                he[:, h * T:(h + 1) * T], he[:, h * T:(h + 1) * T],
                                mkb[je][:, :]),
                                reads=[heb[hslot], mkbB[je]], writes=[heb[hslot]])
                    for f in range(NCH):
                        for je, e in enumerate(exps):
                            w = wdt[f % 2][je]
                            wb = wdB[f % 2][je]
                            sb.op("gq", lambda q, e=e, f=f, w=w: q.dma_start(
                                out=w[:, :], in_=wd_d[e, f].rearrange("p a b -> p (a b)")),
                                writes=[wb])
                            for i in range(NI):
                                h = je * 8 + i
                                first = (je == 0 and i == 0)
                                last = (je == 1 and i == NI - 1)
                                pt = PT[2 + (f % 2)]
                                pb = ptb[2 + (f % 2)]
                                for n in range(2):
                                    sb.op("pe", lambda q, w=w, i=i, n=n, h=h, pt=pt,
                                          fi=first, la=last: q.matmul(
                                        pt[:, n * 512:(n + 1) * 512],
                                        w[:, i * 128:(i + 1) * 128],
                                        he[:, h * T + n * 512:h * T + (n + 1) * 512],
                                        start=fi, stop=la),
                                        reads=[wb] + ([heb[h]] if n == 0 else []),
                                        writes=[pb],
                                        inc=(last and n == 1))
                        pt = PT[2 + (f % 2)]
                        pb = ptb[2 + (f % 2)]
                        sb.op("dve", lambda q, f=f, pt=pt: q.tensor_add(
                            xfm[:, f * T:(f + 1) * T], pt[:, :],
                            xfm[:, f * T:(f + 1) * T]),
                            reads=[pb, xb[f]], writes=[xb[f]])
                sb.barrier()

        def emit_qkv_and_exchange():
            sc = contextlib.ExitStack()
            with sc:
                vstg0 = sc.enter_context(nc.sbuf_tensor("vstg0", [128, 2080], BF16))
                vstg1 = sc.enter_context(nc.sbuf_tensor("vstg1", [128, 2080], BF16))
                bvb = sc.enter_context(nc.sbuf_tensor("bvb", [128, D], BF16))
                q2t = sc.enter_context(nc.sbuf_tensor("q2t", [128, T], BF16))
                sel2 = sc.enter_context(nc.sbuf_tensor("sel2", [128, 2], BF16))
                bndq = sc.enter_context(nc.sbuf_tensor("bndq", [32, T], F32))
                bnds = sc.enter_context(nc.sbuf_tensor("bnds", [32, T], F32))
                bndb = sc.enter_context(nc.sbuf_tensor("bndb", [32, 2 * T], BF16))
                k2f = sc.enter_context(nc.sbuf_tensor("k2f", [32, S], F32))
                kmx = sc.enter_context(nc.sbuf_tensor("kmx", [32, 2], F32))
                vstgB = [buf("vstg0"), buf("vstg1")]
                vstg = [vstg0, vstg1]
                m_bvb = buf("bvb")
                m_q2 = buf("q2t")
                m_sel2 = buf("sel2")
                m_bndq = buf("bndq")
                m_bnds = buf("bnds")
                m_bndb = buf("bndb")
                m_k2f = buf("k2f")
                m_kmx = buf("kmx")

                emit_ln(bounce_stats=False)

                sb.op("dve", lambda q: q.memset(sel2[:, :], 0.0), writes=[m_sel2])
                sb.op("dve", lambda q: q.memset(sel2[0:64, 0:1], 1.0), writes=[m_sel2])
                sb.op("dve", lambda q: q.memset(sel2[64:128, 1:2], 1.0), writes=[m_sel2])

                sb.op("sq", lambda q: q.dma_start(
                    out=mean_b[0:1, 0:T],
                    in_=bv_d[0:T].rearrange("(o n) -> o n", o=1)),
                    writes=[m_meanb])
                sb.op("sq", lambda q: q.dma_start(
                    out=rstd_b[0:1, 0:T],
                    in_=bv_d[T:2 * T].rearrange("(o n) -> o n", o=1)),
                    writes=[m_rstdb])
                bvrow = [mean_b, rstd_b]
                m_bvrow = [m_meanb, m_rstdb]
                for j in range(4):
                    pt = PT[2] if j < 2 else PT[3]
                    pb = ptb[2] if j < 2 else ptb[3]
                    off = (j % 2) * 512
                    sb.op("pe", lambda q, j=j, pt=pt, off=off: q.matmul(
                        pt[:, off:off + 512], ones_r[:, :],
                        bvrow[j // 2][0:1, (j % 2) * 512:(j % 2) * 512 + 512],
                        start=True, stop=True),
                        reads=[m_ones, m_bvrow[j // 2]], writes=[pb],
                        inc=(j % 2 == 1))
                for j in range(4):
                    pt = PT[2] if j < 2 else PT[3]
                    pb = ptb[2] if j < 2 else ptb[3]
                    off = (j % 2) * 512
                    sb.op("act", lambda q, j=j, pt=pt, off=off: q.copy(
                        out=bvb[:, j * 512:(j + 1) * 512], in_=pt[:, off:off + 512]),
                        reads=[pb], writes=[m_bvb])

                def proj_fm(w_d, bcol, parkd, m_park, normd, m_normd):
                    for fc in range(NCH):
                        slot = fc % 2
                        wt = wga if slot == 0 else wgb
                        wtb = wgab if slot == 0 else wgbb
                        sb.op("gq", lambda q, fc=fc, wt=wt, w_d=w_d: q.dma_start(
                            out=wt[:, :], in_=w_d[fc].rearrange("p a b -> p (a b)")),
                            writes=[wtb])
                        pt, pb = PT[slot], ptb[slot]
                        for kc in range(NCH):
                            for n in range(2):
                                sb.op("pe", lambda q, wt=wt, kc=kc, n=n, pt=pt: q.matmul(
                                    pt[:, n * 512:(n + 1) * 512],
                                    wt[:, kc * 128:(kc + 1) * 128],
                                    xn[:, kc * T + n * 512:kc * T + (n + 1) * 512],
                                    start=(kc == 0), stop=(kc == NCH - 1)),
                                    reads=[wtb] + ([xnb[kc]] if n == 0 else []),
                                    writes=[pb], inc=(kc == NCH - 1 and n == 1))
                        st = stg0 if slot == 0 else stg1
                        m_st = m_stg0 if slot == 0 else m_stg1
                        sb.op("act", lambda q, pt=pt, st=st, fc=fc, bcol=bcol: q.activation(
                            st[:, :], pt[:, :], AF.Identity,
                            bias=biases[:, bcol * 16 + fc:bcol * 16 + fc + 1]),
                            reads=[pb, m_bias], writes=[m_st])
                        sb.op("sq", lambda q, st=st, fc=fc, parkd=parkd: q.dma_start(
                            out=parkd[fc * 128:(fc + 1) * 128, :], in_=st[:, :]),
                            reads=[m_st], writes=[m_park])
                        sb.op("act", lambda q, st=st: q.activation(
                            q2t[:, :], st[:, :], AF.Square),
                            reads=[m_st], writes=[m_q2])
                        for n in range(2):
                            sb.op("pe", lambda q, n=n: q.matmul(
                                PT[2][0:2, n * 512:(n + 1) * 512], sel2[:, :],
                                q2t[:, n * 512:(n + 1) * 512],
                                start=True, stop=True),
                                reads=[m_q2, m_sel2], writes=[ptb[2]], inc=(n == 1))
                        sb.op("act", lambda q: q.copy(
                            out=scr32[0:2, 0:T], in_=PT[2][0:2, :]),
                            reads=[ptb[2]], writes=[m_scr])
                        for hh in range(2):
                            sb.op("sq", lambda q, fc=fc, hh=hh, normd=normd: q.dma_start(
                                out=normd[fc * 2 + hh:fc * 2 + hh + 1, :],
                                in_=scr32[hh:hh + 1, 0:T]),
                                reads=[m_scr], writes=[m_normd])

                proj_fm(wk_d, 1, k_park, m_kpark, kn2_loc, m_kn2d)

                sb.op("dve", lambda q: q.memset(
                    vstg0.ap().rearrange("p (h e) -> p h e", e=65)[:, :, 64:65], 1.0),
                    writes=[vstgB[0]])
                sb.op("dve", lambda q: q.memset(
                    vstg1.ap().rearrange("p (h e) -> p h e", e=65)[:, :, 64:65], 1.0),
                    writes=[vstgB[1]])
                for tp in range(4):
                    for kc in range(NCH):
                        slot = kc % 2
                        wt = wga if slot == 0 else wgb
                        wtb = wgab if slot == 0 else wgbb
                        sb.op("gq", lambda q, kc=kc, wt=wt: q.dma_start(
                            out=wt[:, :], in_=wv_d[kc * 128:(kc + 1) * 128, :]),
                            writes=[wtb])
                        for half in range(2):
                            tc = tp * 2 + half
                            pts = (PT[0], PT[1]) if half == 0 else (PT[2], PT[3])
                            pbs = (ptb[0], ptb[1]) if half == 0 else (ptb[2], ptb[3])
                            for j in range(4):
                                pt = pts[j // 2]
                                pb = pbs[j // 2]
                                off = (j % 2) * 512
                                sb.op("pe", lambda q, tc=tc, kc=kc, j=j, wt=wt,
                                      pt=pt, off=off: q.matmul(
                                    pt[:, off:off + 512],
                                    xn[:, kc * T + tc * 128:kc * T + (tc + 1) * 128],
                                    wt[:, j * 512:(j + 1) * 512],
                                    start=(kc == 0), stop=(kc == NCH - 1)),
                                    reads=[wtb] + ([xnb[kc]] if j == 0 else []),
                                    writes=[pb],
                                    inc=(kc == NCH - 1 and j == 3))
                    for half in range(2):
                        tc = tp * 2 + half
                        pts = (PT[0], PT[1]) if half == 0 else (PT[2], PT[3])
                        pbs = (ptb[0], ptb[1]) if half == 0 else (ptb[2], ptb[3])
                        vs = vstg[half]
                        vb = vstgB[half]
                        for j in range(4):
                            pt = pts[j // 2]
                            pb = pbs[j // 2]
                            off = (j % 2) * 512
                            sb.op("dve", lambda q, vs=vs, j=j, pt=pt, off=off: q.tensor_add(
                                vs.ap()[:, j * 520:(j + 1) * 520].rearrange(
                                    "p (h e) -> p h e", e=65)[:, :, 0:64],
                                pt[:, off:off + 512].rearrange("p (h e) -> p h e", e=64),
                                bvb[:, j * 512:(j + 1) * 512].rearrange(
                                    "p (h e) -> p h e", e=64)),
                                reads=[pb, m_bvb], writes=[vb])
                        sb.op("sq", lambda q, tc=tc, vs=vs: q.dma_start(
                            out=v_park[tc * 128:(tc + 1) * 128, :], in_=vs[:, :]),
                            reads=[vb], writes=[m_vpark])

                PAIRS = [[0, 1], [2, 3], [4, 5], [6, 7]]
                sb.op("gq", lambda q: q.dma_start(out=u32t[0:1, 3:4],
                                                  in_=u32t[0:1, 2:3]),
                      reads=[m_kpark, m_vpark, m_kn2d])

                def colls(q):
                    q.collective_compute(
                        "AllGather", ALU.bypass, replica_groups=PAIRS,
                        ins=[k_park.opt()], outs=[kg.opt()]).then_inc(cc_k)
                    q.collective_compute(
                        "AllGather", ALU.bypass, replica_groups=PAIRS,
                        ins=[v_park.opt()], outs=[vg.opt()]).then_inc(cc_v)
                    q.collective_compute(
                        "AllGather", ALU.bypass, replica_groups=PAIRS,
                        ins=[kn2_loc.opt()], outs=[ng.opt()]).then_inc(cc_n)
                sb.raw("gq", colls)

                proj_fm(wq_d, 0, q_park, m_qpark, bnc_qn, m_qnd)
                sb.op("sq", lambda q: q.dma_start(out=bndq[:, :], in_=bnc_qn),
                      reads=[m_qnd], writes=[m_bndq])
                sb.op("act", lambda q: q.activation(bnds[:, :], bndq[:, :], AF.Sqrt),
                      reads=[m_bndq], writes=[m_bnds])

                def collwait(q):
                    q.wait_ge(cc_k, 1)
                    q.wait_ge(cc_v, 1)
                    q.wait_ge(cc_n, 1)
                sb.raw("gq", collwait)
                sb.op("gq", lambda q: q.dma_start(out=u32t[0:1, 3:4],
                                                  in_=u32t[0:1, 2:3]),
                      writes=[m_sh])

                sb.op("sq", lambda q: q.dma_start(out=k2f[:, 0:T], in_=ng[0:NH, :]),
                      reads=[m_sh], writes=[m_k2f])
                sb.op("sq", lambda q: q.dma_start(out=k2f[:, T:2 * T], in_=ng[NH:2 * NH, :]),
                      reads=[m_sh], writes=[m_k2f])
                sb.op("dve", lambda q: q.reduce_max(kmx[:, 0:1], k2f[:, :], axis=AX.X),
                      reads=[m_k2f], writes=[m_kmx])
                sb.op("act", lambda q: q.activation(kmx[:, 1:2], kmx[:, 0:1], AF.Sqrt),
                      reads=[m_kmx], writes=[m_kmx])
                sb.op("dve", lambda q: q.tensor_scalar(
                    bndq[:, :], bnds[:, :], kmx[:, 1:2], -1.01, ALU.mult, ALU.mult),
                    reads=[m_bnds, m_kmx], writes=[m_bndq])
                sb.op("dve", lambda q: q.tensor_scalar_add(
                    bndq[:, :], bndq[:, :], -6.0),
                    reads=[m_bndq], writes=[m_bndq])
                sb.op("dve", lambda q: q.tensor_copy(bndb[:, 0:T], bndq[:, :]),
                      reads=[m_bndq], writes=[m_bndb])
                sb.op("dve", lambda q: q.tensor_sub(bnds[:, :], bndq[:, :],
                                                    bndb[:, 0:T]),
                      reads=[m_bndq, m_bndb], writes=[m_bnds])
                sb.op("dve", lambda q: q.tensor_copy(bndb[:, T:2 * T], bnds[:, :]),
                      reads=[m_bnds], writes=[m_bndb])
                sb.op("sq", lambda q: q.dma_start(out=bhl_park[0:32, :], in_=bndb[:, 0:T]),
                      reads=[m_bndb], writes=[m_bhl])
                sb.op("sq", lambda q: q.dma_start(out=bhl_park[32:64, :], in_=bndb[:, T:2 * T]),
                      reads=[m_bndb], writes=[m_bhl])
                sb.barrier()

        def emit_attention_core():
            sc = contextlib.ExitStack()
            with sc:
                atf = sc.enter_context(nc.sbuf_tensor("atf", [128, 16 * T], BF16))
                vq = sc.enter_context(nc.sbuf_tensor("vq", [128, 16 * 520], BF16))
                ka0 = sc.enter_context(nc.sbuf_tensor("ka0", [66, S], BF16))
                ka1 = sc.enter_context(nc.sbuf_tensor("ka1", [66, S], BF16))
                qa0 = sc.enter_context(nc.sbuf_tensor("qa0", [66, T], BF16))
                qa1 = sc.enter_context(nc.sbuf_tensor("qa1", [66, T], BF16))
                pb0 = sc.enter_context(nc.sbuf_tensor("pb0", [128, T], BF16))
                pb1 = sc.enter_context(nc.sbuf_tensor("pb1", [128, T], BF16))
                shm = sc.enter_context(nc.sbuf_tensor("shm", [64, 128], BF16))
                odd = sc.enter_context(nc.sbuf_tensor("odd", [64, T], BF16))
                atfb = [buf(f"atf{j}") for j in range(NCH)]
                m_vq = [buf("vq0"), buf("vq1")]
                kaB = [buf("ka0"), buf("ka1")]
                qaB = [buf("qa0"), buf("qa1")]
                pbB = [buf("pb0"), buf("pb1")]
                m_shm = buf("shm")
                m_odd = buf("odd")
                ka = [ka0, ka1]
                qa = [qa0, qa1]
                pbt = [pb0, pb1]

                sb.op("dve", lambda q: q.memset(ka0[64:66, :], 1.0), writes=[kaB[0]])
                sb.op("dve", lambda q: q.memset(ka1[64:66, :], 1.0), writes=[kaB[1]])
                sb.op("gq", lambda q: q.dma_start(out=shm[:, :], in_=shmat_d),
                      writes=[m_shm])

                dnrow = [mean_b, rstd_b]
                m_dnr = [m_meanb, m_rstdb]
                m_sums = [buf("sums0"), buf("sums1")]

                def emit_bounce(h):
                    sp = h % 2
                    avp, avb = PT[2 + sp], ptb[2 + sp]
                    sb.op("act", lambda q, avp=avp: q.copy(
                        out=scr32[64:65, 0:T], in_=avp[64:65, :]),
                        reads=[avb], writes=[m_scr])
                    sb.op("sq", lambda q, sp=sp: q.dma_start(
                        out=bnc_sums[sp:sp + 1, :], in_=scr32[64:65, 0:T]),
                        reads=[m_scr], writes=[m_sums[sp]])
                    sb.op("sq", lambda q, sp=sp: q.dma_start(
                        out=dnrow[sp][0:1, 0:T],
                        in_=bnc_sums[sp:sp + 1, :]),
                        reads=[m_sums[sp]], writes=[m_dnr[sp]])
                    sb.op("dve", lambda q, sp=sp: q.reciprocal(
                        dnrow[sp][0:1, 0:T], dnrow[sp][0:1, 0:T]),
                        reads=[m_dnr[sp]], writes=[m_dnr[sp]])

                def emit_norm(h):
                    sp = h % 2
                    avp, avb = PT[2 + sp], ptb[2 + sp]
                    for n in range(2):
                        sb.op("pe", lambda q, n=n, sp=sp: q.matmul(
                            PT[0][0:64, n * 512:(n + 1) * 512],
                            ones_r[:, 0:64],
                            dnrow[sp][0:1, n * 512:(n + 1) * 512],
                            start=True, stop=True),
                            reads=[m_ones, m_dnr[sp]], writes=[ptb[0]],
                            inc=(n == 1))
                    sb.op("act", lambda q: q.copy(
                        out=scr32[0:64, :], in_=PT[0][0:64, :]),
                        reads=[ptb[0]], writes=[m_scr])
                    if sp == 0:
                        sb.op("dve", lambda q, h=h, avp=avp: q.tensor_mul(
                            atf[0:64, (h // 2) * T:(h // 2 + 1) * T],
                            avp[0:64, :], scr32[0:64, :]),
                            reads=[avb, m_scr], writes=[atfb[h // 2]])
                    else:
                        sb.op("dve", lambda q, avp=avp: q.tensor_mul(
                            odd[:, :], avp[0:64, :], scr32[0:64, :]),
                            reads=[avb, m_scr], writes=[m_odd])
                        for n in range(2):
                            sb.op("pe", lambda q, n=n: q.matmul(
                                PT[1][:, n * 512:(n + 1) * 512], shm[:, :],
                                odd[:, n * 512:(n + 1) * 512],
                                start=True, stop=True),
                                reads=[m_shm, m_odd], writes=[ptb[1]],
                                inc=(n == 1))
                        sb.op("act", lambda q, h=h: q.copy(
                            out=atf[64:128, (h // 2) * T:(h // 2 + 1) * T],
                            in_=PT[1][64:128, :]),
                            reads=[ptb[1]], writes=[atfb[h // 2]])

                for quarter in range(4):
                    for kc in range(NCH):
                        sb.op("aq", lambda q, quarter=quarter, kc=kc: q.dma_start(
                            out=vq[:, kc * 520:(kc + 1) * 520],
                            in_=vg[kc * 128:(kc + 1) * 128,
                                   quarter * 520:(quarter + 1) * 520]),
                            reads=[m_sh], writes=[m_vq[kc // 8]])
                    for hh in range(8):
                        h = quarter * 8 + hh
                        hs = hh % 2
                        kat, qat = ka[hs], qa[hs]
                        kab, qab = kaB[hs], qaB[hs]
                        sb.op("aq", lambda q, h=h, kat=kat: q.dma_start(
                            out=kat[0:64, 0:T], in_=kg[h * 64:(h + 1) * 64, :]),
                            reads=[m_sh], writes=[kab])
                        sb.op("aq", lambda q, h=h, kat=kat: q.dma_start(
                            out=kat[0:64, T:2 * T],
                            in_=kg[D + h * 64:D + (h + 1) * 64, :]),
                            reads=[m_sh], writes=[kab])
                        sb.op("aq", lambda q, h=h, qat=qat: q.dma_start(
                            out=qat[0:64, :], in_=q_park[h * 64:(h + 1) * 64, :]),
                            reads=[m_qpark], writes=[qab])
                        sb.op("aq", lambda q, h=h, qat=qat: q.dma_start(
                            out=qat[64:65, :], in_=bhl_park[h:h + 1, :]),
                            reads=[m_bhl], writes=[qab])
                        sb.op("aq", lambda q, h=h, qat=qat: q.dma_start(
                            out=qat[65:66, :], in_=bhl_park[32 + h:33 + h, :]),
                            reads=[m_bhl], writes=[qab])
                        avp = PT[2 + hs]
                        avb = ptb[2 + hs]

                        def emit_qk(kc):
                            qkp, qkb = PT[kc % 2], ptb[kc % 2]
                            for n in range(2):
                                sb.op("pe", lambda q, kat=kat, qat=qat, kc=kc,
                                      n=n, qkp=qkp: q.matmul(
                                    qkp[:, n * 512:(n + 1) * 512],
                                    kat[:, kc * 128:(kc + 1) * 128],
                                    qat[:, n * 512:(n + 1) * 512],
                                    start=True, stop=True),
                                    reads=[kab, qab], writes=[qkb], inc=(n == 1))

                        emit_qk(0)
                        for kc in range(NCH):
                            qkp = PT[kc % 2]
                            qkb = ptb[kc % 2]
                            pbuf = pbt[kc % 2]
                            pbb = pbB[kc % 2]
                            sb.op("act", lambda q, qkp=qkp, pbuf=pbuf: q.activation(
                                pbuf[:, :], qkp[:, :], AF.Exp, scale=SCALE),
                                reads=[qkb], writes=[pbb])
                            if kc + 1 < NCH:
                                emit_qk(kc + 1)
                            for n in range(2):
                                sb.op("pe", lambda q, kc=kc, hh=hh, n=n, pbuf=pbuf,
                                      avp=avp: q.matmul(
                                    avp[0:65, n * 512:(n + 1) * 512],
                                    vq[:, kc * 520 + hh * 65:kc * 520 + (hh + 1) * 65],
                                    pbuf[:, n * 512:(n + 1) * 512],
                                    start=(kc == 0), stop=(kc == NCH - 1)),
                                    reads=[m_vq[kc // 8], pbb], writes=[avb],
                                    inc=(n == 1))
                        emit_bounce(h)
                        if h > 0:
                            emit_norm(h - 1)
                emit_norm(NH - 1)

                for fc in range(NCH):
                    slot = fc % 2
                    wt = wga if slot == 0 else wgb
                    wtb = wgab if slot == 0 else wgbb
                    sb.op("gq", lambda q, fc=fc, wt=wt: q.dma_start(
                        out=wt[:, :], in_=wo_d[fc].rearrange("p a b -> p (a b)")),
                        writes=[wtb])
                    pt, pb = PT[slot], ptb[slot]
                    for kc in range(NCH):
                        for n in range(2):
                            sb.op("pe", lambda q, wt=wt, kc=kc, n=n, pt=pt: q.matmul(
                                pt[:, n * 512:(n + 1) * 512],
                                wt[:, kc * 128:(kc + 1) * 128],
                                atf[:, kc * T + n * 512:kc * T + (n + 1) * 512],
                                start=(kc == 0), stop=(kc == NCH - 1)),
                                reads=[wtb] + ([atfb[kc]] if n == 0 else []),
                                writes=[pb], inc=(kc == NCH - 1 and n == 1))
                    sb.op("dve", lambda q, fc=fc, pt=pt: q.scalar_tensor_tensor(
                        xfm[:, fc * T:(fc + 1) * T], pt[:, :],
                        biases[:, 32 + fc:33 + fc], xfm[:, fc * T:(fc + 1) * T],
                        ALU.add, ALU.add),
                        reads=[pb, m_bias, xb[fc]], writes=[xb[fc]])
                sb.barrier()

        def dump_x(dst):
            for c in range(NCH):
                sb.op("sq", lambda q, c=c: q.dma_start(
                    out=dst[c * 128:(c + 1) * 128, :],
                    in_=xfm[:, c * T:(c + 1) * T]),
                    reads=[xb[c]])

        if stage.startswith("ffn1_"):
            emit_ffn("f1", sub=stage.split("_", 1)[1])
            dump_x(dbg1_d)
        elif stage in ("ffn1", "qkv", "attn", "full"):
            emit_ffn("f1")
            dump_x(dbg1_d)
        if stage in ("qkv", "attn", "full"):
            emit_qkv_and_exchange()
        if stage in ("attn", "full"):
            emit_attention_core()
            dump_x(dbg2_d)
        if stage == "full":
            emit_ffn("f2")

        m_out = buf("out")
        for c in range(NCH):
            sb.op("sq", lambda q, c=c: q.dma_start(
                out=out_d[c * 128:(c + 1) * 128, :], in_=xfm[:, c * T:(c + 1) * T]),
                reads=[xb[c]], writes=[m_out])
        fin = sb.cnt["sq"]
        sb.raw("sq", lambda q: q.wait_ge(sb.sems["sq"], fin))

        blk = es.enter_context(nc.Block())
        for sect in ("tensor", "scalar", "vector", "gpsimd", "sync"):
            fns = sb.prog[sect]

            def body(q, _fns=fns):
                for f in _fns:
                    f(q)
            getattr(blk, sect)(body)

    return nc



_CACHE = {}


def _relayout(inputs):
    f = np.float32
    wq = inputs["L0_attn_Wq"].astype(f, copy=False)
    wk = inputs["L0_attn_Wk"].astype(f, copy=False)
    wo = inputs["L0_attn_Wo"].astype(f, copy=False)

    def wpl(w):
        return np.ascontiguousarray(
            w.reshape(16, 128, 16, 128).transpose(2, 1, 0, 3))

    wg = inputs["L0_ffn_wg"]
    wu = inputs["L0_ffn_wu"]
    wd = inputs["L0_ffn_wd"]
    wg_s = inputs["L0_ffn_wg_s"]
    wu_s = inputs["L0_ffn_wu_s"]
    wd_s = inputs["L0_ffn_wd_s"]

    def gu_pl(m, ms):
        a = m.reshape(16, 16, 128, 8, 128).transpose(0, 3, 2, 1, 4)
        b = ms.reshape(16, 128, 2, 8, 128).transpose(2, 3, 1, 0, 4)
        return np.ascontiguousarray(np.concatenate([a, b], axis=0))

    def d_pl(m, ms):
        a = m.reshape(16, 8, 128, 16, 128).transpose(0, 3, 2, 1, 4)
        b = ms.reshape(2, 8, 128, 16, 128).transpose(0, 3, 2, 1, 4)
        return np.ascontiguousarray(np.concatenate([a, b], axis=0))

    selb = np.zeros((64, NE * 128), np.float32)
    for e in range(NE):
        r = e if e < 16 else 16 + e
        selb[r, e * 128:(e + 1) * 128] = 1.0
    import ml_dtypes
    selb = selb.astype(ml_dtypes.bfloat16)
    shmat = np.zeros((64, 128), np.float32)
    for p in range(64):
        shmat[p, 64 + p] = 1.0
    shmat = shmat.astype(ml_dtypes.bfloat16)

    ct = np.ascontiguousarray(
        inputs["L0_ffn_centroids"].astype(f, copy=False).T.reshape(16, 128, 16))

    return {
        "wq": wpl(wq), "wk": wpl(wk), "wo": wpl(wo),
        "wv": np.ascontiguousarray(inputs["L0_attn_Wv"].astype(f, copy=False)),
        "wg": gu_pl(wg, wg_s), "wu": gu_pl(wu, wu_s), "wd": d_pl(wd, wd_s),
        "ct": ct,
        "bq": np.ascontiguousarray(inputs["L0_attn_bq"].astype(f, copy=False)),
        "bk": np.ascontiguousarray(inputs["L0_attn_bk"].astype(f, copy=False)),
        "bv": np.ascontiguousarray(inputs["L0_attn_bv"].astype(f, copy=False)),
        "bo": np.ascontiguousarray(inputs["L0_attn_bo"].astype(f, copy=False)),
        "gb": np.ascontiguousarray(inputs["L0_ffn_gate_bias"].astype(f, copy=False)),
        "ident": np.eye(128, dtype=np.float32),
        "selb": selb, "shmat": shmat,
    }


def kernel(**inputs):
    x = np.asarray(inputs["L0_attn_x"], dtype=np.float32)
    shared = _relayout(inputs)

    nc = build_program()
    in_maps = []
    for c in range(8):
        b, h = c // 2, c % 2
        m = dict(shared)
        m["x_in"] = np.ascontiguousarray(x[b, h * T:(h + 1) * T, :].T)
        in_maps.append(m)
    res = run_bass_kernel_spmd(nc, in_maps, core_ids=list(range(8)))
    out = np.empty((4, 2048, 2048), np.float32)
    for c in range(8):
        b, h = c // 2, c % 2
        out[b, h * T:(h + 1) * T, :] = res.results[c]["out"].T
    return out
